# version 08
# baseline (speedup 1.0000x reference)
import contextlib
import numpy as np

import concourse.bass as bass
import concourse.mybir as mybir
from concourse.bass_utils import run_bass_kernel_spmd

F32 = mybir.dt.float32
BF16 = mybir.dt.bfloat16
OP = mybir.AluOpType
ACT = mybir.ActivationFunctionType
AX = mybir.AxisListType.X

MAGIC = 12582912.0
EXP_SHIFT = -24.0
E = 1024
ECH = 8
I3 = 3072
ICH = 24
NB = 512


class Sem:
    def __init__(self, ctx, nc, name):
        self.h = ctx.enter_context(nc.semaphore(name))
        self.n = 0
        self.name = name


_CHAIN_OPS = {
    "tensor_copy", "tensor_tensor", "tensor_scalar", "tensor_single_scalar",
    "tensor_add", "tensor_sub", "tensor_mul", "tensor_max", "tensor_reduce",
    "reduce_max", "reduce_sum", "reciprocal", "memset", "activation", "iota",
    "scalar_tensor_tensor", "select", "tensor_scalar_add", "tensor_scalar_sub",
    "tensor_scalar_mul", "tensor_scalar_max", "tensor_scalar_min",
}


class Eng:

    def __init__(self, eng, sem):
        self._e = eng
        self._s = sem

    def __getattr__(self, name):
        fn = getattr(self._e, name)
        if name in _CHAIN_OPS:
            def wrap(*a, **k):
                if self._s.n > 0:
                    self._e.wait_ge(self._s.h, self._s.n)
                ins = fn(*a, **k)
                ins.then_inc(self._s.h, 1)
                self._s.n += 1
                return ins
            return wrap
        if name in ("dma_start", "collective_compute", "dma_start_transpose"):
            def wrapd(*a, **k):
                if self._s.n > 0:
                    self._e.wait_ge(self._s.h, self._s.n)
                return fn(*a, **k)
            return wrapd
        return fn


def build_program(S=4096):
    TB = S // NB
    KT = S // 128
    SL = max(ECH * NB, S)

    nc = bass.Bass("TRN2", target_bir_lowering=False)

    xT_in = nc.dram_tensor("xT", [ECH, 128, S], F32, kind="ExternalInput").ap()
    awT = {w: nc.dram_tensor(f"awT_{w}", [ECH, 128, E], F32, kind="ExternalInput").ap()
           for w in ("q", "k", "v", "p")}
    bias_in = nc.dram_tensor("bproj", [128, ECH], F32, kind="ExternalInput").ap()
    wT_in = {"g": nc.dram_tensor("wgT", [ECH, 128, I3], F32, kind="ExternalInput").ap(),
             "u": nc.dram_tensor("wuT", [ECH, 128, I3], F32, kind="ExternalInput").ap(),
             "d": nc.dram_tensor("wdT", [ICH, 128, E], F32, kind="ExternalInput").ap()}
    w_nat = {"g": nc.dram_tensor("wg_nat", [ICH, 128, E], F32, kind="ExternalInput").ap(),
             "u": nc.dram_tensor("wu_nat", [ICH, 128, E], F32, kind="ExternalInput").ap(),
             "d": nc.dram_tensor("wd_nat", [ECH, 128, I3], F32, kind="ExternalInput").ap()}
    ident_in = nc.dram_tensor("ident32", [128, 128], F32, kind="ExternalInput").ap()

    out_d = nc.dram_tensor("out", [ECH, 128, S], F32, kind="ExternalOutput").ap()

    qh_d = nc.dram_tensor("qh_d", [TB, ECH, 128, NB], BF16).ap()
    ql_d = nc.dram_tensor("ql_d", [TB, ECH, 128, NB], BF16).ap()
    kh_d = nc.dram_tensor("kh_d", [KT, ECH, 128, 128], BF16).ap()
    kl_d = nc.dram_tensor("kl_d", [KT, ECH, 128, 128], BF16).ap()
    vh_d = nc.dram_tensor("vh_d", [KT, 128, E], BF16).ap()
    vl_d = nc.dram_tensor("vl_d", [KT, 128, E], BF16).ap()
    x1_d = nc.dram_tensor("x1_d", [ECH, 128, S], F32).ap()
    whl_d = {("g", 0): nc.dram_tensor("wgh_d", [ECH, 128, I3], BF16).ap(),
             ("g", 1): nc.dram_tensor("wgl_d", [ECH, 128, I3], BF16).ap(),
             ("u", 0): nc.dram_tensor("wuh_d", [ECH, 128, I3], BF16).ap(),
             ("u", 1): nc.dram_tensor("wul_d", [ECH, 128, I3], BF16).ap(),
             ("d", 0): nc.dram_tensor("wdh_d", [ICH, 128, E], BF16).ap(),
             ("d", 1): nc.dram_tensor("wdl_d", [ICH, 128, E], BF16).ap()}
    sc_d = {"g": (nc.dram_tensor("sg_d", [32, I3], F32).ap(),
                  nc.dram_tensor("sgr_d", [32, I3], F32).ap()),
            "u": (nc.dram_tensor("su_d", [32, I3], F32).ap(),
                  nc.dram_tensor("sur_d", [32, I3], F32).ap()),
            "d": (nc.dram_tensor("sd_d", [96, E], F32).ap(),
                  nc.dram_tensor("sdr_d", [96, E], F32).ap())}
    upr_d = nc.dram_tensor("upr_d", [ICH, 128, S], F32).ap()
    gtr_d = nc.dram_tensor("gtr_d", [ICH, 128, S], F32).ap()
    tr_d = nc.dram_tensor("tr_d", [ICH, 128, S], F32).ap()
    or_d = nc.dram_tensor("or_d", [ICH, 128, S], F32).ap()
    cc_in = [nc.dram_tensor(f"cc_in{i}", [1, 16], F32).ap() for i in range(4)]
    cc_out = [nc.dram_tensor(f"cc_out{i}", [1, 16], F32).ap() for i in range(4)]

    ctx = contextlib.ExitStack()
    with ctx:
        def sb(name, shape, dtype, st=None):
            return (st or ctx).enter_context(nc.sbuf_tensor(name, shape, dtype))

        ones_b = sb("ones_b", [128, 128], BF16)
        ones_r = sb("ones_r", [1, 128], F32)
        ident = sb("ident", [128, 128], F32)
        scl = sb("scl", [128, 16], F32)
        row = sb("row", [1, 32], F32)
        colb = sb("colb", [128, 8], F32)
        eps_c = sb("eps_c", [128, 1], F32)
        bexp_c = sb("bexp_c", [128, 1], F32)
        magic_c = sb("magic_c", [128, 1], F32)
        nmagic_c = sb("nmagic_c", [128, 1], F32)
        bias_sb = sb("bias_sb", [128, ECH], F32)
        ccs = sb("ccs", [1, 16], F32)

        R127, RR127, R32K, RSX, RSU, RSG, RST, RSO, RT0, RT1 = range(10)
        ISXc, KGc, SGc, KUc, KTc, KOc, SOc = range(7)
        SW = {"q": 8, "k": 9, "v": 10, "p": 11}

        def c_(t, a):
            return t[:, a:a + 1]

        ps_all = ctx.enter_context(nc.psum_tensor("psall", [128, 8 * NB], F32))

        def bank(i):
            return ps_all[:, i * NB:(i + 1) * NB]

        prow = ps_all[:1, 7 * NB:7 * NB + 128]
        prow2 = ps_all[:1, 6 * NB:6 * NB + 128]
        pbc8 = ps_all[:, 7 * NB:7 * NB + 8]
        pbc2 = ps_all[:, 7 * NB:7 * NB + 2]

        s_in = Sem(ctx, nc, "s_in")
        s_x = Sem(ctx, nc, "s_x")
        s_pw = Sem(ctx, nc, "s_pw")
        s_kp = [Sem(ctx, nc, f"s_k{i}") for i in range(2)]
        s_vp = [Sem(ctx, nc, f"s_v{i}") for i in range(2)]
        s_fp = [Sem(ctx, nc, f"s_f{i}") for i in range(2)]
        s_gn = [Sem(ctx, nc, f"s_g{i}") for i in range(2)]
        s_x1b = Sem(ctx, nc, "s_x1b")
        s_q = Sem(ctx, nc, "s_q")
        s_ps = Sem(ctx, nc, "s_ps")
        s_sq = [Sem(ctx, nc, f"s_sq{i}") for i in range(2)]
        s_sx = [Sem(ctx, nc, f"s_sx{i}") for i in range(2)]
        s_so = [Sem(ctx, nc, f"s_so{i}") for i in range(2)]
        s_or = [Sem(ctx, nc, f"s_or{i}") for i in range(2)]
        s_gd = Sem(ctx, nc, "s_gd")
        s_mm = Sem(ctx, nc, "s_mm")
        s_dv = Sem(ctx, nc, "s_dv")
        s_sc = Sem(ctx, nc, "s_sc")
        s_cc = Sem(ctx, nc, "s_cc")
        s_gp = Sem(ctx, nc, "s_gp")
        s_out = Sem(ctx, nc, "s_out")

        block = ctx.enter_context(nc.Block())
        V = Eng(nc.vector, s_dv)
        C = Eng(nc.scalar, s_sc)
        G = Eng(nc.gpsimd, s_gp)
        T = nc.tensor
        SY = nc.sync

        stA = contextlib.ExitStack()
        xts = sb("xts", [128, ECH * NB], F32, stA)
        xspl = sb("xspl", [128, 2 * ECH * NB], BF16, stA)
        qst = sb("qst", [128, NB], F32, stA)
        qhl = sb("qhl", [128, 4 * NB], BF16, stA)
        awp = sb("awp", [128, ECH * E], BF16, stA)

        stQ = contextlib.ExitStack()
        aw3 = sb("aw3", [128, 3 * ECH * E], BF16, stQ)
        stW = contextlib.ExitStack()
        wstg = sb("wstg", [128, ECH * E], F32, stW)

        awq = {"q": aw3[:, 0:ECH * E], "k": aw3[:, ECH * E:2 * ECH * E],
               "v": aw3[:, 2 * ECH * E:], "p": awp[:]}

        xh_s, xl_s = xspl[:, :ECH * NB], xspl[:, ECH * NB:]

        SY.dma_start(out=ident[:], in_=ident_in).then_inc(s_in.h, 16)
        s_in.n += 16
        SY.dma_start(out=bias_sb[:], in_=bias_in).then_inc(s_in.h, 16)
        s_in.n += 16

        V.memset(ones_b[:], 1.0)
        V.memset(ones_r[:], 1.0)
        V.memset(eps_c[:], 1e-5)
        V.memset(bexp_c[:], EXP_SHIFT)
        V.memset(magic_c[:], MAGIC)
        V.memset(nmagic_c[:], -MAGIC)
        V.memset(colb[:], 0.0)
        V.memset(row[:], 0.0)
        V.memset(ccs[:], 0.0)
        V.memset(scl[:], 0.0)
        V.memset(c_(row, R127), 127.0)
        V.reciprocal(c_(row, RR127), c_(row, R127))
        V.memset(c_(row, R32K), 32767.0)
        V.reciprocal(c_(row, R32K), c_(row, R32K))
        V.memset(c_(colb, 5), 7.0)
        V.reciprocal(c_(colb, 5), c_(colb, 5))

        for j, w in enumerate(("q", "k", "v", "p")):
            SY.wait_ge(s_dv.h, s_dv.n)
            SY.dma_start(out=wstg[:].rearrange("p (c n) -> p c n", n=E),
                         in_=awT[w].transpose([1, 0, 2])).then_inc(s_in.h, 16)
            s_in.n += 16

            nc.vector.wait_ge(s_in.h, s_in.n)
            V.reduce_max(c_(colb, 6), wstg[:], AX, apply_absolute_value=True)
            nc.tensor.wait_ge(s_dv.h, s_dv.n)
            T.transpose(prow, colb[:, 6:7], ident[:]).then_inc(s_mm.h, 1)
            s_mm.n += 1

            nc.vector.wait_ge(s_mm.h, s_mm.n)
            V.reduce_max(c_(row, RT0), prow, AX)
            V.tensor_tensor(c_(row, RT0), c_(row, RT0), c_(row, RR127), OP.mult)
            V.reciprocal(c_(row, RT1), c_(row, RT0))
            nc.tensor.wait_ge(s_dv.h, s_dv.n)
            T.matmul(pbc2, ones_r[:], row[:1, RT0:RT0 + 2],
                     start=True, stop=True).then_inc(s_mm.h, 1)
            s_mm.n += 1

            nc.vector.wait_ge(s_mm.h, s_mm.n)
            V.tensor_copy(c_(scl, SW[w]), pbc2[:, 0:1])
            V.tensor_copy(c_(colb, 7), pbc2[:, 1:2])
            V.tensor_scalar(wstg[:], wstg[:], c_(colb, 7), MAGIC, OP.mult, OP.add)
            V.tensor_scalar(awq[w], wstg[:], MAGIC, None, OP.subtract)

        stW.close()
        SCLS = sb("SCLS", [128, 2304], F32, stQ)
        SCLR = sb("SCLR", [128, 2304], F32, stQ)
        prepw2 = sb("prepw2", [128, 2 * I3], F32, stQ)
        preps2 = sb("preps2", [128, 2 * I3], F32, stQ)
        prepr2 = sb("prepr2", [128, 2 * I3], F32, stQ)
        prepq2 = sb("prepq2", [128, I3], BF16, stQ)
        prepl2 = sb("prepl2", [128, I3], BF16, stQ)
        tbuf2 = sb("tbuf2", [96, 2 * 256], F32, stQ)

        prep_units = ([("scale", "g", c) for c in range(ICH)]
                      + [("scale", "u", c) for c in range(ICH)]
                      + [("scale", "d", c) for c in range(ECH)]
                      + [("tbatch", None, None)]
                      + [("deq", "g", c) for c in range(ECH)]
                      + [("deq", "u", c) for c in range(ECH)]
                      + [("deq", "d", c) for c in range(ICH)])
        scl_off = {"g": 0, "u": 768, "d": 1536}

        prep_slot = [0]
        prep_vmark = {0: 0, 1: 0}
        prep_pw = [Sem(ctx, nc, "s_pw0"), Sem(ctx, nc, "s_pw1")]
        prep_ps = [Sem(ctx, nc, "s_ps0"), Sem(ctx, nc, "s_ps1")]

        def emit_prep_load(kind, which, c):
            sl = prep_slot[0] & 1
            prep_slot[0] += 1
            prepw = prepw2[:, sl * I3:]
            preps = preps2[:, sl * I3:]
            prepr = prepr2[:, sl * I3:]
            spw = prep_pw[sl]
            sps = prep_ps[sl]
            if kind == "scale":
                n_in = E if which in ("g", "u") else I3
                if prep_vmark[sl] > 0:
                    SY.wait_ge(s_dv.h, prep_vmark[sl])
                SY.dma_start(out=prepw[:, :n_in], in_=w_nat[which][c]
                             ).then_inc(spw.h, 16)
                spw.n += 16
            else:
                n_in = I3 if which in ("g", "u") else E
                sd_, srd_ = sc_d[which]
                if prep_vmark[sl] > 0:
                    SY.wait_ge(s_dv.h, prep_vmark[sl])
                if spw.n > 0:
                    SY.wait_ge(spw.h, spw.n)
                SY.dma_start(out=prepw[:, :n_in], in_=wT_in[which][c]
                             ).then_inc(spw.h, 16)
                sin = sd_[c * 4:c * 4 + 4].unsqueeze(1).broadcast_to([4, 32, n_in])
                rin = srd_[c * 4:c * 4 + 4].unsqueeze(1).broadcast_to([4, 32, n_in])
                SY.dma_start(out=preps[:, :n_in], in_=sin).then_inc(spw.h, 16)
                SY.dma_start(out=prepr[:, :n_in], in_=rin).then_inc(spw.h, 16)
                spw.n += 48
            return sl

        def emit_prep_compute(kind, which, c, sl):
            prepw = prepw2[:, sl * I3:]
            preps = preps2[:, sl * I3:]
            prepr = prepr2[:, sl * I3:]
            prepq = prepq2[:, 0:]
            prepl = prepl2[:, 0:]
            spw = prep_pw[sl]
            sps = prep_ps[sl]
            if kind == "scale":
                n_in = E if which in ("g", "u") else I3
                nb_ = n_in // 32
                cols = scl_off[which] + c * nb_
                nc.vector.wait_ge(spw.h, spw.n)
                V.tensor_reduce(SCLS[:, cols:cols + nb_],
                                prepw[:, :n_in].rearrange("p (b c) -> p b c", c=32),
                                AX, OP.max, apply_absolute_value=True)
                V.tensor_scalar(SCLS[:, cols:cols + nb_], SCLS[:, cols:cols + nb_],
                                c_(colb, 5), 1e-12, OP.mult, OP.max)
                V.reciprocal(SCLR[:, cols:cols + nb_], SCLS[:, cols:cols + nb_])
                prep_vmark[sl] = s_dv.n
            else:
                n_in = I3 if which in ("g", "u") else E
                nc.vector.wait_ge(spw.h, spw.n)
                if sps.n > 0:
                    nc.vector.wait_ge(sps.h, sps.n)
                V.tensor_tensor(prepw[:, :n_in], prepw[:, :n_in],
                                prepr[:, :n_in], OP.mult)
                V.tensor_scalar(prepw[:, :n_in], prepw[:, :n_in], MAGIC, MAGIC,
                                OP.add, OP.subtract)
                V.tensor_tensor(prepw[:, :n_in], prepw[:, :n_in],
                                preps[:, :n_in], OP.mult)
                osps = prep_ps[1 - sl]
                if osps.n > 0:
                    nc.vector.wait_ge(osps.h, osps.n)
                V.tensor_copy(prepq[:, :n_in], prepw[:, :n_in])
                V.tensor_tensor(prepl[:, :n_in], prepw[:, :n_in],
                                prepq[:, :n_in], OP.subtract)
                prep_vmark[sl] = s_dv.n
                nc.scalar.wait_ge(s_dv.h, s_dv.n)
                for half in range(2):
                    src_t = (prepq if half == 0 else prepl)[:, :n_in]
                    C.dma_start(out=whl_d[(which, half)][c],
                                in_=src_t).then_inc(sps.h, 16)
                sps.n += 32

        def emit_tbatch():
                nc.tensor.wait_ge(s_dv.h, s_dv.n)
                jobs = []
                for wh in ("g", "u", "d"):
                    nchunks = ICH if wh in ("g", "u") else ECH
                    nb2 = 32 if wh in ("g", "u") else 96
                    for cc2 in range(nchunks):
                        for half2, (srcb, dstb) in enumerate(
                                ((SCLS, sc_d[wh][0]), (SCLR, sc_d[wh][1]))):
                            jobs.append((wh, cc2, nb2, srcb, dstb))
                tsems = [prep_ps[0], prep_ps[1], prep_pw[0], prep_pw[1]]
                vmarks = []
                smarks = []
                for ji, (wh, cc2, nb2, srcb, dstb) in enumerate(jobs):
                    cols = scl_off[wh] + cc2 * nb2
                    pslot = ji % 2
                    preg = ps_all[:nb2, (6 + pslot) * NB:(6 + pslot) * NB + 128]
                    if len(vmarks) >= 2:
                        nc.tensor.wait_ge(s_dv.h, vmarks[-2])
                    T.transpose(preg, srcb[:, cols:cols + nb2],
                                ident[:]).then_inc(s_mm.h, 1)
                    s_mm.n += 1
                    tslot = ji % 4
                    ts_ = tsems[tslot]
                    nc.vector.wait_ge(s_mm.h, s_mm.n)
                    if len(smarks) >= 4:
                        nc.vector.wait_ge(ts_.h, smarks[-4])
                    V.tensor_copy(tbuf2[:nb2, tslot * 128:(tslot + 1) * 128], preg)
                    vmarks.append(s_dv.n)
                    nc.scalar.wait_ge(s_dv.h, s_dv.n)
                    C.dma_start(out=dstb[:nb2, cc2 * 128:(cc2 + 1) * 128],
                                in_=tbuf2[:nb2, tslot * 128:(tslot + 1) * 128]
                                ).then_inc(ts_.h, 16)
                    ts_.n += 16
                    smarks.append(ts_.n)

        prep_iter = iter(prep_units)
        prep_pending = [None]

        def flush_prep():
            if prep_pending[0] is not None:
                emit_prep_compute(*prep_pending[0])
                prep_pending[0] = None

        def take_prep(kn):
            for _ in range(kn):
                u = next(prep_iter, None)
                if u is None:
                    continue
                if u[0] == "tbatch":
                    flush_prep()
                    emit_tbatch()
                else:
                    sl = emit_prep_load(*u)
                    flush_prep()
                    prep_pending[0] = (u[0], u[1], u[2], sl)

        qkv_evac_marks = []
        qkv_spill_marks = []

        for tb in range(TB):
            SY.wait_ge(s_dv.h, s_dv.n)
            SY.dma_start(out=xts[:].rearrange("p (c n) -> p c n", n=NB),
                         in_=xT_in[:, :, tb * NB:(tb + 1) * NB].transpose([1, 0, 2])
                         ).then_inc(s_x.h, 16)
            s_x.n += 16

            nc.vector.wait_ge(s_x.h, s_x.n)
            nc.vector.wait_ge(s_mm.h, s_mm.n)
            V.tensor_copy(xh_s, xts[:])
            V.tensor_tensor(xl_s, xts[:], xh_s, OP.subtract)
            sd_split = s_dv.n

            tiles = [("q", oc, None) for oc in range(ECH)] \
                + [("k", oc, None) for oc in range(ECH)] \
                + [("v", t_, h_) for t_ in range(NB // 128) for h_ in range(2)]
            for idx, (w, a, b_) in enumerate(tiles):
                pb = idx % 4
                nc.tensor.wait_ge(s_dv.h, sd_split)
                if len(qkv_evac_marks) >= 4:
                    nc.tensor.wait_ge(s_dv.h, qkv_evac_marks[-4])
                if w in ("q", "k"):
                    for ec in range(ECH):
                        lw = awq[w][:, ec * E + a * 128: ec * E + a * 128 + 128]
                        T.matmul(bank(pb), lw, xh_s[:, ec * NB:(ec + 1) * NB],
                                 start=(ec == 0), stop=False)
                        last = T.matmul(bank(pb), lw, xl_s[:, ec * NB:(ec + 1) * NB],
                                        start=False, stop=(ec == ECH - 1))
                else:
                    for ec in range(ECH):
                        lwh = xh_s[:, ec * NB + a * 128: ec * NB + a * 128 + 128]
                        lwl = xl_s[:, ec * NB + a * 128: ec * NB + a * 128 + 128]
                        rh = awq["v"][:, ec * E + b_ * NB: ec * E + (b_ + 1) * NB]
                        T.matmul(bank(pb), lwh, rh, start=(ec == 0), stop=False)
                        last = T.matmul(bank(pb), lwl, rh, start=False,
                                        stop=(ec == ECH - 1))
                last.then_inc(s_mm.h, 1)
                s_mm.n += 1

                nc.vector.wait_ge(s_mm.h, s_mm.n)
                sqs = s_sq[pb % 2]
                if sqs.n > 0:
                    nc.vector.wait_ge(sqs.h, sqs.n)
                st_ = qst[:, 0:NB]
                hi = qhl[:, (pb % 2) * 2 * NB:(pb % 2) * 2 * NB + NB]
                lo = qhl[:, (pb % 2) * 2 * NB + NB:(pb % 2) * 2 * NB + 2 * NB]
                V.tensor_scalar(st_, bank(pb), c_(scl, SW[w]), None, OP.mult)
                V.tensor_copy(hi, st_)
                V.tensor_tensor(lo, st_, hi, OP.subtract)
                qkv_evac_marks.append(s_dv.n)

                nc.scalar.wait_ge(s_dv.h, s_dv.n)
                if w == "q":
                    C.dma_start(out=qh_d[tb, a], in_=hi).then_inc(sqs.h, 16)
                    C.dma_start(out=ql_d[tb, a], in_=lo).then_inc(sqs.h, 16)
                elif w == "k":
                    KPT = NB // 128
                    C.dma_start(out=kh_d[tb * KPT:(tb + 1) * KPT, a]
                                .transpose([1, 0, 2]),
                                in_=hi.rearrange("p (t r) -> p t r", r=128)
                                ).then_inc(sqs.h, 16)
                    C.dma_start(out=kl_d[tb * KPT:(tb + 1) * KPT, a]
                                .transpose([1, 0, 2]),
                                in_=lo.rearrange("p (t r) -> p t r", r=128)
                                ).then_inc(sqs.h, 16)
                else:
                    kt_ = tb * (NB // 128) + a
                    C.dma_start(out=vh_d[kt_, :, b_ * NB:(b_ + 1) * NB],
                                in_=hi).then_inc(sqs.h, 16)
                    C.dma_start(out=vl_d[kt_, :, b_ * NB:(b_ + 1) * NB],
                                in_=lo).then_inc(sqs.h, 16)
                sqs.n += 32
                qkv_spill_marks.append((pb % 2, sqs.n))
                if idx % 2 == 1:
                    take_prep(1)

        take_prep(len(prep_units))
        flush_prep()
        stQ.close()

        stT = contextlib.ExitStack()
        SS = sb("SS", [128, 2 * KT * NB], BF16, stT)
        QT = sb("QT", [128, 2 * ECH * NB], BF16, stT)
        KV = sb("KV", [128, 2 * 2 * ECH * 128], BF16, stT)
        VV = sb("VV", [128, 2 * 2 * E], BF16, stT)
        S32 = sb("S32", [128, 2 * NB], F32, stT)
        SUMR = sb("SUMR", [128, NB], F32, stT)
        X1P = sb("X1P", [128, ECH * NB], F32, stT)
        XB = sb("XB", [128, 2 * ECH * NB], BF16, stT)
        MEAN = sb("MEAN", [128, NB], F32, stT)
        VARB = sb("VARB", [128, NB], F32, stT)
        RSTD = sb("RSTD", [128, NB], F32, stT)
        X1O = sb("X1O", [128, 2 * NB], F32, stT)

        ah_s, al_s = xh_s, xl_s

        def sh_t(kt):
            return SS[:, kt * NB:(kt + 1) * NB]

        def sl_t(kt):
            return SS[:, KT * NB + kt * NB:KT * NB + (kt + 1) * NB]

        SY.wait_ge(s_sq[0].h, s_sq[0].n)
        SY.wait_ge(s_sq[1].h, s_sq[1].n)
        SY.wait_ge(s_mm.h, s_mm.n)
        SY.wait_ge(s_dv.h, s_dv.n)
        SY.wait_ge(s_sc.h, s_sc.n)
        SY.wait_ge(s_gp.h, s_gp.n)
        SY.wait_ge(prep_ps[0].h, prep_ps[0].n)
        SY.wait_ge(prep_ps[1].h, prep_ps[1].n)
        SY.wait_ge(prep_pw[0].h, prep_pw[0].n)
        SY.wait_ge(prep_pw[1].h, prep_pw[1].n)
        prev_evac = 0
        for qb in range(TB):
            SY.wait_ge(s_mm.h, s_mm.n)
            SY.dma_start(out=QT[:, :ECH * NB].rearrange("p (c n) -> p c n", n=NB),
                         in_=qh_d[qb].transpose([1, 0, 2])).then_inc(s_q.h, 16)
            SY.dma_start(out=QT[:, ECH * NB:].rearrange("p (c n) -> p c n", n=NB),
                         in_=ql_d[qb].transpose([1, 0, 2])).then_inc(s_q.h, 16)
            s_q.n += 32
            q_ready = s_q.n

            base_kmm = s_mm.n
            k_marks = []
            exp_marks = []
            sl_marks = []
            for kt in range(KT):
                dbk = kt % 2
                if kt >= 2:
                    SY.wait_ge(s_mm.h, base_kmm + kt - 1)
                skp = s_kp[dbk]
                SY.dma_start(
                    out=KV[:, dbk * 2 * ECH * 128: dbk * 2 * ECH * 128 + ECH * 128]
                    .rearrange("p (c n) -> p c n", n=128),
                    in_=kh_d[kt].transpose([1, 0, 2])).then_inc(skp.h, 16)
                SY.dma_start(
                    out=KV[:, dbk * 2 * ECH * 128 + ECH * 128:
                           (dbk + 1) * 2 * ECH * 128]
                    .rearrange("p (c n) -> p c n", n=128),
                    in_=kl_d[kt].transpose([1, 0, 2])).then_inc(skp.h, 16)
                skp.n += 32
                k_marks.append((dbk, skp.n))

            for kt in range(KT):
                dbk = kt % 2
                sbk = 6 + (kt % 2)
                nc.tensor.wait_ge(s_q.h, q_ready)
                nc.tensor.wait_ge(s_kp[k_marks[kt][0]].h, k_marks[kt][1])
                if kt < 2 and prev_evac > 0:
                    nc.tensor.wait_ge(s_dv.h, prev_evac)
                if len(exp_marks) >= 2:
                    nc.tensor.wait_ge(s_sc.h, exp_marks[-2])
                kh0 = KV[:, dbk * 2 * ECH * 128:]
                for ec in range(ECH):
                    lwh = kh0[:, ec * 128:(ec + 1) * 128]
                    lwl = kh0[:, ECH * 128 + ec * 128: ECH * 128 + (ec + 1) * 128]
                    T.matmul(bank(sbk), lwh, QT[:, ec * NB:(ec + 1) * NB],
                             start=(ec == 0), stop=False)
                    T.matmul(bank(sbk), lwh,
                             QT[:, ECH * NB + ec * NB: ECH * NB + (ec + 1) * NB],
                             start=False, stop=False)
                    last = T.matmul(bank(sbk), lwl, QT[:, ec * NB:(ec + 1) * NB],
                                    start=False, stop=(ec == ECH - 1))
                last.then_inc(s_mm.h, 1)
                s_mm.n += 1

                if kt >= 1:
                    nc.tensor.wait_ge(s_dv.h, s_dv.n)
                    T.matmul(bank(5), ones_b[:], sh_t(kt - 1),
                             start=(kt == 1), stop=False)
                    T.matmul(bank(5), ones_b[:], sl_t(kt - 1),
                             start=False, stop=False)

                nc.scalar.wait_ge(s_mm.h, s_mm.n)
                if len(sl_marks) >= 2:
                    nc.scalar.wait_ge(s_dv.h, sl_marks[-2])
                C.activation(S32[:, dbk * NB:(dbk + 1) * NB], bank(sbk), ACT.Exp,
                             bias=bexp_c[:], scale=0.125)
                C.activation(sh_t(kt), bank(sbk), ACT.Exp,
                             bias=bexp_c[:], scale=0.125)
                exp_marks.append(s_sc.n)

                nc.vector.wait_ge(s_sc.h, s_sc.n)
                V.tensor_tensor(sl_t(kt), S32[:, dbk * NB:(dbk + 1) * NB],
                                sh_t(kt), OP.subtract)
                sl_marks.append(s_dv.n)

            nc.tensor.wait_ge(s_dv.h, s_dv.n)
            T.matmul(bank(5), ones_b[:], sh_t(KT - 1),
                     start=(KT == 1), stop=False)
            T.matmul(bank(5), ones_b[:], sl_t(KT - 1),
                     start=False, stop=True).then_inc(s_mm.h, 1)
            s_mm.n += 1

            nc.vector.wait_ge(s_mm.h, s_mm.n)
            V.reciprocal(SUMR[:], bank(5))
            sumr_done = s_dv.n

            base_bmm = s_mm.n
            for kt in range(KT):
                dbk = kt % 2
                if kt >= 2:
                    SY.wait_ge(s_mm.h, base_bmm + kt - 1)
                svp = s_vp[dbk]
                SY.dma_start(out=VV[:, dbk * 2 * E: dbk * 2 * E + E],
                             in_=vh_d[kt]).then_inc(svp.h, 16)
                SY.dma_start(out=VV[:, dbk * 2 * E + E: (dbk + 1) * 2 * E],
                             in_=vl_d[kt]).then_inc(svp.h, 16)
                svp.n += 32

                nc.tensor.wait_ge(svp.h, svp.n)
                if kt == 0:
                    nc.tensor.wait_ge(s_dv.h, sumr_done)
                    nc.tensor.wait_ge(s_sc.h, s_sc.n)
                vh0 = VV[:, dbk * 2 * E:]
                for ec in range(ECH):
                    lwh = vh0[:, ec * 128:(ec + 1) * 128]
                    lwl = vh0[:, E + ec * 128: E + (ec + 1) * 128]
                    T.matmul(bank(ec), lwh, sh_t(kt), start=(kt == 0), stop=False)
                    T.matmul(bank(ec), lwh, sl_t(kt), start=False, stop=False)
                    last = T.matmul(bank(ec), lwl, sh_t(kt), start=False,
                                    stop=(kt == KT - 1))
                last.then_inc(s_mm.h, 1)
                s_mm.n += 1

            nc.vector.wait_ge(s_mm.h, s_mm.n)
            for ec in range(ECH):
                st_ = qst[:, 0:NB]
                V.tensor_tensor(st_, bank(ec), SUMR[:], OP.mult)
                V.tensor_copy(ah_s[:, ec * NB:(ec + 1) * NB], st_)
                V.tensor_tensor(al_s[:, ec * NB:(ec + 1) * NB], st_,
                                ah_s[:, ec * NB:(ec + 1) * NB], OP.subtract)
            evac_done = s_dv.n
            prev_evac = evac_done

            SY.wait_ge(s_dv.h, evac_done)
            SY.dma_start(out=xts[:].rearrange("p (c n) -> p c n", n=NB),
                         in_=xT_in[:, :, qb * NB:(qb + 1) * NB].transpose([1, 0, 2])
                         ).then_inc(s_x.h, 16)
            s_x.n += 16
            xts_ready = s_x.n

            proj_evac_marks = []
            for oc in range(ECH):
                pb = oc % 2
                nc.tensor.wait_ge(s_dv.h, evac_done)
                if len(proj_evac_marks) >= 2:
                    nc.tensor.wait_ge(s_dv.h, proj_evac_marks[-2])
                for ec in range(ECH):
                    lw = awq["p"][:, ec * E + oc * 128: ec * E + oc * 128 + 128]
                    T.matmul(bank(pb), lw, ah_s[:, ec * NB:(ec + 1) * NB],
                             start=(ec == 0), stop=False)
                    last = T.matmul(bank(pb), lw, al_s[:, ec * NB:(ec + 1) * NB],
                                    start=False, stop=(ec == ECH - 1))
                last.then_inc(s_mm.h, 1)
                s_mm.n += 1

                nc.vector.wait_ge(s_mm.h, s_mm.n)
                if oc == 0:
                    nc.vector.wait_ge(s_x.h, xts_ready)
                xp = X1P[:, oc * NB:(oc + 1) * NB]
                V.tensor_scalar(xp, bank(pb), c_(scl, SW["p"]),
                                c_(bias_sb, oc), OP.mult, OP.add)
                V.tensor_tensor(xp, xp, xts[:, oc * NB:(oc + 1) * NB], OP.add)
                V.tensor_copy(XB[:, oc * NB:(oc + 1) * NB], xp)
                proj_evac_marks.append(s_dv.n)

                nc.scalar.wait_ge(s_dv.h, s_dv.n)
                C.activation(XB[:, ECH * NB + oc * NB: ECH * NB + (oc + 1) * NB],
                             XB[:, oc * NB:(oc + 1) * NB], ACT.Square)

                nc.tensor.wait_ge(s_sc.h, s_sc.n)
                T.matmul(bank(2), ones_b[:], XB[:, oc * NB:(oc + 1) * NB],
                         start=(oc == 0), stop=(oc == ECH - 1))
                mm2 = T.matmul(bank(3), ones_b[:],
                               XB[:, ECH * NB + oc * NB: ECH * NB + (oc + 1) * NB],
                               start=(oc == 0), stop=(oc == ECH - 1))
                if oc == ECH - 1:
                    mm2.then_inc(s_mm.h, 1)
                    s_mm.n += 1

            nc.vector.wait_ge(s_mm.h, s_mm.n)
            V.tensor_scalar(MEAN[:], bank(2), 1.0 / E, None, OP.mult)
            V.tensor_scalar(VARB[:], bank(3), 1.0 / E, None, OP.mult)
            V.tensor_tensor(RSTD[:], MEAN[:], MEAN[:], OP.mult)
            V.tensor_tensor(VARB[:], VARB[:], RSTD[:], OP.subtract)
            nc.scalar.wait_ge(s_dv.h, s_dv.n)
            C.activation(RSTD[:], VARB[:], ACT.Sqrt, bias=eps_c[:])
            nc.vector.wait_ge(s_sc.h, s_sc.n)
            V.reciprocal(RSTD[:], RSTD[:])

            for oc in range(ECH):
                sxs = s_sx[oc % 2]
                if sxs.n > 0:
                    nc.vector.wait_ge(sxs.h, sxs.n)
                xo = X1O[:, (oc % 2) * NB:((oc % 2) + 1) * NB]
                xp = X1P[:, oc * NB:(oc + 1) * NB]
                V.tensor_tensor(xo, xp, MEAN[:], OP.subtract)
                V.tensor_tensor(xo, xo, RSTD[:], OP.mult)
                V.reduce_max(c_(colb, 6), xo, AX, apply_absolute_value=True)
                V.tensor_tensor(c_(colb, 0), c_(colb, 0), c_(colb, 6), OP.max)

                nc.scalar.wait_ge(s_dv.h, s_dv.n)
                C.dma_start(out=x1_d[oc, :, qb * NB:(qb + 1) * NB],
                            in_=xo).then_inc(sxs.h, 16)
                sxs.n += 16

        stT.close()
        stA.close()

        def emit_collective(idx, cols, finish_fn):
            nc.tensor.wait_ge(s_dv.h, s_dv.n)
            T.transpose(prow, colb[:, cols[0]:cols[0] + 1], ident[:]
                        ).then_inc(s_mm.h, 1)
            s_mm.n += 1
            if len(cols) > 1:
                T.transpose(prow2, colb[:, cols[1]:cols[1] + 1], ident[:]
                            ).then_inc(s_mm.h, 1)
                s_mm.n += 1
            nc.vector.wait_ge(s_mm.h, s_mm.n)
            V.reduce_max(ccs[:, 1:2], prow, AX)
            if len(cols) > 1:
                V.reduce_max(ccs[:, 2:3], prow2, AX)
            nc.gpsimd.wait_ge(s_dv.h, s_dv.n)
            G.dma_start(out=cc_in[idx], in_=ccs[:]).then_inc(s_gd.h, 16)
            s_gd.n += 16
            nc.gpsimd.wait_ge(s_gd.h, s_gd.n)
            G.collective_compute(
                "AllReduce", OP.max, replica_groups=[list(range(8))],
                ins=[cc_in[idx].opt()], outs=[cc_out[idx].opt()]
            ).then_inc(s_cc.h, 1)
            s_cc.n += 1
            nc.gpsimd.wait_ge(s_cc.h, s_cc.n)
            G.dma_start(out=ccs[:], in_=cc_out[idx]).then_inc(s_gd.h, 16)
            s_gd.n += 16
            nc.vector.wait_ge(s_gd.h, s_gd.n)
            finish_fn()
            nc.tensor.wait_ge(s_dv.h, s_dv.n)
            T.matmul(pbc8, ones_r[:], row[:1, 16:24], start=True, stop=True
                     ).then_inc(s_mm.h, 1)
            s_mm.n += 1
            nc.vector.wait_ge(s_mm.h, s_mm.n)
            nc.vector.wait_ge(s_sc.h, s_sc.n)
            V.tensor_copy(scl[:, 0:8], pbc8)
            return s_dv.n

        FSTI = sb("FSTI", [128, 2 * SL], F32, ctx)
        FOUT = sb("FOUT", [128, 2 * SL], F32, ctx)
        WCH = sb("WCH", [128, 2 * 2 * ECH * 128], BF16, ctx)
        stN = contextlib.ExitStack()
        NL = sb("NL", [128, 2 * ECH * S], BF16, stN)
        nh_s = NL[:, :ECH * S]
        nl_s = NL[:, ECH * S:]

        def fin1():
            V.tensor_scalar(c_(row, RSX), ccs[:, 1:2], c_(row, R32K), 1e-12,
                            OP.mult, OP.max)
            V.reciprocal(row[:1, 16:17], c_(row, RSX))
        emit_collective(0, (0,), fin1)

        SY.wait_ge(s_sx[0].h, s_sx[0].n)
        SY.wait_ge(s_sx[1].h, s_sx[1].n)
        p3_marks = []
        for tb in range(TB):
            dbk = tb % 2
            if len(p3_marks) >= 2:
                SY.wait_ge(s_dv.h, p3_marks[-2][0])
                SY.wait_ge(s_gp.h, p3_marks[-2][1])
            sfp = s_fp[dbk]
            SY.dma_start(out=FSTI[:, dbk * SL: dbk * SL + ECH * NB]
                         .rearrange("p (c n) -> p c n", n=NB),
                         in_=x1_d[:, :, tb * NB:(tb + 1) * NB].transpose([1, 0, 2])
                         ).then_inc(sfp.h, 16)
            sfp.n += 16

            nc.vector.wait_ge(sfp.h, sfp.n)
            src = FSTI[:, dbk * SL: dbk * SL + ECH * NB]
            st_ = FOUT[:, dbk * SL: dbk * SL + ECH * NB]
            st3 = st_.rearrange("p (c n) -> p c n", n=NB)
            nh_v = nh_s.rearrange("p (c s) -> p c s", s=S)[:, :, tb * NB:(tb + 1) * NB]
            nl_v = nl_s.rearrange("p (c s) -> p c s", s=S)[:, :, tb * NB:(tb + 1) * NB]
            V.tensor_scalar(st_, src, c_(scl, ISXc), MAGIC, OP.mult, OP.add)
            V.tensor_scalar(st_, st_, MAGIC, None, OP.subtract)
            V.tensor_copy(nh_v, st3)
            V.tensor_tensor(nl_v, st3, nh_v, OP.subtract)
            p3_marks.append((s_dv.n, s_gp.n))
        p3_done = s_dv.n

        SY.wait_ge(prep_ps[0].h, prep_ps[0].n)
        SY.wait_ge(prep_ps[1].h, prep_ps[1].n)
        wch_last = {0: 0, 1: 0}
        ev_marks = []

        def emit_w_chunk_load(which, ich):
            dbw = ich % 2
            if wch_last[dbw] > 0:
                SY.wait_ge(s_mm.h, wch_last[dbw])
            skp = s_kp[dbw]
            for half in range(2):
                src = whl_d[(which, half)][:, :, ich * 128:(ich + 1) * 128]
                SY.dma_start(
                    out=WCH[:, dbw * 2 * ECH * 128 + half * ECH * 128:
                           dbw * 2 * ECH * 128 + (half + 1) * ECH * 128]
                    .rearrange("p (c n) -> p c n", n=128),
                    in_=src.transpose([1, 0, 2])).then_inc(skp.h, 16)
            skp.n += 32
            return dbw, skp

        def emit_wmm_tiles(ich, dbw):
            for tb in range(TB):
                pb = (ich * TB + tb) % 2
                nc.tensor.wait_ge(s_dv.h, p3_marks[tb][0])
                if len(ev_marks) >= 2:
                    nc.tensor.wait_ge(s_sc.h, ev_marks[-2])
                w0 = WCH[:, dbw * 2 * ECH * 128:]
                for ec in range(ECH):
                    lwh = w0[:, ec * 128:(ec + 1) * 128]
                    lwl = w0[:, ECH * 128 + ec * 128: ECH * 128 + (ec + 1) * 128]
                    nh_v = nh_s[:, ec * S + tb * NB: ec * S + (tb + 1) * NB]
                    nl_v = nl_s[:, ec * S + tb * NB: ec * S + (tb + 1) * NB]
                    T.matmul(bank(pb), lwh, nh_v, start=(ec == 0), stop=False)
                    T.matmul(bank(pb), lwh, nl_v, start=False, stop=False)
                    last = T.matmul(bank(pb), lwl, nh_v, start=False,
                                    stop=(ec == ECH - 1))
                last.then_inc(s_mm.h, 1)
                s_mm.n += 1
                wch_last[dbw] = s_mm.n
                yield tb, pb

        for ich in range(ICH):
            dbw, skp = emit_w_chunk_load("g", ich)
            nc.tensor.wait_ge(skp.h, skp.n)
            sos = s_so[ich % 2]
            for tb, pb in emit_wmm_tiles(ich, dbw):
                nc.scalar.wait_ge(s_mm.h, s_mm.n)
                if tb == 0 and sos.n > 0:
                    nc.scalar.wait_ge(sos.h, sos.n)
                off = (ich % 2) * SL
                C.activation(FOUT[:, off + tb * NB: off + (tb + 1) * NB],
                             bank(pb), ACT.Copy)
                ev_marks.append(s_sc.n)
                nc.vector.wait_ge(s_sc.h, s_sc.n)
                V.reduce_max(c_(colb, 6),
                             FOUT[:, off + tb * NB: off + (tb + 1) * NB],
                             AX, apply_absolute_value=True)
                V.tensor_tensor(c_(colb, 2), c_(colb, 2), c_(colb, 6), OP.max)
            nc.scalar.wait_ge(s_dv.h, s_dv.n)
            C.dma_start(out=gtr_d[ich],
                        in_=FOUT[:, (ich % 2) * SL:(ich % 2) * SL + TB * NB]
                        ).then_inc(sos.h, 16)
            sos.n += 16

        def fin2g():
            V.tensor_tensor(c_(row, RT0), ccs[:, 1:2], c_(row, RSX), OP.mult)
            V.tensor_scalar(c_(row, RSG), c_(row, RT0), c_(row, R32K), 1e-12,
                            OP.mult, OP.max)
            V.reciprocal(c_(row, RT0), c_(row, RSG))
            V.tensor_tensor(row[:1, 17:18], c_(row, RSX), c_(row, RT0), OP.mult)
            V.tensor_copy(row[:1, 18:19], c_(row, RSG))
        emit_collective(1, (2,), fin2g)

        SY.wait_ge(s_so[0].h, s_so[0].n)
        SY.wait_ge(s_so[1].h, s_so[1].n)
        f2_spill_mark = [0]
        f4_state = {"vmark": 0, "spill": 0}

        def f4_units():
            for ich in range(ICH):
                src = FSTI[:, 0:S]
                gq = FSTI[:, SL:SL + S] if ich % 2 == 0 else FOUT[:, SL:SL + S]
                if f4_state["vmark"] > 0:
                    SY.wait_ge(s_dv.h, f4_state["vmark"])
                if f4_state["spill"] > 0:
                    SY.wait_ge(s_gn[1].h, f4_state["spill"])
                SY.dma_start(out=src, in_=gtr_d[ich]).then_inc(s_gn[0].h, 16)
                s_gn[0].n += 16

                nc.vector.wait_ge(s_gn[0].h, s_gn[0].n)
                V.tensor_scalar(gq, src, c_(scl, KGc), MAGIC, OP.mult, OP.add)
                V.tensor_scalar(gq, gq, MAGIC, None, OP.subtract)
                nc.scalar.wait_ge(s_dv.h, s_dv.n)
                C.activation(src, gq, ACT.Sigmoid, scale=c_(scl, SGc))
                C.activation(src, src, ACT.Identity, bias=magic_c[:],
                             scale=65536.0)
                C.activation(src, src, ACT.Identity, bias=nmagic_c[:])
                nc.vector.wait_ge(s_sc.h, s_sc.n)
                V.tensor_scalar(src, src, 65535.0, None, OP.min)
                V.tensor_tensor(src, src, gq, OP.mult)
                V.reduce_max(c_(colb, 6), src, AX, apply_absolute_value=True)
                V.tensor_tensor(c_(colb, 3), c_(colb, 3), c_(colb, 6), OP.max)
                f4_state["vmark"] = s_dv.n
                nc.scalar.wait_ge(s_dv.h, s_dv.n)
                C.dma_start(out=tr_d[ich], in_=src).then_inc(s_gn[1].h, 16)
                s_gn[1].n += 16
                f4_state["spill"] = s_gn[1].n
                yield
        f4_iter = f4_units()

        for ich in range(ICH):
            dbw, skp = emit_w_chunk_load("u", ich)
            nc.tensor.wait_ge(skp.h, skp.n)
            for tb, pb in emit_wmm_tiles(ich, dbw):
                nc.scalar.wait_ge(s_mm.h, s_mm.n)
                if tb == 0 and f2_spill_mark[0] > 0:
                    nc.scalar.wait_ge(s_so[0].h, f2_spill_mark[0])
                C.activation(FOUT[:, tb * NB:(tb + 1) * NB], bank(pb), ACT.Copy)
                ev_marks.append(s_sc.n)
                nc.vector.wait_ge(s_sc.h, s_sc.n)
                V.reduce_max(c_(colb, 6), FOUT[:, tb * NB:(tb + 1) * NB],
                             AX, apply_absolute_value=True)
                V.tensor_tensor(c_(colb, 1), c_(colb, 1), c_(colb, 6), OP.max)
            nc.scalar.wait_ge(s_dv.h, s_dv.n)
            C.dma_start(out=upr_d[ich], in_=FOUT[:, 0:TB * NB]
                        ).then_inc(s_so[0].h, 16)
            s_so[0].n += 16
            f2_spill_mark[0] = s_so[0].n
            next(f4_iter, None)
        for _ in f4_iter:
            pass

        def fin23():
            V.tensor_tensor(c_(row, RT0), ccs[:, 1:2], c_(row, RSX), OP.mult)
            V.tensor_scalar(c_(row, RSU), c_(row, RT0), c_(row, R32K), 1e-12,
                            OP.mult, OP.max)
            V.reciprocal(c_(row, RT0), c_(row, RSU))
            V.tensor_tensor(row[:1, 19:20], c_(row, RSX), c_(row, RT0), OP.mult)
            V.tensor_scalar(c_(row, RT0), ccs[:, 2:3], 1.0 / 65536.0, None,
                            OP.mult)
            V.tensor_tensor(c_(row, RT0), c_(row, RT0), c_(row, RSG), OP.mult)
            V.tensor_scalar(c_(row, RT0), c_(row, RT0), c_(row, R32K), None,
                            OP.mult)
            V.tensor_scalar(c_(row, RST), c_(row, RT0), 1e-12, None, OP.max)
            V.reciprocal(c_(row, RT0), c_(row, RST))
            V.tensor_scalar(c_(row, RT1), c_(row, RSG), 1.0 / 65536.0, None,
                            OP.mult)
            V.tensor_tensor(row[:1, 20:21], c_(row, RT1), c_(row, RT0), OP.mult)
        bc23 = emit_collective(2, (1, 3), fin23)

        stN.close()

        stF4 = contextlib.ExitStack()
        GN = sb("GN", [128, 2 * SL], F32, stF4)

        SY.wait_ge(s_so[0].h, s_so[0].n)
        SY.wait_ge(s_gn[1].h, s_gn[1].n)
        f5_dv = []
        f5_sc = []
        for ich in range(ICH):
            dbk = ich % 2
            if len(f5_dv) >= 2:
                SY.wait_ge(s_dv.h, f5_dv[-2])
            sfp = s_fp[dbk]
            sgn = s_gn[dbk]
            SY.dma_start(out=FSTI[:, dbk * SL: dbk * SL + S],
                         in_=upr_d[ich]).then_inc(sfp.h, 16)
            SY.dma_start(out=GN[:, dbk * SL: dbk * SL + S],
                         in_=tr_d[ich]).then_inc(sgn.h, 16)
            sfp.n += 16
            sgn.n += 16

            up = FSTI[:, dbk * SL: dbk * SL + S]
            tr = GN[:, dbk * SL: dbk * SL + S]
            ot = FOUT[:, dbk * SL: dbk * SL + S]
            nc.scalar.wait_ge(sfp.h, sfp.n)
            if ich == 0:
                nc.scalar.wait_ge(s_dv.h, bc23)
            C.activation(up, up, ACT.Identity, bias=magic_c[:], scale=c_(scl, KUc))
            C.activation(up, up, ACT.Identity, bias=nmagic_c[:])
            f5_sc.append(s_sc.n)

            nc.vector.wait_ge(sgn.h, sgn.n)
            sos = s_or[dbk]
            if ich == 0:
                nc.vector.wait_ge(s_so[0].h, s_so[0].n)
            if sos.n > 0:
                nc.vector.wait_ge(sos.h, sos.n)
            V.tensor_scalar(tr, tr, c_(scl, KTc), MAGIC, OP.mult, OP.add)
            V.tensor_scalar(tr, tr, MAGIC, None, OP.subtract)
            nc.vector.wait_ge(s_sc.h, f5_sc[-1])
            V.tensor_tensor(ot, tr, up, OP.mult)
            V.reduce_max(c_(colb, 6), ot, AX, apply_absolute_value=True)
            V.tensor_tensor(c_(colb, 4), c_(colb, 4), c_(colb, 6), OP.max)
            f5_dv.append(s_dv.n)

            nc.gpsimd.wait_ge(s_dv.h, s_dv.n)
            G.dma_start(out=or_d[ich], in_=FOUT[:, dbk * SL: dbk * SL + TB * NB]
                        ).then_inc(sos.h, 16)
            sos.n += 16

        stF4.close()


        def fin4():
            V.tensor_tensor(c_(row, RT0), c_(row, RST), c_(row, RSU), OP.mult)
            V.tensor_tensor(c_(row, RT1), ccs[:, 1:2], c_(row, RT0), OP.mult)
            V.tensor_scalar(c_(row, RSO), c_(row, RT1), c_(row, R32K), 1e-12,
                            OP.mult, OP.max)
            V.reciprocal(c_(row, RT1), c_(row, RSO))
            V.tensor_tensor(row[:1, 21:22], c_(row, RT0), c_(row, RT1), OP.mult)
            V.tensor_copy(row[:1, 22:23], c_(row, RSO))
        emit_collective(3, (4,), fin4)

        stF2 = contextlib.ExitStack()
        WD = sb("WD", [128, 2 * ICH * E], BF16, stF2)
        OTB = sb("OTB", [128, 2 * 2 * NB], BF16, stF2)
        XSB = sb("XSB", [128, 4 * NB], BF16, stF2)

        SY.wait_ge(s_or[0].h, s_or[0].n)
        SY.wait_ge(s_or[1].h, s_or[1].n)
        SY.dma_start(out=WD[:, :ICH * E].rearrange("p (c n) -> p c n", n=E),
                     in_=whl_d[("d", 0)].transpose([1, 0, 2])).then_inc(s_kp[0].h, 16)
        SY.dma_start(out=WD[:, ICH * E:].rearrange("p (c n) -> p c n", n=E),
                     in_=whl_d[("d", 1)].transpose([1, 0, 2])).then_inc(s_kp[0].h, 16)
        s_kp[0].n += 32
        nc.tensor.wait_ge(s_kp[0].h, s_kp[0].n)

        for tb in range(TB):
            SY.wait_ge(s_dv.h, s_dv.n)
            SY.wait_ge(s_out.h, s_out.n)
            SY.dma_start(out=FSTI[:, SL:SL + ECH * NB]
                         .rearrange("p (c n) -> p c n", n=NB),
                         in_=x1_d[:, :, tb * NB:(tb + 1) * NB].transpose([1, 0, 2])
                         ).then_inc(s_x1b.h, 16)
            s_x1b.n += 16

            ot_marks = []
            f6_dv = []
            for ich in range(ICH):
                dbk = ich % 2
                if len(f6_dv) >= 2:
                    SY.wait_ge(s_dv.h, f6_dv[-2])
                sfp = s_fp[dbk]
                SY.dma_start(out=FSTI[:, dbk * NB:(dbk + 1) * NB],
                             in_=or_d[ich, :, tb * NB:(tb + 1) * NB]
                             ).then_inc(sfp.h, 16)
                sfp.n += 16

                nc.vector.wait_ge(sfp.h, sfp.n)
                if len(ot_marks) >= 2:
                    nc.vector.wait_ge(s_mm.h, ot_marks[-2])
                src = FSTI[:, dbk * NB:(dbk + 1) * NB]
                oh = OTB[:, dbk * 2 * NB: dbk * 2 * NB + NB]
                ol = OTB[:, dbk * 2 * NB + NB: (dbk + 1) * 2 * NB]
                V.tensor_scalar(src, src, c_(scl, KOc), MAGIC, OP.mult, OP.add)
                V.tensor_scalar(src, src, MAGIC, None, OP.subtract)
                V.tensor_copy(oh, src)
                V.tensor_tensor(ol, src, oh, OP.subtract)
                f6_dv.append(s_dv.n)

                nc.tensor.wait_ge(s_dv.h, s_dv.n)
                for hc in range(ECH):
                    lwh = WD[:, ich * E + hc * 128: ich * E + hc * 128 + 128]
                    lwl = WD[:, ICH * E + ich * E + hc * 128:
                             ICH * E + ich * E + hc * 128 + 128]
                    T.matmul(bank(hc), lwh, oh, start=(ich == 0), stop=False)
                    T.matmul(bank(hc), lwh, ol, start=False, stop=False)
                    last = T.matmul(bank(hc), lwl, oh, start=False,
                                    stop=(ich == ICH - 1))
                last.then_inc(s_mm.h, 1)
                s_mm.n += 1
                ot_marks.append(s_mm.n)

            nc.vector.wait_ge(s_mm.h, s_mm.n)
            nc.vector.wait_ge(s_x1b.h, s_x1b.n)
            for hc in range(ECH):
                xp = FOUT[:, hc * NB:(hc + 1) * NB]
                V.tensor_scalar(xp, bank(hc), c_(scl, SOc), None, OP.mult)
                V.tensor_tensor(xp, xp, FSTI[:, SL + hc * NB: SL + (hc + 1) * NB],
                                OP.add)
            f6_evac_done = s_dv.n

            xstat_marks = []
            for hc in range(ECH):
                nc.scalar.wait_ge(s_dv.h, f6_evac_done)
                if len(xstat_marks) >= 2:
                    nc.scalar.wait_ge(s_mm.h, xstat_marks[-2])
                xp = FOUT[:, hc * NB:(hc + 1) * NB]
                xb = XSB[:, (hc % 2) * 2 * NB:(hc % 2) * 2 * NB + NB]
                xq = XSB[:, (hc % 2) * 2 * NB + NB:((hc % 2) + 1) * 2 * NB]
                C.activation(xb, xp, ACT.Copy)
                C.activation(xq, xp, ACT.Square)

                nc.tensor.wait_ge(s_sc.h, s_sc.n)
                nc.tensor.wait_ge(s_dv.h, f6_evac_done)
                T.matmul(bank(0), ones_b[:], xb,
                         start=(hc == 0), stop=(hc == ECH - 1))
                T.matmul(bank(1), ones_b[:], xq,
                         start=(hc == 0), stop=(hc == ECH - 1)
                         ).then_inc(s_mm.h, 1)
                s_mm.n += 1
                xstat_marks.append(s_mm.n)

            nc.vector.wait_ge(s_mm.h, s_mm.n)
            mn = FSTI[:, 2 * NB:3 * NB]
            vb = FSTI[:, 3 * NB:4 * NB]
            rs = FSTI[:, 4 * NB:5 * NB]
            V.tensor_scalar(mn, bank(0), 1.0 / E, None, OP.mult)
            V.tensor_scalar(vb, bank(1), 1.0 / E, None, OP.mult)
            V.tensor_tensor(rs, mn, mn, OP.mult)
            V.tensor_tensor(vb, vb, rs, OP.subtract)
            nc.scalar.wait_ge(s_dv.h, s_dv.n)
            C.activation(rs, vb, ACT.Sqrt, bias=eps_c[:])
            nc.vector.wait_ge(s_sc.h, s_sc.n)
            V.reciprocal(rs, rs)

            for hc in range(ECH):
                xp = FOUT[:, hc * NB:(hc + 1) * NB]
                V.tensor_tensor(xp, xp, mn, OP.subtract)
                V.tensor_tensor(xp, xp, rs, OP.mult)

                nc.scalar.wait_ge(s_dv.h, s_dv.n)
                C.dma_start(out=out_d[hc, :, tb * NB:(tb + 1) * NB],
                            in_=xp).then_inc(s_out.h, 16)
                s_out.n += 16

        nc.gpsimd.wait_ge(s_out.h, s_out.n)
        stF2.close()
    return nc


_CACHE = {}


def _get_program(S):
    if S not in _CACHE:
        _CACHE[S] = build_program(S)
    return _CACHE[S]


def stage_inputs(L0_attn_x, L0_attn_w_query, L0_attn_w_key, L0_attn_w_value,
                 L0_attn_w_proj, L0_attn_b_proj, L0_ffn_w_gate, L0_ffn_w_up,
                 L0_ffn_w_down):
    x = np.ascontiguousarray(np.asarray(L0_attn_x, dtype=np.float32))
    B, S, _E = x.shape

    def t8(w):
        return np.ascontiguousarray(np.asarray(w, np.float32).T.reshape(ECH, 128, E))

    shared = {
        "awT_q": t8(L0_attn_w_query), "awT_k": t8(L0_attn_w_key),
        "awT_v": t8(L0_attn_w_value), "awT_p": t8(L0_attn_w_proj),
        "bproj": np.ascontiguousarray(np.asarray(L0_attn_b_proj, np.float32)
                                      .reshape(ECH, 128).T),
        "wgT": np.ascontiguousarray(np.asarray(L0_ffn_w_gate, np.float32).T
                                    .reshape(ECH, 128, I3)),
        "wuT": np.ascontiguousarray(np.asarray(L0_ffn_w_up, np.float32).T
                                    .reshape(ECH, 128, I3)),
        "wdT": np.ascontiguousarray(np.asarray(L0_ffn_w_down, np.float32).T
                                    .reshape(ICH, 128, E)),
        "wg_nat": np.ascontiguousarray(np.asarray(L0_ffn_w_gate, np.float32)
                                       .reshape(ICH, 128, E)),
        "wu_nat": np.ascontiguousarray(np.asarray(L0_ffn_w_up, np.float32)
                                       .reshape(ICH, 128, E)),
        "wd_nat": np.ascontiguousarray(np.asarray(L0_ffn_w_down, np.float32)
                                       .reshape(ECH, 128, I3)),
        "ident32": np.eye(128, dtype=np.float32),
    }
    in_maps = []
    for c in range(B):
        m = dict(shared)
        m["xT"] = np.ascontiguousarray(x[c].T.reshape(ECH, 128, S))
        in_maps.append(m)
    return in_maps


def kernel(**inputs):
    in_maps = stage_inputs(**inputs)
    B = len(in_maps)
    S = in_maps[0]["xT"].shape[2]
    nc = _get_program(S)
    res = run_bass_kernel_spmd(nc, in_maps, core_ids=list(range(B)))
    outs = []
    for c in range(B):
        y = np.asarray(res.results[c]["out"]).reshape(E, S).T
        outs.append(y)
    return np.ascontiguousarray(np.stack(outs)).astype(np.float32)


# version 09
# speedup vs baseline: 1.0448x; 1.0448x over previous
import contextlib
import numpy as np

import concourse.bass as bass
import concourse.mybir as mybir
from concourse.bass_utils import run_bass_kernel_spmd

F32 = mybir.dt.float32
BF16 = mybir.dt.bfloat16
OP = mybir.AluOpType
ACT = mybir.ActivationFunctionType
AX = mybir.AxisListType.X

MAGIC = 12582912.0
EXP_SHIFT = -24.0
E = 1024
ECH = 8
I3 = 3072
ICH = 24
NB = 512


class Sem:
    def __init__(self, ctx, nc, name):
        self.h = ctx.enter_context(nc.semaphore(name))
        self.n = 0
        self.name = name


_CHAIN_OPS = {
    "tensor_copy", "tensor_tensor", "tensor_scalar", "tensor_single_scalar",
    "tensor_add", "tensor_sub", "tensor_mul", "tensor_max", "tensor_reduce",
    "reduce_max", "reduce_sum", "reciprocal", "memset", "activation", "iota",
    "scalar_tensor_tensor", "select", "tensor_scalar_add", "tensor_scalar_sub",
    "tensor_scalar_mul", "tensor_scalar_max", "tensor_scalar_min",
}


class Eng:

    def __init__(self, eng, sem):
        self._e = eng
        self._s = sem

    def __getattr__(self, name):
        fn = getattr(self._e, name)
        if name in _CHAIN_OPS:
            def wrap(*a, **k):
                if self._s.n > 0:
                    self._e.wait_ge(self._s.h, self._s.n)
                ins = fn(*a, **k)
                ins.then_inc(self._s.h, 1)
                self._s.n += 1
                return ins
            return wrap
        if name in ("dma_start", "collective_compute", "dma_start_transpose"):
            def wrapd(*a, **k):
                if self._s.n > 0:
                    self._e.wait_ge(self._s.h, self._s.n)
                return fn(*a, **k)
            return wrapd
        return fn


def build_program(S=4096):
    TB = S // NB
    KT = S // 128
    SL = max(ECH * NB, S)

    nc = bass.Bass("TRN2", target_bir_lowering=False)

    xT_in = nc.dram_tensor("xT", [ECH, 128, S], F32, kind="ExternalInput").ap()
    awT = {w: nc.dram_tensor(f"awT_{w}", [ECH, 128, E], F32, kind="ExternalInput").ap()
           for w in ("q", "k", "v", "p")}
    bias_in = nc.dram_tensor("bproj", [128, ECH], F32, kind="ExternalInput").ap()
    wT_in = {"g": nc.dram_tensor("wgT", [ECH, 128, I3], F32, kind="ExternalInput").ap(),
             "u": nc.dram_tensor("wuT", [ECH, 128, I3], F32, kind="ExternalInput").ap(),
             "d": nc.dram_tensor("wdT", [ICH, 128, E], F32, kind="ExternalInput").ap()}
    w_nat = {"g": nc.dram_tensor("wg_nat", [ICH, 128, E], F32, kind="ExternalInput").ap(),
             "u": nc.dram_tensor("wu_nat", [ICH, 128, E], F32, kind="ExternalInput").ap(),
             "d": nc.dram_tensor("wd_nat", [ECH, 128, I3], F32, kind="ExternalInput").ap()}
    ident_in = nc.dram_tensor("ident32", [128, 128], F32, kind="ExternalInput").ap()

    out_d = nc.dram_tensor("out", [ECH, 128, S], F32, kind="ExternalOutput").ap()

    qh_d = nc.dram_tensor("qh_d", [TB, ECH, 128, NB], BF16).ap()
    ql_d = nc.dram_tensor("ql_d", [TB, ECH, 128, NB], BF16).ap()
    kh_d = nc.dram_tensor("kh_d", [KT, ECH, 128, 128], BF16).ap()
    kl_d = nc.dram_tensor("kl_d", [KT, ECH, 128, 128], BF16).ap()
    vh_d = nc.dram_tensor("vh_d", [KT, 128, E], BF16).ap()
    vl_d = nc.dram_tensor("vl_d", [KT, 128, E], BF16).ap()
    x1_d = nc.dram_tensor("x1_d", [ECH, 128, S], F32).ap()
    whl_d = {("g", 0): nc.dram_tensor("wgh_d", [ECH, 128, I3], BF16).ap(),
             ("g", 1): nc.dram_tensor("wgl_d", [ECH, 128, I3], BF16).ap(),
             ("u", 0): nc.dram_tensor("wuh_d", [ECH, 128, I3], BF16).ap(),
             ("u", 1): nc.dram_tensor("wul_d", [ECH, 128, I3], BF16).ap(),
             ("d", 0): nc.dram_tensor("wdh_d", [ICH, 128, E], BF16).ap(),
             ("d", 1): nc.dram_tensor("wdl_d", [ICH, 128, E], BF16).ap()}
    sc_d = {"g": (nc.dram_tensor("sg_d", [32, I3], F32).ap(),
                  nc.dram_tensor("sgr_d", [32, I3], F32).ap()),
            "u": (nc.dram_tensor("su_d", [32, I3], F32).ap(),
                  nc.dram_tensor("sur_d", [32, I3], F32).ap()),
            "d": (nc.dram_tensor("sd_d", [96, E], F32).ap(),
                  nc.dram_tensor("sdr_d", [96, E], F32).ap())}
    upr_d = nc.dram_tensor("upr_d", [ICH, 128, S], F32).ap()
    gtr_d = nc.dram_tensor("gtr_d", [ICH, 128, S], F32).ap()
    tr_d = nc.dram_tensor("tr_d", [ICH, 128, S], F32).ap()
    or_d = nc.dram_tensor("or_d", [ICH, 128, S], F32).ap()
    cc_in = [nc.dram_tensor(f"cc_in{i}", [1, 16], F32).ap() for i in range(4)]
    cc_out = [nc.dram_tensor(f"cc_out{i}", [1, 16], F32).ap() for i in range(4)]

    ctx = contextlib.ExitStack()
    with ctx:
        def sb(name, shape, dtype, st=None):
            return (st or ctx).enter_context(nc.sbuf_tensor(name, shape, dtype))

        ones_b = sb("ones_b", [128, 128], BF16)
        ones_r = sb("ones_r", [1, 128], F32)
        ident = sb("ident", [128, 128], F32)
        scl = sb("scl", [128, 16], F32)
        row = sb("row", [1, 32], F32)
        colb = sb("colb", [128, 8], F32)
        eps_c = sb("eps_c", [128, 1], F32)
        bexp_c = sb("bexp_c", [128, 1], F32)
        magic_c = sb("magic_c", [128, 1], F32)
        nmagic_c = sb("nmagic_c", [128, 1], F32)
        bias_sb = sb("bias_sb", [128, ECH], F32)
        ccs = sb("ccs", [1, 16], F32)

        R127, RR127, R32K, RSX, RSU, RSG, RST, RSO, RT0, RT1 = range(10)
        ISXc, KGc, SGc, KUc, KTc, KOc, SOc = range(7)
        SW = {"q": 8, "k": 9, "v": 10, "p": 11}

        def c_(t, a):
            return t[:, a:a + 1]

        ps_all = ctx.enter_context(nc.psum_tensor("psall", [128, 8 * NB], F32))

        def bank(i):
            return ps_all[:, i * NB:(i + 1) * NB]

        prow = ps_all[:1, 7 * NB:7 * NB + 128]
        prow2 = ps_all[:1, 6 * NB:6 * NB + 128]
        pbc8 = ps_all[:, 7 * NB:7 * NB + 8]
        pbc2 = ps_all[:, 7 * NB:7 * NB + 2]

        s_in = Sem(ctx, nc, "s_in")
        s_x = Sem(ctx, nc, "s_x")
        s_pw = Sem(ctx, nc, "s_pw")
        s_kp = [Sem(ctx, nc, f"s_k{i}") for i in range(2)]
        s_vp = [Sem(ctx, nc, f"s_v{i}") for i in range(2)]
        s_fp = [Sem(ctx, nc, f"s_f{i}") for i in range(2)]
        s_gn = [Sem(ctx, nc, f"s_g{i}") for i in range(2)]
        s_x1b = Sem(ctx, nc, "s_x1b")
        s_q = Sem(ctx, nc, "s_q")
        s_ps = Sem(ctx, nc, "s_ps")
        s_sq = [Sem(ctx, nc, f"s_sq{i}") for i in range(2)]
        s_sx = [Sem(ctx, nc, f"s_sx{i}") for i in range(2)]
        s_so = [Sem(ctx, nc, f"s_so{i}") for i in range(2)]
        s_or = [Sem(ctx, nc, f"s_or{i}") for i in range(2)]
        s_tr = Sem(ctx, nc, "s_tr")
        s_gd = Sem(ctx, nc, "s_gd")
        s_mm = Sem(ctx, nc, "s_mm")
        s_dv = Sem(ctx, nc, "s_dv")
        s_sc = Sem(ctx, nc, "s_sc")
        s_cc = Sem(ctx, nc, "s_cc")
        s_gp = Sem(ctx, nc, "s_gp")
        s_out = Sem(ctx, nc, "s_out")

        block = ctx.enter_context(nc.Block())
        V = Eng(nc.vector, s_dv)
        C = Eng(nc.scalar, s_sc)
        G = Eng(nc.gpsimd, s_gp)
        T = nc.tensor
        SY = nc.sync

        stA = contextlib.ExitStack()
        xts = sb("xts", [128, ECH * NB], F32, stA)
        xspl = sb("xspl", [128, 2 * ECH * NB], BF16, stA)
        qst = sb("qst", [128, NB], F32, stA)
        qhl = sb("qhl", [128, 4 * NB], BF16, stA)
        awp = sb("awp", [128, ECH * E], BF16, stA)

        stQ = contextlib.ExitStack()
        aw3 = sb("aw3", [128, 3 * ECH * E], BF16, stQ)
        stW = contextlib.ExitStack()
        wstg = sb("wstg", [128, ECH * E], F32, stW)

        awq = {"q": aw3[:, 0:ECH * E], "k": aw3[:, ECH * E:2 * ECH * E],
               "v": aw3[:, 2 * ECH * E:], "p": awp[:]}

        xh_s, xl_s = xspl[:, :ECH * NB], xspl[:, ECH * NB:]

        SY.dma_start(out=ident[:], in_=ident_in).then_inc(s_in.h, 16)
        s_in.n += 16
        SY.dma_start(out=bias_sb[:], in_=bias_in).then_inc(s_in.h, 16)
        s_in.n += 16

        V.memset(ones_b[:], 1.0)
        V.memset(ones_r[:], 1.0)
        V.memset(eps_c[:], 1e-5)
        V.memset(bexp_c[:], EXP_SHIFT)
        V.memset(magic_c[:], MAGIC)
        V.memset(nmagic_c[:], -MAGIC)
        V.memset(colb[:], 0.0)
        V.memset(row[:], 0.0)
        V.memset(ccs[:], 0.0)
        V.memset(scl[:], 0.0)
        V.memset(c_(row, R127), 127.0)
        V.reciprocal(c_(row, RR127), c_(row, R127))
        V.memset(c_(row, R32K), 32767.0)
        V.reciprocal(c_(row, R32K), c_(row, R32K))
        V.memset(c_(colb, 5), 7.0)
        V.reciprocal(c_(colb, 5), c_(colb, 5))

        for j, w in enumerate(("q", "k", "v", "p")):
            SY.wait_ge(s_dv.h, s_dv.n)
            SY.dma_start(out=wstg[:].rearrange("p (c n) -> p c n", n=E),
                         in_=awT[w].transpose([1, 0, 2])).then_inc(s_in.h, 16)
            s_in.n += 16

            nc.vector.wait_ge(s_in.h, s_in.n)
            V.reduce_max(c_(colb, 6), wstg[:], AX, apply_absolute_value=True)
            nc.tensor.wait_ge(s_dv.h, s_dv.n)
            T.transpose(prow, colb[:, 6:7], ident[:]).then_inc(s_mm.h, 1)
            s_mm.n += 1

            nc.vector.wait_ge(s_mm.h, s_mm.n)
            V.reduce_max(c_(row, RT0), prow, AX)
            V.tensor_tensor(c_(row, RT0), c_(row, RT0), c_(row, RR127), OP.mult)
            V.reciprocal(c_(row, RT1), c_(row, RT0))
            nc.tensor.wait_ge(s_dv.h, s_dv.n)
            T.matmul(pbc2, ones_r[:], row[:1, RT0:RT0 + 2],
                     start=True, stop=True).then_inc(s_mm.h, 1)
            s_mm.n += 1

            nc.vector.wait_ge(s_mm.h, s_mm.n)
            V.tensor_copy(c_(scl, SW[w]), pbc2[:, 0:1])
            V.tensor_copy(c_(colb, 7), pbc2[:, 1:2])
            V.tensor_scalar(wstg[:], wstg[:], c_(colb, 7), MAGIC, OP.mult, OP.add)
            V.tensor_scalar(awq[w], wstg[:], MAGIC, None, OP.subtract)

        stW.close()
        SCLS = sb("SCLS", [128, 2304], F32, stQ)
        SCLR = sb("SCLR", [128, 2304], F32, stQ)
        prepw2 = sb("prepw2", [128, 2 * I3], F32, stQ)
        preps2 = sb("preps2", [128, 2 * I3], F32, stQ)
        prepr2 = sb("prepr2", [128, 2 * I3], F32, stQ)
        prepq2 = sb("prepq2", [128, I3], BF16, stQ)
        prepl2 = sb("prepl2", [128, I3], BF16, stQ)
        tbuf2 = sb("tbuf2", [96, 2 * 256], F32, stQ)

        prep_units = ([("scale", "g", c) for c in range(ICH)]
                      + [("scale", "u", c) for c in range(ICH)]
                      + [("scale", "d", c) for c in range(ECH)]
                      + [("tbatch", None, None)]
                      + [("deq", "g", c) for c in range(ECH)]
                      + [("deq", "u", c) for c in range(ECH)]
                      + [("deq", "d", c) for c in range(ICH)])
        scl_off = {"g": 0, "u": 768, "d": 1536}

        prep_slot = [0]
        prep_vmark = {0: 0, 1: 0}
        prep_pw = [Sem(ctx, nc, "s_pw0"), Sem(ctx, nc, "s_pw1")]
        prep_ps = [Sem(ctx, nc, "s_ps0"), Sem(ctx, nc, "s_ps1")]

        def emit_prep_load(kind, which, c):
            sl = prep_slot[0] & 1
            prep_slot[0] += 1
            prepw = prepw2[:, sl * I3:]
            preps = preps2[:, sl * I3:]
            prepr = prepr2[:, sl * I3:]
            spw = prep_pw[sl]
            sps = prep_ps[sl]
            if kind == "scale":
                n_in = E if which in ("g", "u") else I3
                if prep_vmark[sl] > 0:
                    SY.wait_ge(s_dv.h, prep_vmark[sl])
                SY.dma_start(out=prepw[:, :n_in], in_=w_nat[which][c]
                             ).then_inc(spw.h, 16)
                spw.n += 16
            else:
                n_in = I3 if which in ("g", "u") else E
                sd_, srd_ = sc_d[which]
                if prep_vmark[sl] > 0:
                    SY.wait_ge(s_dv.h, prep_vmark[sl])
                if spw.n > 0:
                    SY.wait_ge(spw.h, spw.n)
                SY.dma_start(out=prepw[:, :n_in], in_=wT_in[which][c]
                             ).then_inc(spw.h, 16)
                sin = sd_[c * 4:c * 4 + 4].unsqueeze(1).broadcast_to([4, 32, n_in])
                rin = srd_[c * 4:c * 4 + 4].unsqueeze(1).broadcast_to([4, 32, n_in])
                SY.dma_start(out=preps[:, :n_in], in_=sin).then_inc(spw.h, 16)
                SY.dma_start(out=prepr[:, :n_in], in_=rin).then_inc(spw.h, 16)
                spw.n += 48
            return sl

        def emit_prep_compute(kind, which, c, sl):
            prepw = prepw2[:, sl * I3:]
            preps = preps2[:, sl * I3:]
            prepr = prepr2[:, sl * I3:]
            prepq = prepq2[:, 0:]
            prepl = prepl2[:, 0:]
            spw = prep_pw[sl]
            sps = prep_ps[sl]
            if kind == "scale":
                n_in = E if which in ("g", "u") else I3
                nb_ = n_in // 32
                cols = scl_off[which] + c * nb_
                nc.vector.wait_ge(spw.h, spw.n)
                V.tensor_reduce(SCLS[:, cols:cols + nb_],
                                prepw[:, :n_in].rearrange("p (b c) -> p b c", c=32),
                                AX, OP.max, apply_absolute_value=True)
                V.tensor_scalar(SCLS[:, cols:cols + nb_], SCLS[:, cols:cols + nb_],
                                c_(colb, 5), 1e-12, OP.mult, OP.max)
                V.reciprocal(SCLR[:, cols:cols + nb_], SCLS[:, cols:cols + nb_])
                prep_vmark[sl] = s_dv.n
            else:
                n_in = I3 if which in ("g", "u") else E
                nc.vector.wait_ge(spw.h, spw.n)
                if sps.n > 0:
                    nc.vector.wait_ge(sps.h, sps.n)
                V.tensor_tensor(prepw[:, :n_in], prepw[:, :n_in],
                                prepr[:, :n_in], OP.mult)
                V.tensor_scalar(prepw[:, :n_in], prepw[:, :n_in], MAGIC, MAGIC,
                                OP.add, OP.subtract)
                V.tensor_tensor(prepw[:, :n_in], prepw[:, :n_in],
                                preps[:, :n_in], OP.mult)
                osps = prep_ps[1 - sl]
                if osps.n > 0:
                    nc.vector.wait_ge(osps.h, osps.n)
                V.tensor_copy(prepq[:, :n_in], prepw[:, :n_in])
                V.tensor_tensor(prepl[:, :n_in], prepw[:, :n_in],
                                prepq[:, :n_in], OP.subtract)
                prep_vmark[sl] = s_dv.n
                nc.scalar.wait_ge(s_dv.h, s_dv.n)
                for half in range(2):
                    src_t = (prepq if half == 0 else prepl)[:, :n_in]
                    C.dma_start(out=whl_d[(which, half)][c],
                                in_=src_t).then_inc(sps.h, 16)
                sps.n += 32

        def emit_tbatch():
                nc.tensor.wait_ge(s_dv.h, s_dv.n)
                jobs = []
                for wh in ("g", "u", "d"):
                    nchunks = ICH if wh in ("g", "u") else ECH
                    nb2 = 32 if wh in ("g", "u") else 96
                    for cc2 in range(nchunks):
                        for half2, (srcb, dstb) in enumerate(
                                ((SCLS, sc_d[wh][0]), (SCLR, sc_d[wh][1]))):
                            jobs.append((wh, cc2, nb2, srcb, dstb))
                tsems = [prep_ps[0], prep_ps[1], prep_pw[0], prep_pw[1]]
                vmarks = []
                smarks = []
                for ji, (wh, cc2, nb2, srcb, dstb) in enumerate(jobs):
                    cols = scl_off[wh] + cc2 * nb2
                    pslot = ji % 2
                    preg = ps_all[:nb2, (6 + pslot) * NB:(6 + pslot) * NB + 128]
                    if len(vmarks) >= 2:
                        nc.tensor.wait_ge(s_dv.h, vmarks[-2])
                    T.transpose(preg, srcb[:, cols:cols + nb2],
                                ident[:]).then_inc(s_mm.h, 1)
                    s_mm.n += 1
                    tslot = ji % 4
                    ts_ = tsems[tslot]
                    nc.vector.wait_ge(s_mm.h, s_mm.n)
                    if len(smarks) >= 4:
                        nc.vector.wait_ge(ts_.h, smarks[-4])
                    V.tensor_copy(tbuf2[:nb2, tslot * 128:(tslot + 1) * 128], preg)
                    vmarks.append(s_dv.n)
                    nc.scalar.wait_ge(s_dv.h, s_dv.n)
                    C.dma_start(out=dstb[:nb2, cc2 * 128:(cc2 + 1) * 128],
                                in_=tbuf2[:nb2, tslot * 128:(tslot + 1) * 128]
                                ).then_inc(ts_.h, 16)
                    ts_.n += 16
                    smarks.append(ts_.n)

        prep_iter = iter(prep_units)
        prep_pending = [None]

        def flush_prep():
            if prep_pending[0] is not None:
                emit_prep_compute(*prep_pending[0])
                prep_pending[0] = None

        def take_prep(kn):
            for _ in range(kn):
                u = next(prep_iter, None)
                if u is None:
                    continue
                if u[0] == "tbatch":
                    flush_prep()
                    emit_tbatch()
                else:
                    sl = emit_prep_load(*u)
                    flush_prep()
                    prep_pending[0] = (u[0], u[1], u[2], sl)

        qkv_evac_marks = []
        qkv_spill_marks = []

        for tb in range(TB):
            SY.wait_ge(s_dv.h, s_dv.n)
            SY.dma_start(out=xts[:].rearrange("p (c n) -> p c n", n=NB),
                         in_=xT_in[:, :, tb * NB:(tb + 1) * NB].transpose([1, 0, 2])
                         ).then_inc(s_x.h, 16)
            s_x.n += 16

            nc.vector.wait_ge(s_x.h, s_x.n)
            nc.vector.wait_ge(s_mm.h, s_mm.n)
            V.tensor_copy(xh_s, xts[:])
            V.tensor_tensor(xl_s, xts[:], xh_s, OP.subtract)
            sd_split = s_dv.n

            tiles = [("q", oc, None) for oc in range(ECH)] \
                + [("k", oc, None) for oc in range(ECH)] \
                + [("v", t_, h_) for t_ in range(NB // 128) for h_ in range(2)]
            for idx, (w, a, b_) in enumerate(tiles):
                pb = idx % 4
                nc.tensor.wait_ge(s_dv.h, sd_split)
                if len(qkv_evac_marks) >= 4:
                    nc.tensor.wait_ge(s_dv.h, qkv_evac_marks[-4])
                if w in ("q", "k"):
                    for ec in range(ECH):
                        lw = awq[w][:, ec * E + a * 128: ec * E + a * 128 + 128]
                        T.matmul(bank(pb), lw, xh_s[:, ec * NB:(ec + 1) * NB],
                                 start=(ec == 0), stop=False)
                        last = T.matmul(bank(pb), lw, xl_s[:, ec * NB:(ec + 1) * NB],
                                        start=False, stop=(ec == ECH - 1))
                else:
                    for ec in range(ECH):
                        lwh = xh_s[:, ec * NB + a * 128: ec * NB + a * 128 + 128]
                        lwl = xl_s[:, ec * NB + a * 128: ec * NB + a * 128 + 128]
                        rh = awq["v"][:, ec * E + b_ * NB: ec * E + (b_ + 1) * NB]
                        T.matmul(bank(pb), lwh, rh, start=(ec == 0), stop=False)
                        last = T.matmul(bank(pb), lwl, rh, start=False,
                                        stop=(ec == ECH - 1))
                last.then_inc(s_mm.h, 1)
                s_mm.n += 1

                nc.vector.wait_ge(s_mm.h, s_mm.n)
                sqs = s_sq[pb % 2]
                if sqs.n > 0:
                    nc.vector.wait_ge(sqs.h, sqs.n)
                st_ = qst[:, 0:NB]
                hi = qhl[:, (pb % 2) * 2 * NB:(pb % 2) * 2 * NB + NB]
                lo = qhl[:, (pb % 2) * 2 * NB + NB:(pb % 2) * 2 * NB + 2 * NB]
                V.tensor_scalar(st_, bank(pb), c_(scl, SW[w]), None, OP.mult)
                V.tensor_copy(hi, st_)
                V.tensor_tensor(lo, st_, hi, OP.subtract)
                qkv_evac_marks.append(s_dv.n)

                nc.scalar.wait_ge(s_dv.h, s_dv.n)
                if w == "q":
                    C.dma_start(out=qh_d[tb, a], in_=hi).then_inc(sqs.h, 16)
                    C.dma_start(out=ql_d[tb, a], in_=lo).then_inc(sqs.h, 16)
                elif w == "k":
                    KPT = NB // 128
                    C.dma_start(out=kh_d[tb * KPT:(tb + 1) * KPT, a]
                                .transpose([1, 0, 2]),
                                in_=hi.rearrange("p (t r) -> p t r", r=128)
                                ).then_inc(sqs.h, 16)
                    C.dma_start(out=kl_d[tb * KPT:(tb + 1) * KPT, a]
                                .transpose([1, 0, 2]),
                                in_=lo.rearrange("p (t r) -> p t r", r=128)
                                ).then_inc(sqs.h, 16)
                else:
                    kt_ = tb * (NB // 128) + a
                    C.dma_start(out=vh_d[kt_, :, b_ * NB:(b_ + 1) * NB],
                                in_=hi).then_inc(sqs.h, 16)
                    C.dma_start(out=vl_d[kt_, :, b_ * NB:(b_ + 1) * NB],
                                in_=lo).then_inc(sqs.h, 16)
                sqs.n += 32
                qkv_spill_marks.append((pb % 2, sqs.n))
                if idx % 2 == 1:
                    take_prep(1)

        take_prep(len(prep_units))
        flush_prep()
        stQ.close()

        stT = contextlib.ExitStack()
        SS = sb("SS", [128, 2 * KT * NB], BF16, stT)
        QT = sb("QT", [128, 2 * ECH * NB], BF16, stT)
        KV = sb("KV", [128, 2 * 2 * ECH * 128], BF16, stT)
        VV = sb("VV", [128, 2 * 2 * E], BF16, stT)
        S32 = sb("S32", [128, 2 * NB], F32, stT)
        SUMR = sb("SUMR", [128, NB], F32, stT)
        X1P = sb("X1P", [128, ECH * NB], F32, stT)
        XB = sb("XB", [128, 2 * ECH * NB], BF16, stT)
        MEAN = sb("MEAN", [128, NB], F32, stT)
        VARB = sb("VARB", [128, NB], F32, stT)
        RSTD = sb("RSTD", [128, NB], F32, stT)
        X1O = sb("X1O", [128, 2 * NB], F32, stT)

        ah_s, al_s = xh_s, xl_s

        def sh_t(kt):
            return SS[:, kt * NB:(kt + 1) * NB]

        def sl_t(kt):
            return SS[:, KT * NB + kt * NB:KT * NB + (kt + 1) * NB]

        SY.wait_ge(s_sq[0].h, s_sq[0].n)
        SY.wait_ge(s_sq[1].h, s_sq[1].n)
        SY.wait_ge(s_mm.h, s_mm.n)
        SY.wait_ge(s_dv.h, s_dv.n)
        SY.wait_ge(s_sc.h, s_sc.n)
        SY.wait_ge(s_gp.h, s_gp.n)
        SY.wait_ge(prep_ps[0].h, prep_ps[0].n)
        SY.wait_ge(prep_ps[1].h, prep_ps[1].n)
        SY.wait_ge(prep_pw[0].h, prep_pw[0].n)
        SY.wait_ge(prep_pw[1].h, prep_pw[1].n)
        prev_evac = 0
        for qb in range(TB):
            SY.wait_ge(s_mm.h, s_mm.n)
            SY.dma_start(out=QT[:, :ECH * NB].rearrange("p (c n) -> p c n", n=NB),
                         in_=qh_d[qb].transpose([1, 0, 2])).then_inc(s_q.h, 16)
            SY.dma_start(out=QT[:, ECH * NB:].rearrange("p (c n) -> p c n", n=NB),
                         in_=ql_d[qb].transpose([1, 0, 2])).then_inc(s_q.h, 16)
            s_q.n += 32
            q_ready = s_q.n

            base_kmm = s_mm.n
            k_marks = []
            exp_marks = []
            sl_marks = []
            for kt in range(KT):
                dbk = kt % 2
                if kt >= 2:
                    SY.wait_ge(s_mm.h, base_kmm + kt - 1)
                skp = s_kp[dbk]
                SY.dma_start(
                    out=KV[:, dbk * 2 * ECH * 128: dbk * 2 * ECH * 128 + ECH * 128]
                    .rearrange("p (c n) -> p c n", n=128),
                    in_=kh_d[kt].transpose([1, 0, 2])).then_inc(skp.h, 16)
                SY.dma_start(
                    out=KV[:, dbk * 2 * ECH * 128 + ECH * 128:
                           (dbk + 1) * 2 * ECH * 128]
                    .rearrange("p (c n) -> p c n", n=128),
                    in_=kl_d[kt].transpose([1, 0, 2])).then_inc(skp.h, 16)
                skp.n += 32
                k_marks.append((dbk, skp.n))

            for kt in range(KT):
                dbk = kt % 2
                sbk = 6 + (kt % 2)
                nc.tensor.wait_ge(s_q.h, q_ready)
                nc.tensor.wait_ge(s_kp[k_marks[kt][0]].h, k_marks[kt][1])
                if kt < 2 and prev_evac > 0:
                    nc.tensor.wait_ge(s_dv.h, prev_evac)
                if len(exp_marks) >= 2:
                    nc.tensor.wait_ge(s_sc.h, exp_marks[-2])
                kh0 = KV[:, dbk * 2 * ECH * 128:]
                for ec in range(ECH):
                    lwh = kh0[:, ec * 128:(ec + 1) * 128]
                    lwl = kh0[:, ECH * 128 + ec * 128: ECH * 128 + (ec + 1) * 128]
                    T.matmul(bank(sbk), lwh, QT[:, ec * NB:(ec + 1) * NB],
                             start=(ec == 0), stop=False)
                    T.matmul(bank(sbk), lwh,
                             QT[:, ECH * NB + ec * NB: ECH * NB + (ec + 1) * NB],
                             start=False, stop=False)
                    last = T.matmul(bank(sbk), lwl, QT[:, ec * NB:(ec + 1) * NB],
                                    start=False, stop=(ec == ECH - 1))
                last.then_inc(s_mm.h, 1)
                s_mm.n += 1

                if kt >= 1:
                    nc.tensor.wait_ge(s_dv.h, s_dv.n)
                    T.matmul(bank(5), ones_b[:], sh_t(kt - 1),
                             start=(kt == 1), stop=False)
                    T.matmul(bank(5), ones_b[:], sl_t(kt - 1),
                             start=False, stop=False)

                nc.scalar.wait_ge(s_mm.h, s_mm.n)
                if len(sl_marks) >= 2:
                    nc.scalar.wait_ge(s_dv.h, sl_marks[-2])
                C.activation(S32[:, dbk * NB:(dbk + 1) * NB], bank(sbk), ACT.Exp,
                             bias=bexp_c[:], scale=0.125)
                C.activation(sh_t(kt), bank(sbk), ACT.Exp,
                             bias=bexp_c[:], scale=0.125)
                exp_marks.append(s_sc.n)

                nc.vector.wait_ge(s_sc.h, s_sc.n)
                V.tensor_tensor(sl_t(kt), S32[:, dbk * NB:(dbk + 1) * NB],
                                sh_t(kt), OP.subtract)
                sl_marks.append(s_dv.n)

            nc.tensor.wait_ge(s_dv.h, s_dv.n)
            T.matmul(bank(5), ones_b[:], sh_t(KT - 1),
                     start=(KT == 1), stop=False)
            T.matmul(bank(5), ones_b[:], sl_t(KT - 1),
                     start=False, stop=True).then_inc(s_mm.h, 1)
            s_mm.n += 1

            nc.vector.wait_ge(s_mm.h, s_mm.n)
            V.reciprocal(SUMR[:], bank(5))
            sumr_done = s_dv.n

            base_bmm = s_mm.n
            for kt in range(KT):
                dbk = kt % 2
                if kt >= 2:
                    SY.wait_ge(s_mm.h, base_bmm + kt - 1)
                svp = s_vp[dbk]
                SY.dma_start(out=VV[:, dbk * 2 * E: dbk * 2 * E + E],
                             in_=vh_d[kt]).then_inc(svp.h, 16)
                SY.dma_start(out=VV[:, dbk * 2 * E + E: (dbk + 1) * 2 * E],
                             in_=vl_d[kt]).then_inc(svp.h, 16)
                svp.n += 32

                nc.tensor.wait_ge(svp.h, svp.n)
                if kt == 0:
                    nc.tensor.wait_ge(s_dv.h, sumr_done)
                    nc.tensor.wait_ge(s_sc.h, s_sc.n)
                vh0 = VV[:, dbk * 2 * E:]
                for ec in range(ECH):
                    lwh = vh0[:, ec * 128:(ec + 1) * 128]
                    lwl = vh0[:, E + ec * 128: E + (ec + 1) * 128]
                    T.matmul(bank(ec), lwh, sh_t(kt), start=(kt == 0), stop=False)
                    T.matmul(bank(ec), lwh, sl_t(kt), start=False, stop=False)
                    last = T.matmul(bank(ec), lwl, sh_t(kt), start=False,
                                    stop=(kt == KT - 1))
                last.then_inc(s_mm.h, 1)
                s_mm.n += 1

            nc.vector.wait_ge(s_mm.h, s_mm.n)
            for ec in range(ECH):
                st_ = qst[:, 0:NB]
                V.tensor_tensor(st_, bank(ec), SUMR[:], OP.mult)
                V.tensor_copy(ah_s[:, ec * NB:(ec + 1) * NB], st_)
                V.tensor_tensor(al_s[:, ec * NB:(ec + 1) * NB], st_,
                                ah_s[:, ec * NB:(ec + 1) * NB], OP.subtract)
            evac_done = s_dv.n
            prev_evac = evac_done

            SY.wait_ge(s_dv.h, evac_done)
            SY.dma_start(out=xts[:].rearrange("p (c n) -> p c n", n=NB),
                         in_=xT_in[:, :, qb * NB:(qb + 1) * NB].transpose([1, 0, 2])
                         ).then_inc(s_x.h, 16)
            s_x.n += 16
            xts_ready = s_x.n

            proj_evac_marks = []
            for oc in range(ECH):
                pb = oc % 2
                nc.tensor.wait_ge(s_dv.h, evac_done)
                if len(proj_evac_marks) >= 2:
                    nc.tensor.wait_ge(s_dv.h, proj_evac_marks[-2])
                for ec in range(ECH):
                    lw = awq["p"][:, ec * E + oc * 128: ec * E + oc * 128 + 128]
                    T.matmul(bank(pb), lw, ah_s[:, ec * NB:(ec + 1) * NB],
                             start=(ec == 0), stop=False)
                    last = T.matmul(bank(pb), lw, al_s[:, ec * NB:(ec + 1) * NB],
                                    start=False, stop=(ec == ECH - 1))
                last.then_inc(s_mm.h, 1)
                s_mm.n += 1

                nc.vector.wait_ge(s_mm.h, s_mm.n)
                if oc == 0:
                    nc.vector.wait_ge(s_x.h, xts_ready)
                xp = X1P[:, oc * NB:(oc + 1) * NB]
                V.tensor_scalar(xp, bank(pb), c_(scl, SW["p"]),
                                c_(bias_sb, oc), OP.mult, OP.add)
                V.tensor_tensor(xp, xp, xts[:, oc * NB:(oc + 1) * NB], OP.add)
                V.tensor_copy(XB[:, oc * NB:(oc + 1) * NB], xp)
                proj_evac_marks.append(s_dv.n)

                nc.scalar.wait_ge(s_dv.h, s_dv.n)
                C.activation(XB[:, ECH * NB + oc * NB: ECH * NB + (oc + 1) * NB],
                             XB[:, oc * NB:(oc + 1) * NB], ACT.Square)

                nc.tensor.wait_ge(s_sc.h, s_sc.n)
                T.matmul(bank(2), ones_b[:], XB[:, oc * NB:(oc + 1) * NB],
                         start=(oc == 0), stop=(oc == ECH - 1))
                mm2 = T.matmul(bank(3), ones_b[:],
                               XB[:, ECH * NB + oc * NB: ECH * NB + (oc + 1) * NB],
                               start=(oc == 0), stop=(oc == ECH - 1))
                if oc == ECH - 1:
                    mm2.then_inc(s_mm.h, 1)
                    s_mm.n += 1

            nc.vector.wait_ge(s_mm.h, s_mm.n)
            V.tensor_scalar(MEAN[:], bank(2), 1.0 / E, None, OP.mult)
            V.tensor_scalar(VARB[:], bank(3), 1.0 / E, None, OP.mult)
            V.tensor_tensor(RSTD[:], MEAN[:], MEAN[:], OP.mult)
            V.tensor_tensor(VARB[:], VARB[:], RSTD[:], OP.subtract)
            nc.scalar.wait_ge(s_dv.h, s_dv.n)
            C.activation(RSTD[:], VARB[:], ACT.Sqrt, bias=eps_c[:])
            nc.vector.wait_ge(s_sc.h, s_sc.n)
            V.reciprocal(RSTD[:], RSTD[:])

            for oc in range(ECH):
                sxs = s_sx[oc % 2]
                if sxs.n > 0:
                    nc.vector.wait_ge(sxs.h, sxs.n)
                xo = X1O[:, (oc % 2) * NB:((oc % 2) + 1) * NB]
                xp = X1P[:, oc * NB:(oc + 1) * NB]
                V.tensor_tensor(xo, xp, MEAN[:], OP.subtract)
                V.tensor_tensor(xo, xo, RSTD[:], OP.mult)
                V.reduce_max(c_(colb, 6), xo, AX, apply_absolute_value=True)
                V.tensor_tensor(c_(colb, 0), c_(colb, 0), c_(colb, 6), OP.max)

                nc.scalar.wait_ge(s_dv.h, s_dv.n)
                C.dma_start(out=x1_d[oc, :, qb * NB:(qb + 1) * NB],
                            in_=xo).then_inc(sxs.h, 16)
                sxs.n += 16

        stT.close()
        stA.close()

        def emit_collective(idx, cols, finish_fn):
            nc.tensor.wait_ge(s_dv.h, s_dv.n)
            T.transpose(prow, colb[:, cols[0]:cols[0] + 1], ident[:]
                        ).then_inc(s_mm.h, 1)
            s_mm.n += 1
            if len(cols) > 1:
                T.transpose(prow2, colb[:, cols[1]:cols[1] + 1], ident[:]
                            ).then_inc(s_mm.h, 1)
                s_mm.n += 1
            nc.vector.wait_ge(s_mm.h, s_mm.n)
            V.reduce_max(ccs[:, 1:2], prow, AX)
            if len(cols) > 1:
                V.reduce_max(ccs[:, 2:3], prow2, AX)
            nc.gpsimd.wait_ge(s_dv.h, s_dv.n)
            G.dma_start(out=cc_in[idx], in_=ccs[:]).then_inc(s_gd.h, 16)
            s_gd.n += 16
            nc.gpsimd.wait_ge(s_gd.h, s_gd.n)
            G.collective_compute(
                "AllReduce", OP.max, replica_groups=[list(range(8))],
                ins=[cc_in[idx].opt()], outs=[cc_out[idx].opt()]
            ).then_inc(s_cc.h, 1)
            s_cc.n += 1
            nc.gpsimd.wait_ge(s_cc.h, s_cc.n)
            G.dma_start(out=ccs[:], in_=cc_out[idx]).then_inc(s_gd.h, 16)
            s_gd.n += 16
            nc.vector.wait_ge(s_gd.h, s_gd.n)
            finish_fn()
            nc.tensor.wait_ge(s_dv.h, s_dv.n)
            T.matmul(pbc8, ones_r[:], row[:1, 16:24], start=True, stop=True
                     ).then_inc(s_mm.h, 1)
            s_mm.n += 1
            nc.vector.wait_ge(s_mm.h, s_mm.n)
            nc.vector.wait_ge(s_sc.h, s_sc.n)
            V.tensor_copy(scl[:, 0:8], pbc8)
            return s_dv.n

        FSTI = sb("FSTI", [128, 2 * SL], F32, ctx)
        FOUT = sb("FOUT", [128, 2 * SL], F32, ctx)
        WCH = sb("WCH", [128, 2 * 2 * ECH * 128], BF16, ctx)
        stN = contextlib.ExitStack()
        NL = sb("NL", [128, 2 * ECH * S], BF16, stN)
        nh_s = NL[:, :ECH * S]
        nl_s = NL[:, ECH * S:]

        def fin1():
            V.tensor_scalar(c_(row, RSX), ccs[:, 1:2], c_(row, R32K), 1e-12,
                            OP.mult, OP.max)
            V.reciprocal(row[:1, 16:17], c_(row, RSX))
        emit_collective(0, (0,), fin1)

        SY.wait_ge(s_sx[0].h, s_sx[0].n)
        SY.wait_ge(s_sx[1].h, s_sx[1].n)
        p3_marks = []
        for tb in range(TB):
            dbk = tb % 2
            if len(p3_marks) >= 2:
                SY.wait_ge(s_dv.h, p3_marks[-2][0])
                SY.wait_ge(s_gp.h, p3_marks[-2][1])
            sfp = s_fp[dbk]
            SY.dma_start(out=FSTI[:, dbk * SL: dbk * SL + ECH * NB]
                         .rearrange("p (c n) -> p c n", n=NB),
                         in_=x1_d[:, :, tb * NB:(tb + 1) * NB].transpose([1, 0, 2])
                         ).then_inc(sfp.h, 16)
            sfp.n += 16

            nc.vector.wait_ge(sfp.h, sfp.n)
            src = FSTI[:, dbk * SL: dbk * SL + ECH * NB]
            st_ = FOUT[:, dbk * SL: dbk * SL + ECH * NB]
            st3 = st_.rearrange("p (c n) -> p c n", n=NB)
            nh_v = nh_s.rearrange("p (c s) -> p c s", s=S)[:, :, tb * NB:(tb + 1) * NB]
            nl_v = nl_s.rearrange("p (c s) -> p c s", s=S)[:, :, tb * NB:(tb + 1) * NB]
            V.tensor_scalar(st_, src, c_(scl, ISXc), MAGIC, OP.mult, OP.add)
            V.tensor_scalar(st_, st_, MAGIC, None, OP.subtract)
            V.tensor_copy(nh_v, st3)
            V.tensor_tensor(nl_v, st3, nh_v, OP.subtract)
            p3_marks.append((s_dv.n, s_gp.n))
        p3_done = s_dv.n

        SY.wait_ge(prep_ps[0].h, prep_ps[0].n)
        SY.wait_ge(prep_ps[1].h, prep_ps[1].n)
        wch_last = {0: 0, 1: 0}
        ev_marks = []

        def emit_w_chunk_load(which, ich):
            dbw = ich % 2
            if wch_last[dbw] > 0:
                SY.wait_ge(s_mm.h, wch_last[dbw])
            skp = s_kp[dbw]
            for half in range(2):
                src = whl_d[(which, half)][:, :, ich * 128:(ich + 1) * 128]
                SY.dma_start(
                    out=WCH[:, dbw * 2 * ECH * 128 + half * ECH * 128:
                           dbw * 2 * ECH * 128 + (half + 1) * ECH * 128]
                    .rearrange("p (c n) -> p c n", n=128),
                    in_=src.transpose([1, 0, 2])).then_inc(skp.h, 16)
            skp.n += 32
            return dbw, skp

        def emit_wmm_tiles(ich, dbw):
            for tb in range(TB):
                pb = (ich * TB + tb) % 4
                nc.tensor.wait_ge(s_dv.h, p3_marks[tb][0])
                if len(ev_marks) >= 4:
                    nc.tensor.wait_ge(s_sc.h, ev_marks[-4])
                w0 = WCH[:, dbw * 2 * ECH * 128:]
                for ec in range(ECH):
                    lwh = w0[:, ec * 128:(ec + 1) * 128]
                    lwl = w0[:, ECH * 128 + ec * 128: ECH * 128 + (ec + 1) * 128]
                    nh_v = nh_s[:, ec * S + tb * NB: ec * S + (tb + 1) * NB]
                    nl_v = nl_s[:, ec * S + tb * NB: ec * S + (tb + 1) * NB]
                    T.matmul(bank(pb), lwh, nh_v, start=(ec == 0), stop=False)
                    T.matmul(bank(pb), lwh, nl_v, start=False, stop=False)
                    last = T.matmul(bank(pb), lwl, nh_v, start=False,
                                    stop=(ec == ECH - 1))
                last.then_inc(s_mm.h, 1)
                s_mm.n += 1
                wch_last[dbw] = s_mm.n
                yield tb, pb

        wload_pend = {0: emit_w_chunk_load("g", 0)}
        for ich in range(ICH):
            dbw, skp = wload_pend.pop(ich)
            if ich + 1 < ICH:
                wload_pend[ich + 1] = emit_w_chunk_load("g", ich + 1)
            nc.tensor.wait_ge(skp.h, skp.n)
            sos = s_so[ich % 2]
            for tb, pb in emit_wmm_tiles(ich, dbw):
                nc.scalar.wait_ge(s_mm.h, s_mm.n)
                if tb == 0 and sos.n > 0:
                    nc.scalar.wait_ge(sos.h, sos.n)
                off = (ich % 2) * SL
                C.activation(FOUT[:, off + tb * NB: off + (tb + 1) * NB],
                             bank(pb), ACT.Copy)
                ev_marks.append(s_sc.n)
                nc.vector.wait_ge(s_sc.h, s_sc.n)
                V.reduce_max(c_(colb, 6),
                             FOUT[:, off + tb * NB: off + (tb + 1) * NB],
                             AX, apply_absolute_value=True)
                V.tensor_tensor(c_(colb, 2), c_(colb, 2), c_(colb, 6), OP.max)
            nc.scalar.wait_ge(s_dv.h, s_dv.n)
            C.dma_start(out=gtr_d[ich],
                        in_=FOUT[:, (ich % 2) * SL:(ich % 2) * SL + TB * NB]
                        ).then_inc(sos.h, 16)
            sos.n += 16

        def fin2g():
            V.tensor_tensor(c_(row, RT0), ccs[:, 1:2], c_(row, RSX), OP.mult)
            V.tensor_scalar(c_(row, RSG), c_(row, RT0), c_(row, R32K), 1e-12,
                            OP.mult, OP.max)
            V.reciprocal(c_(row, RT0), c_(row, RSG))
            V.tensor_tensor(row[:1, 17:18], c_(row, RSX), c_(row, RT0), OP.mult)
            V.tensor_copy(row[:1, 18:19], c_(row, RSG))
        emit_collective(1, (2,), fin2g)

        SY.wait_ge(s_so[0].h, s_so[0].n)
        SY.wait_ge(s_so[1].h, s_so[1].n)
        f2_spill_mark = [0]
        f4_state = {"vmark": 0, "spill": 0}

        def f4_units():
            for ich in range(ICH):
                src = FSTI[:, 0:S]
                gq = FSTI[:, SL:SL + S] if ich % 2 == 0 else FOUT[:, SL:SL + S]
                if f4_state["vmark"] > 0:
                    SY.wait_ge(s_dv.h, f4_state["vmark"])
                if f4_state["spill"] > 0:
                    SY.wait_ge(s_tr.h, f4_state["spill"])
                SY.dma_start(out=src, in_=gtr_d[ich]).then_inc(s_gn[0].h, 16)
                s_gn[0].n += 16

                nc.vector.wait_ge(s_gn[0].h, s_gn[0].n)
                V.tensor_scalar(gq, src, c_(scl, KGc), MAGIC, OP.mult, OP.add)
                V.tensor_scalar(gq, gq, MAGIC, None, OP.subtract)
                rmark = s_dv.n
                yield
                nc.scalar.wait_ge(s_dv.h, rmark)
                C.activation(src, gq, ACT.Sigmoid, scale=c_(scl, SGc))
                yield
                C.activation(src, src, ACT.Identity, bias=magic_c[:],
                             scale=65536.0)
                yield
                C.activation(src, src, ACT.Identity, bias=nmagic_c[:])
                smark = s_sc.n
                yield
                nc.vector.wait_ge(s_sc.h, smark)
                V.tensor_scalar(src, src, 65535.0, None, OP.min)
                V.tensor_tensor(src, src, gq, OP.mult)
                V.reduce_max(c_(colb, 6), src, AX, apply_absolute_value=True)
                V.tensor_tensor(c_(colb, 3), c_(colb, 3), c_(colb, 6), OP.max)
                f4_state["vmark"] = s_dv.n
                yield
                nc.gpsimd.wait_ge(s_dv.h, f4_state["vmark"])
                G.dma_start(out=tr_d[ich], in_=src).then_inc(s_tr.h, 16)
                s_tr.n += 16
                f4_state["spill"] = s_tr.n
                yield
        f4_iter = f4_units()

        wload_pend = {0: emit_w_chunk_load("u", 0)}
        for ich in range(ICH):
            dbw, skp = wload_pend.pop(ich)
            if ich + 1 < ICH:
                wload_pend[ich + 1] = emit_w_chunk_load("u", ich + 1)
            nc.tensor.wait_ge(skp.h, skp.n)
            for tb, pb in emit_wmm_tiles(ich, dbw):
                nc.scalar.wait_ge(s_mm.h, s_mm.n)
                if tb == 0 and f2_spill_mark[0] > 0:
                    nc.scalar.wait_ge(s_so[0].h, f2_spill_mark[0])
                C.activation(FOUT[:, tb * NB:(tb + 1) * NB], bank(pb), ACT.Copy)
                ev_marks.append(s_sc.n)
                nc.vector.wait_ge(s_sc.h, s_sc.n)
                V.reduce_max(c_(colb, 6), FOUT[:, tb * NB:(tb + 1) * NB],
                             AX, apply_absolute_value=True)
                V.tensor_tensor(c_(colb, 1), c_(colb, 1), c_(colb, 6), OP.max)
                if tb != 1 and tb != 2:
                    next(f4_iter, None)
            nc.scalar.wait_ge(s_dv.h, s_dv.n)
            C.dma_start(out=upr_d[ich], in_=FOUT[:, 0:TB * NB]
                        ).then_inc(s_so[0].h, 16)
            s_so[0].n += 16
            f2_spill_mark[0] = s_so[0].n
        for _ in f4_iter:
            pass

        def fin23():
            V.tensor_tensor(c_(row, RT0), ccs[:, 1:2], c_(row, RSX), OP.mult)
            V.tensor_scalar(c_(row, RSU), c_(row, RT0), c_(row, R32K), 1e-12,
                            OP.mult, OP.max)
            V.reciprocal(c_(row, RT0), c_(row, RSU))
            V.tensor_tensor(row[:1, 19:20], c_(row, RSX), c_(row, RT0), OP.mult)
            V.tensor_scalar(c_(row, RT0), ccs[:, 2:3], 1.0 / 65536.0, None,
                            OP.mult)
            V.tensor_tensor(c_(row, RT0), c_(row, RT0), c_(row, RSG), OP.mult)
            V.tensor_scalar(c_(row, RT0), c_(row, RT0), c_(row, R32K), None,
                            OP.mult)
            V.tensor_scalar(c_(row, RST), c_(row, RT0), 1e-12, None, OP.max)
            V.reciprocal(c_(row, RT0), c_(row, RST))
            V.tensor_scalar(c_(row, RT1), c_(row, RSG), 1.0 / 65536.0, None,
                            OP.mult)
            V.tensor_tensor(row[:1, 20:21], c_(row, RT1), c_(row, RT0), OP.mult)
        bc23 = emit_collective(2, (1, 3), fin23)

        stN.close()

        stF4 = contextlib.ExitStack()
        GN = sb("GN", [128, 2 * SL], F32, stF4)

        SY.wait_ge(s_so[0].h, s_so[0].n)
        SY.wait_ge(s_tr.h, s_tr.n)
        f5_dv = []
        f5_sc = []
        for ich in range(ICH):
            dbk = ich % 2
            if len(f5_dv) >= 2:
                SY.wait_ge(s_dv.h, f5_dv[-2])
            sfp = s_fp[dbk]
            sgn = s_gn[dbk]
            SY.dma_start(out=FSTI[:, dbk * SL: dbk * SL + S],
                         in_=upr_d[ich]).then_inc(sfp.h, 16)
            SY.dma_start(out=GN[:, dbk * SL: dbk * SL + S],
                         in_=tr_d[ich]).then_inc(sgn.h, 16)
            sfp.n += 16
            sgn.n += 16

            up = FSTI[:, dbk * SL: dbk * SL + S]
            tr = GN[:, dbk * SL: dbk * SL + S]
            ot = FOUT[:, dbk * SL: dbk * SL + S]
            nc.scalar.wait_ge(sfp.h, sfp.n)
            if ich == 0:
                nc.scalar.wait_ge(s_dv.h, bc23)
            C.activation(up, up, ACT.Identity, bias=magic_c[:], scale=c_(scl, KUc))
            C.activation(up, up, ACT.Identity, bias=nmagic_c[:])
            f5_sc.append(s_sc.n)

            nc.vector.wait_ge(sgn.h, sgn.n)
            sos = s_or[dbk]
            if ich == 0:
                nc.vector.wait_ge(s_so[0].h, s_so[0].n)
            if sos.n > 0:
                nc.vector.wait_ge(sos.h, sos.n)
            V.tensor_scalar(tr, tr, c_(scl, KTc), MAGIC, OP.mult, OP.add)
            V.tensor_scalar(tr, tr, MAGIC, None, OP.subtract)
            nc.vector.wait_ge(s_sc.h, f5_sc[-1])
            V.tensor_tensor(ot, tr, up, OP.mult)
            V.reduce_max(c_(colb, 6), ot, AX, apply_absolute_value=True)
            V.tensor_tensor(c_(colb, 4), c_(colb, 4), c_(colb, 6), OP.max)
            f5_dv.append(s_dv.n)

            nc.gpsimd.wait_ge(s_dv.h, s_dv.n)
            G.dma_start(out=or_d[ich], in_=FOUT[:, dbk * SL: dbk * SL + TB * NB]
                        ).then_inc(sos.h, 16)
            sos.n += 16

        stF4.close()


        def fin4():
            V.tensor_tensor(c_(row, RT0), c_(row, RST), c_(row, RSU), OP.mult)
            V.tensor_tensor(c_(row, RT1), ccs[:, 1:2], c_(row, RT0), OP.mult)
            V.tensor_scalar(c_(row, RSO), c_(row, RT1), c_(row, R32K), 1e-12,
                            OP.mult, OP.max)
            V.reciprocal(c_(row, RT1), c_(row, RSO))
            V.tensor_tensor(row[:1, 21:22], c_(row, RT0), c_(row, RT1), OP.mult)
            V.tensor_copy(row[:1, 22:23], c_(row, RSO))
        emit_collective(3, (4,), fin4)

        stF2 = contextlib.ExitStack()
        WD = sb("WD", [128, 2 * ICH * E], BF16, stF2)
        OTB = sb("OTB", [128, 2 * 2 * NB], BF16, stF2)
        XSB = sb("XSB", [128, 4 * NB], BF16, stF2)

        SY.wait_ge(s_or[0].h, s_or[0].n)
        SY.wait_ge(s_or[1].h, s_or[1].n)
        SY.dma_start(out=WD[:, :ICH * E].rearrange("p (c n) -> p c n", n=E),
                     in_=whl_d[("d", 0)].transpose([1, 0, 2])).then_inc(s_kp[0].h, 16)
        SY.dma_start(out=WD[:, ICH * E:].rearrange("p (c n) -> p c n", n=E),
                     in_=whl_d[("d", 1)].transpose([1, 0, 2])).then_inc(s_kp[0].h, 16)
        s_kp[0].n += 32
        nc.tensor.wait_ge(s_kp[0].h, s_kp[0].n)

        for tb in range(TB):
            SY.wait_ge(s_dv.h, s_dv.n)
            SY.wait_ge(s_out.h, s_out.n)
            SY.dma_start(out=FSTI[:, SL:SL + ECH * NB]
                         .rearrange("p (c n) -> p c n", n=NB),
                         in_=x1_d[:, :, tb * NB:(tb + 1) * NB].transpose([1, 0, 2])
                         ).then_inc(s_x1b.h, 16)
            s_x1b.n += 16

            ot_marks = []
            f6_dv = []
            for ich in range(ICH):
                dbk = ich % 2
                if len(f6_dv) >= 2:
                    SY.wait_ge(s_dv.h, f6_dv[-2])
                sfp = s_fp[dbk]
                SY.dma_start(out=FSTI[:, dbk * NB:(dbk + 1) * NB],
                             in_=or_d[ich, :, tb * NB:(tb + 1) * NB]
                             ).then_inc(sfp.h, 16)
                sfp.n += 16

                nc.vector.wait_ge(sfp.h, sfp.n)
                if len(ot_marks) >= 2:
                    nc.vector.wait_ge(s_mm.h, ot_marks[-2])
                src = FSTI[:, dbk * NB:(dbk + 1) * NB]
                oh = OTB[:, dbk * 2 * NB: dbk * 2 * NB + NB]
                ol = OTB[:, dbk * 2 * NB + NB: (dbk + 1) * 2 * NB]
                V.tensor_scalar(src, src, c_(scl, KOc), MAGIC, OP.mult, OP.add)
                V.tensor_scalar(src, src, MAGIC, None, OP.subtract)
                V.tensor_copy(oh, src)
                V.tensor_tensor(ol, src, oh, OP.subtract)
                f6_dv.append(s_dv.n)

                nc.tensor.wait_ge(s_dv.h, s_dv.n)
                for hc in range(ECH):
                    lwh = WD[:, ich * E + hc * 128: ich * E + hc * 128 + 128]
                    lwl = WD[:, ICH * E + ich * E + hc * 128:
                             ICH * E + ich * E + hc * 128 + 128]
                    T.matmul(bank(hc), lwh, oh, start=(ich == 0), stop=False)
                    T.matmul(bank(hc), lwh, ol, start=False, stop=False)
                    last = T.matmul(bank(hc), lwl, oh, start=False,
                                    stop=(ich == ICH - 1))
                last.then_inc(s_mm.h, 1)
                s_mm.n += 1
                ot_marks.append(s_mm.n)

            nc.vector.wait_ge(s_mm.h, s_mm.n)
            nc.vector.wait_ge(s_x1b.h, s_x1b.n)
            for hc in range(ECH):
                xp = FOUT[:, hc * NB:(hc + 1) * NB]
                V.tensor_scalar(xp, bank(hc), c_(scl, SOc), None, OP.mult)
                V.tensor_tensor(xp, xp, FSTI[:, SL + hc * NB: SL + (hc + 1) * NB],
                                OP.add)
            f6_evac_done = s_dv.n

            xstat_marks = []
            for hc in range(ECH):
                nc.scalar.wait_ge(s_dv.h, f6_evac_done)
                if len(xstat_marks) >= 2:
                    nc.scalar.wait_ge(s_mm.h, xstat_marks[-2])
                xp = FOUT[:, hc * NB:(hc + 1) * NB]
                xb = XSB[:, (hc % 2) * 2 * NB:(hc % 2) * 2 * NB + NB]
                xq = XSB[:, (hc % 2) * 2 * NB + NB:((hc % 2) + 1) * 2 * NB]
                C.activation(xb, xp, ACT.Copy)
                C.activation(xq, xp, ACT.Square)

                nc.tensor.wait_ge(s_sc.h, s_sc.n)
                nc.tensor.wait_ge(s_dv.h, f6_evac_done)
                T.matmul(bank(0), ones_b[:], xb,
                         start=(hc == 0), stop=(hc == ECH - 1))
                T.matmul(bank(1), ones_b[:], xq,
                         start=(hc == 0), stop=(hc == ECH - 1)
                         ).then_inc(s_mm.h, 1)
                s_mm.n += 1
                xstat_marks.append(s_mm.n)

            nc.vector.wait_ge(s_mm.h, s_mm.n)
            mn = FSTI[:, 2 * NB:3 * NB]
            vb = FSTI[:, 3 * NB:4 * NB]
            rs = FSTI[:, 4 * NB:5 * NB]
            V.tensor_scalar(mn, bank(0), 1.0 / E, None, OP.mult)
            V.tensor_scalar(vb, bank(1), 1.0 / E, None, OP.mult)
            V.tensor_tensor(rs, mn, mn, OP.mult)
            V.tensor_tensor(vb, vb, rs, OP.subtract)
            nc.scalar.wait_ge(s_dv.h, s_dv.n)
            C.activation(rs, vb, ACT.Sqrt, bias=eps_c[:])
            nc.vector.wait_ge(s_sc.h, s_sc.n)
            V.reciprocal(rs, rs)

            for hc in range(ECH):
                xp = FOUT[:, hc * NB:(hc + 1) * NB]
                V.tensor_tensor(xp, xp, mn, OP.subtract)
                V.tensor_tensor(xp, xp, rs, OP.mult)

                nc.scalar.wait_ge(s_dv.h, s_dv.n)
                C.dma_start(out=out_d[hc, :, tb * NB:(tb + 1) * NB],
                            in_=xp).then_inc(s_out.h, 16)
                s_out.n += 16

        nc.gpsimd.wait_ge(s_out.h, s_out.n)
        stF2.close()
    return nc


_CACHE = {}


def _get_program(S):
    if S not in _CACHE:
        _CACHE[S] = build_program(S)
    return _CACHE[S]


def stage_inputs(L0_attn_x, L0_attn_w_query, L0_attn_w_key, L0_attn_w_value,
                 L0_attn_w_proj, L0_attn_b_proj, L0_ffn_w_gate, L0_ffn_w_up,
                 L0_ffn_w_down):
    x = np.ascontiguousarray(np.asarray(L0_attn_x, dtype=np.float32))
    B, S, _E = x.shape

    def t8(w):
        return np.ascontiguousarray(np.asarray(w, np.float32).T.reshape(ECH, 128, E))

    shared = {
        "awT_q": t8(L0_attn_w_query), "awT_k": t8(L0_attn_w_key),
        "awT_v": t8(L0_attn_w_value), "awT_p": t8(L0_attn_w_proj),
        "bproj": np.ascontiguousarray(np.asarray(L0_attn_b_proj, np.float32)
                                      .reshape(ECH, 128).T),
        "wgT": np.ascontiguousarray(np.asarray(L0_ffn_w_gate, np.float32).T
                                    .reshape(ECH, 128, I3)),
        "wuT": np.ascontiguousarray(np.asarray(L0_ffn_w_up, np.float32).T
                                    .reshape(ECH, 128, I3)),
        "wdT": np.ascontiguousarray(np.asarray(L0_ffn_w_down, np.float32).T
                                    .reshape(ICH, 128, E)),
        "wg_nat": np.ascontiguousarray(np.asarray(L0_ffn_w_gate, np.float32)
                                       .reshape(ICH, 128, E)),
        "wu_nat": np.ascontiguousarray(np.asarray(L0_ffn_w_up, np.float32)
                                       .reshape(ICH, 128, E)),
        "wd_nat": np.ascontiguousarray(np.asarray(L0_ffn_w_down, np.float32)
                                       .reshape(ECH, 128, I3)),
        "ident32": np.eye(128, dtype=np.float32),
    }
    in_maps = []
    for c in range(B):
        m = dict(shared)
        m["xT"] = np.ascontiguousarray(x[c].T.reshape(ECH, 128, S))
        in_maps.append(m)
    return in_maps


def kernel(**inputs):
    in_maps = stage_inputs(**inputs)
    B = len(in_maps)
    S = in_maps[0]["xT"].shape[2]
    nc = _get_program(S)
    res = run_bass_kernel_spmd(nc, in_maps, core_ids=list(range(B)))
    outs = []
    for c in range(B):
        y = np.asarray(res.results[c]["out"]).reshape(E, S).T
        outs.append(y)
    return np.ascontiguousarray(np.stack(outs)).astype(np.float32)


# version 10
# speedup vs baseline: 1.0499x; 1.0049x over previous
import contextlib
import numpy as np

import concourse.bass as bass
import concourse.mybir as mybir
from concourse.bass_utils import run_bass_kernel_spmd

F32 = mybir.dt.float32
BF16 = mybir.dt.bfloat16
OP = mybir.AluOpType
ACT = mybir.ActivationFunctionType
AX = mybir.AxisListType.X

MAGIC = 12582912.0
EXP_SHIFT = -24.0
E = 1024
ECH = 8
I3 = 3072
ICH = 24
NB = 512


class Sem:
    def __init__(self, ctx, nc, name):
        self.h = ctx.enter_context(nc.semaphore(name))
        self.n = 0
        self.name = name


_CHAIN_OPS = {
    "tensor_copy", "tensor_tensor", "tensor_scalar", "tensor_single_scalar",
    "tensor_add", "tensor_sub", "tensor_mul", "tensor_max", "tensor_reduce",
    "reduce_max", "reduce_sum", "reciprocal", "memset", "activation", "iota",
    "scalar_tensor_tensor", "select", "tensor_scalar_add", "tensor_scalar_sub",
    "tensor_scalar_mul", "tensor_scalar_max", "tensor_scalar_min",
}


class Eng:

    def __init__(self, eng, sem):
        self._e = eng
        self._s = sem

    def __getattr__(self, name):
        fn = getattr(self._e, name)
        if name in _CHAIN_OPS:
            def wrap(*a, **k):
                if self._s.n > 0:
                    self._e.wait_ge(self._s.h, self._s.n)
                ins = fn(*a, **k)
                ins.then_inc(self._s.h, 1)
                self._s.n += 1
                return ins
            return wrap
        if name in ("dma_start", "collective_compute", "dma_start_transpose"):
            def wrapd(*a, **k):
                if self._s.n > 0:
                    self._e.wait_ge(self._s.h, self._s.n)
                return fn(*a, **k)
            return wrapd
        return fn


def build_program(S=4096):
    TB = S // NB
    KT = S // 128
    SL = max(ECH * NB, S)

    nc = bass.Bass("TRN2", target_bir_lowering=False)

    xT_in = nc.dram_tensor("xT", [ECH, 128, S], F32, kind="ExternalInput").ap()
    awT = {w: nc.dram_tensor(f"awT_{w}", [ECH, 128, E], F32, kind="ExternalInput").ap()
           for w in ("q", "k", "v", "p")}
    bias_in = nc.dram_tensor("bproj", [128, ECH], F32, kind="ExternalInput").ap()
    wT_in = {"g": nc.dram_tensor("wgT", [ECH, 128, I3], F32, kind="ExternalInput").ap(),
             "u": nc.dram_tensor("wuT", [ECH, 128, I3], F32, kind="ExternalInput").ap(),
             "d": nc.dram_tensor("wdT", [ICH, 128, E], F32, kind="ExternalInput").ap()}
    w_nat = {"g": nc.dram_tensor("wg_nat", [ICH, 128, E], F32, kind="ExternalInput").ap(),
             "u": nc.dram_tensor("wu_nat", [ICH, 128, E], F32, kind="ExternalInput").ap(),
             "d": nc.dram_tensor("wd_nat", [ECH, 128, I3], F32, kind="ExternalInput").ap()}
    ident_in = nc.dram_tensor("ident32", [128, 128], F32, kind="ExternalInput").ap()

    out_d = nc.dram_tensor("out", [ECH, 128, S], F32, kind="ExternalOutput").ap()

    qh_d = nc.dram_tensor("qh_d", [TB, ECH, 128, NB], BF16).ap()
    ql_d = nc.dram_tensor("ql_d", [TB, ECH, 128, NB], BF16).ap()
    kh_d = nc.dram_tensor("kh_d", [KT, ECH, 128, 128], BF16).ap()
    kl_d = nc.dram_tensor("kl_d", [KT, ECH, 128, 128], BF16).ap()
    vh_d = nc.dram_tensor("vh_d", [KT, 128, E], BF16).ap()
    vl_d = nc.dram_tensor("vl_d", [KT, 128, E], BF16).ap()
    x1_d = nc.dram_tensor("x1_d", [ECH, 128, S], F32).ap()
    whl_d = {("g", 0): nc.dram_tensor("wgh_d", [ECH, 128, I3], BF16).ap(),
             ("g", 1): nc.dram_tensor("wgl_d", [ECH, 128, I3], BF16).ap(),
             ("u", 0): nc.dram_tensor("wuh_d", [ECH, 128, I3], BF16).ap(),
             ("u", 1): nc.dram_tensor("wul_d", [ECH, 128, I3], BF16).ap(),
             ("d", 0): nc.dram_tensor("wdh_d", [ICH, 128, E], BF16).ap(),
             ("d", 1): nc.dram_tensor("wdl_d", [ICH, 128, E], BF16).ap()}
    sc_d = {"g": (nc.dram_tensor("sg_d", [32, I3], F32).ap(),
                  nc.dram_tensor("sgr_d", [32, I3], F32).ap()),
            "u": (nc.dram_tensor("su_d", [32, I3], F32).ap(),
                  nc.dram_tensor("sur_d", [32, I3], F32).ap()),
            "d": (nc.dram_tensor("sd_d", [96, E], F32).ap(),
                  nc.dram_tensor("sdr_d", [96, E], F32).ap())}
    upr_d = nc.dram_tensor("upr_d", [ICH, 128, S], F32).ap()
    gtr_d = nc.dram_tensor("gtr_d", [ICH, 128, S], F32).ap()
    tr_d = nc.dram_tensor("tr_d", [ICH, 128, S], F32).ap()
    or_d = nc.dram_tensor("or_d", [ICH, 128, S], F32).ap()
    cc_in = [nc.dram_tensor(f"cc_in{i}", [1, 16], F32).ap() for i in range(4)]
    cc_out = [nc.dram_tensor(f"cc_out{i}", [1, 16], F32).ap() for i in range(4)]

    ctx = contextlib.ExitStack()
    with ctx:
        def sb(name, shape, dtype, st=None):
            return (st or ctx).enter_context(nc.sbuf_tensor(name, shape, dtype))

        ones_b = sb("ones_b", [128, 128], BF16)
        ones_r = sb("ones_r", [1, 128], F32)
        ident = sb("ident", [128, 128], F32)
        scl = sb("scl", [128, 16], F32)
        row = sb("row", [1, 32], F32)
        colb = sb("colb", [128, 8], F32)
        eps_c = sb("eps_c", [128, 1], F32)
        bexp_c = sb("bexp_c", [128, 1], F32)
        magic_c = sb("magic_c", [128, 1], F32)
        nmagic_c = sb("nmagic_c", [128, 1], F32)
        bias_sb = sb("bias_sb", [128, ECH], F32)
        ccs = sb("ccs", [1, 16], F32)

        R127, RR127, R32K, RSX, RSU, RSG, RST, RSO, RT0, RT1 = range(10)
        ISXc, KGc, SGc, KUc, KTc, KOc, SOc = range(7)
        SW = {"q": 8, "k": 9, "v": 10, "p": 11}

        def c_(t, a):
            return t[:, a:a + 1]

        ps_all = ctx.enter_context(nc.psum_tensor("psall", [128, 8 * NB], F32))

        def bank(i):
            return ps_all[:, i * NB:(i + 1) * NB]

        prow = ps_all[:1, 7 * NB:7 * NB + 128]
        prow2 = ps_all[:1, 6 * NB:6 * NB + 128]
        pbc8 = ps_all[:, 7 * NB:7 * NB + 8]
        pbc2 = ps_all[:, 7 * NB:7 * NB + 2]

        s_in = Sem(ctx, nc, "s_in")
        s_x = Sem(ctx, nc, "s_x")
        s_pw = Sem(ctx, nc, "s_pw")
        s_kp = [Sem(ctx, nc, f"s_k{i}") for i in range(2)]
        s_vp = [Sem(ctx, nc, f"s_v{i}") for i in range(2)]
        s_fp = [Sem(ctx, nc, f"s_f{i}") for i in range(2)]
        s_gn = [Sem(ctx, nc, f"s_g{i}") for i in range(2)]
        s_x1b = Sem(ctx, nc, "s_x1b")
        s_q = Sem(ctx, nc, "s_q")
        s_ps = Sem(ctx, nc, "s_ps")
        s_sq = [Sem(ctx, nc, f"s_sq{i}") for i in range(2)]
        s_sx = [Sem(ctx, nc, f"s_sx{i}") for i in range(2)]
        s_so = [Sem(ctx, nc, f"s_so{i}") for i in range(2)]
        s_or = [Sem(ctx, nc, f"s_or{i}") for i in range(2)]
        s_tr = Sem(ctx, nc, "s_tr")
        s_pg = [Sem(ctx, nc, f"s_pg{i}") for i in range(2)]
        s_pc = [Sem(ctx, nc, f"s_pc{i}") for i in range(2)]
        s_gd = Sem(ctx, nc, "s_gd")
        s_mm = Sem(ctx, nc, "s_mm")
        s_dv = Sem(ctx, nc, "s_dv")
        s_sc = Sem(ctx, nc, "s_sc")
        s_cc = Sem(ctx, nc, "s_cc")
        s_gp = Sem(ctx, nc, "s_gp")
        s_out = Sem(ctx, nc, "s_out")

        block = ctx.enter_context(nc.Block())
        V = Eng(nc.vector, s_dv)
        C = Eng(nc.scalar, s_sc)
        G = Eng(nc.gpsimd, s_gp)
        T = nc.tensor
        SY = nc.sync

        stA = contextlib.ExitStack()
        xts = sb("xts", [128, ECH * NB], F32, stA)
        xspl = sb("xspl", [128, 2 * ECH * NB], BF16, stA)
        qst = sb("qst", [128, NB], F32, stA)
        qhl = sb("qhl", [128, 4 * NB], BF16, stA)
        awp = sb("awp", [128, ECH * E], BF16, stA)

        stQ = contextlib.ExitStack()
        aw3 = sb("aw3", [128, 3 * ECH * E], BF16, stQ)
        stW = contextlib.ExitStack()
        wstg = sb("wstg", [128, ECH * E], F32, stW)

        awq = {"q": aw3[:, 0:ECH * E], "k": aw3[:, ECH * E:2 * ECH * E],
               "v": aw3[:, 2 * ECH * E:], "p": awp[:]}

        xh_s, xl_s = xspl[:, :ECH * NB], xspl[:, ECH * NB:]

        SY.dma_start(out=ident[:], in_=ident_in).then_inc(s_in.h, 16)
        s_in.n += 16
        SY.dma_start(out=bias_sb[:], in_=bias_in).then_inc(s_in.h, 16)
        s_in.n += 16

        V.memset(ones_b[:], 1.0)
        V.memset(ones_r[:], 1.0)
        V.memset(eps_c[:], 1e-5)
        V.memset(bexp_c[:], EXP_SHIFT)
        V.memset(magic_c[:], MAGIC)
        V.memset(nmagic_c[:], -MAGIC)
        V.memset(colb[:], 0.0)
        V.memset(row[:], 0.0)
        V.memset(ccs[:], 0.0)
        V.memset(scl[:], 0.0)
        V.memset(c_(row, R127), 127.0)
        V.reciprocal(c_(row, RR127), c_(row, R127))
        V.memset(c_(row, R32K), 32767.0)
        V.reciprocal(c_(row, R32K), c_(row, R32K))
        V.memset(c_(colb, 5), 7.0)
        V.reciprocal(c_(colb, 5), c_(colb, 5))

        for j, w in enumerate(("q", "k", "v", "p")):
            SY.wait_ge(s_dv.h, s_dv.n)
            SY.dma_start(out=wstg[:].rearrange("p (c n) -> p c n", n=E),
                         in_=awT[w].transpose([1, 0, 2])).then_inc(s_in.h, 16)
            s_in.n += 16

            nc.vector.wait_ge(s_in.h, s_in.n)
            V.reduce_max(c_(colb, 6), wstg[:], AX, apply_absolute_value=True)
            nc.tensor.wait_ge(s_dv.h, s_dv.n)
            T.transpose(prow, colb[:, 6:7], ident[:]).then_inc(s_mm.h, 1)
            s_mm.n += 1

            nc.vector.wait_ge(s_mm.h, s_mm.n)
            V.reduce_max(c_(row, RT0), prow, AX)
            V.tensor_tensor(c_(row, RT0), c_(row, RT0), c_(row, RR127), OP.mult)
            V.reciprocal(c_(row, RT1), c_(row, RT0))
            nc.tensor.wait_ge(s_dv.h, s_dv.n)
            T.matmul(pbc2, ones_r[:], row[:1, RT0:RT0 + 2],
                     start=True, stop=True).then_inc(s_mm.h, 1)
            s_mm.n += 1

            nc.vector.wait_ge(s_mm.h, s_mm.n)
            V.tensor_copy(c_(scl, SW[w]), pbc2[:, 0:1])
            V.tensor_copy(c_(colb, 7), pbc2[:, 1:2])
            V.tensor_scalar(wstg[:], wstg[:], c_(colb, 7), MAGIC, OP.mult, OP.add)
            V.tensor_scalar(awq[w], wstg[:], MAGIC, None, OP.subtract)

        stW.close()
        SCLS = sb("SCLS", [128, 2304], F32, stQ)
        SCLR = sb("SCLR", [128, 2304], F32, stQ)
        prepw2 = sb("prepw2", [128, 2 * I3], F32, stQ)
        preps2 = sb("preps2", [128, 2 * I3], F32, stQ)
        prepr2 = sb("prepr2", [128, 2 * I3], F32, stQ)
        prepq2 = sb("prepq2", [128, I3], BF16, stQ)
        prepl2 = sb("prepl2", [128, I3], BF16, stQ)
        tbuf2 = sb("tbuf2", [96, 2 * 256], F32, stQ)

        prep_units = ([("scale", "g", c) for c in range(ICH)]
                      + [("scale", "u", c) for c in range(ICH)]
                      + [("scale", "d", c) for c in range(ECH)]
                      + [("tbatch", None, None)]
                      + [("deq", "g", c) for c in range(ECH)]
                      + [("deq", "u", c) for c in range(ECH)]
                      + [("deq", "d", c) for c in range(ICH)])
        scl_off = {"g": 0, "u": 768, "d": 1536}

        prep_slot = [0]
        prep_gwait = [False]
        prep_vmark = {0: 0, 1: 0}
        prep_pw = [Sem(ctx, nc, "s_pw0"), Sem(ctx, nc, "s_pw1")]
        prep_ps = [Sem(ctx, nc, "s_ps0"), Sem(ctx, nc, "s_ps1")]

        def emit_prep_load(kind, which, c):
            sl = prep_slot[0] & 1
            prep_slot[0] += 1
            prepw = prepw2[:, sl * I3:]
            preps = preps2[:, sl * I3:]
            prepr = prepr2[:, sl * I3:]
            spw = prep_pw[sl]
            sps = prep_ps[sl]
            if kind == "scale":
                n_in = E if which in ("g", "u") else I3
                if prep_vmark[sl] > 0:
                    SY.wait_ge(s_dv.h, prep_vmark[sl])
                SY.dma_start(out=prepw[:, :n_in], in_=w_nat[which][c]
                             ).then_inc(spw.h, 16)
                spw.n += 16
            else:
                n_in = I3 if which in ("g", "u") else E
                sd_, srd_ = sc_d[which]
                spg = s_pg[sl]
                if prep_vmark[sl] > 0:
                    SY.wait_ge(s_dv.h, prep_vmark[sl])
                    nc.scalar.wait_ge(s_dv.h, prep_vmark[sl])
                    nc.gpsimd.wait_ge(s_dv.h, prep_vmark[sl])
                if not prep_gwait[0]:
                    for ts_ in (prep_ps[0], prep_ps[1], prep_pw[0], prep_pw[1]):
                        if ts_.n > 0:
                            nc.gpsimd.wait_ge(ts_.h, ts_.n)
                    prep_gwait[0] = True
                if spw.n > 0:
                    SY.wait_ge(spw.h, spw.n)
                SY.dma_start(out=prepw[:, :n_in], in_=wT_in[which][c]
                             ).then_inc(spw.h, 16)
                sin = sd_[c * 4:c * 4 + 4].unsqueeze(1).broadcast_to([4, 32, n_in])
                rin = srd_[c * 4:c * 4 + 4].unsqueeze(1).broadcast_to([4, 32, n_in])
                C.dma_start(out=preps[:, :n_in], in_=sin).then_inc(s_pc[sl].h, 16)
                G.dma_start(out=prepr[:, :n_in], in_=rin).then_inc(spg.h, 16)
                spw.n += 16
                s_pc[sl].n += 16
                spg.n += 16
            return sl

        def emit_prep_compute(kind, which, c, sl):
            prepw = prepw2[:, sl * I3:]
            preps = preps2[:, sl * I3:]
            prepr = prepr2[:, sl * I3:]
            prepq = prepq2[:, 0:]
            prepl = prepl2[:, 0:]
            spw = prep_pw[sl]
            sps = prep_ps[sl]
            if kind == "scale":
                n_in = E if which in ("g", "u") else I3
                nb_ = n_in // 32
                cols = scl_off[which] + c * nb_
                nc.vector.wait_ge(spw.h, spw.n)
                V.tensor_reduce(SCLS[:, cols:cols + nb_],
                                prepw[:, :n_in].rearrange("p (b c) -> p b c", c=32),
                                AX, OP.max, apply_absolute_value=True)
                V.tensor_scalar(SCLS[:, cols:cols + nb_], SCLS[:, cols:cols + nb_],
                                c_(colb, 5), 1e-12, OP.mult, OP.max)
                V.reciprocal(SCLR[:, cols:cols + nb_], SCLS[:, cols:cols + nb_])
                prep_vmark[sl] = s_dv.n
            else:
                n_in = I3 if which in ("g", "u") else E
                nc.vector.wait_ge(spw.h, spw.n)
                nc.vector.wait_ge(s_pc[sl].h, s_pc[sl].n)
                nc.vector.wait_ge(s_pg[sl].h, s_pg[sl].n)
                if sps.n > 0:
                    nc.vector.wait_ge(sps.h, sps.n)
                V.tensor_tensor(prepw[:, :n_in], prepw[:, :n_in],
                                prepr[:, :n_in], OP.mult)
                V.tensor_scalar(prepw[:, :n_in], prepw[:, :n_in], MAGIC, MAGIC,
                                OP.add, OP.subtract)
                V.tensor_tensor(prepw[:, :n_in], prepw[:, :n_in],
                                preps[:, :n_in], OP.mult)
                osps = prep_ps[1 - sl]
                if osps.n > 0:
                    nc.vector.wait_ge(osps.h, osps.n)
                V.tensor_copy(prepq[:, :n_in], prepw[:, :n_in])
                V.tensor_tensor(prepl[:, :n_in], prepw[:, :n_in],
                                prepq[:, :n_in], OP.subtract)
                prep_vmark[sl] = s_dv.n
                nc.scalar.wait_ge(s_dv.h, s_dv.n)
                for half in range(2):
                    src_t = (prepq if half == 0 else prepl)[:, :n_in]
                    C.dma_start(out=whl_d[(which, half)][c],
                                in_=src_t).then_inc(sps.h, 16)
                sps.n += 32

        def emit_tbatch():
                nc.tensor.wait_ge(s_dv.h, s_dv.n)
                jobs = []
                for wh in ("g", "u", "d"):
                    nchunks = ICH if wh in ("g", "u") else ECH
                    nb2 = 32 if wh in ("g", "u") else 96
                    for cc2 in range(nchunks):
                        for half2, (srcb, dstb) in enumerate(
                                ((SCLS, sc_d[wh][0]), (SCLR, sc_d[wh][1]))):
                            jobs.append((wh, cc2, nb2, srcb, dstb))
                tsems = [prep_ps[0], prep_ps[1], prep_pw[0], prep_pw[1]]
                vmarks = []
                smarks = []
                for ji, (wh, cc2, nb2, srcb, dstb) in enumerate(jobs):
                    cols = scl_off[wh] + cc2 * nb2
                    pslot = ji % 2
                    preg = ps_all[:nb2, (6 + pslot) * NB:(6 + pslot) * NB + 128]
                    if len(vmarks) >= 2:
                        nc.tensor.wait_ge(s_dv.h, vmarks[-2])
                    T.transpose(preg, srcb[:, cols:cols + nb2],
                                ident[:]).then_inc(s_mm.h, 1)
                    s_mm.n += 1
                    tslot = ji % 4
                    ts_ = tsems[tslot]
                    nc.vector.wait_ge(s_mm.h, s_mm.n)
                    if len(smarks) >= 4:
                        nc.vector.wait_ge(ts_.h, smarks[-4])
                    V.tensor_copy(tbuf2[:nb2, tslot * 128:(tslot + 1) * 128], preg)
                    vmarks.append(s_dv.n)
                    nc.scalar.wait_ge(s_dv.h, s_dv.n)
                    C.dma_start(out=dstb[:nb2, cc2 * 128:(cc2 + 1) * 128],
                                in_=tbuf2[:nb2, tslot * 128:(tslot + 1) * 128]
                                ).then_inc(ts_.h, 16)
                    ts_.n += 16
                    smarks.append(ts_.n)

        prep_iter = iter(prep_units)
        prep_pending = [None]

        def flush_prep():
            if prep_pending[0] is not None:
                emit_prep_compute(*prep_pending[0])
                prep_pending[0] = None

        def take_prep(kn):
            for _ in range(kn):
                u = next(prep_iter, None)
                if u is None:
                    continue
                if u[0] == "tbatch":
                    flush_prep()
                    emit_tbatch()
                else:
                    sl = emit_prep_load(*u)
                    flush_prep()
                    prep_pending[0] = (u[0], u[1], u[2], sl)

        qkv_evac_marks = []
        qkv_spill_marks = []

        for tb in range(TB):
            SY.wait_ge(s_dv.h, s_dv.n)
            SY.dma_start(out=xts[:].rearrange("p (c n) -> p c n", n=NB),
                         in_=xT_in[:, :, tb * NB:(tb + 1) * NB].transpose([1, 0, 2])
                         ).then_inc(s_x.h, 16)
            s_x.n += 16

            nc.vector.wait_ge(s_x.h, s_x.n)
            nc.vector.wait_ge(s_mm.h, s_mm.n)
            V.tensor_copy(xh_s, xts[:])
            V.tensor_tensor(xl_s, xts[:], xh_s, OP.subtract)
            sd_split = s_dv.n

            tiles = [("q", oc, None) for oc in range(ECH)] \
                + [("k", oc, None) for oc in range(ECH)] \
                + [("v", t_, h_) for t_ in range(NB // 128) for h_ in range(2)]
            for idx, (w, a, b_) in enumerate(tiles):
                pb = idx % 4
                nc.tensor.wait_ge(s_dv.h, sd_split)
                if len(qkv_evac_marks) >= 4:
                    nc.tensor.wait_ge(s_dv.h, qkv_evac_marks[-4])
                if w in ("q", "k"):
                    for ec in range(ECH):
                        lw = awq[w][:, ec * E + a * 128: ec * E + a * 128 + 128]
                        T.matmul(bank(pb), lw, xh_s[:, ec * NB:(ec + 1) * NB],
                                 start=(ec == 0), stop=False)
                        last = T.matmul(bank(pb), lw, xl_s[:, ec * NB:(ec + 1) * NB],
                                        start=False, stop=(ec == ECH - 1))
                else:
                    for ec in range(ECH):
                        lwh = xh_s[:, ec * NB + a * 128: ec * NB + a * 128 + 128]
                        lwl = xl_s[:, ec * NB + a * 128: ec * NB + a * 128 + 128]
                        rh = awq["v"][:, ec * E + b_ * NB: ec * E + (b_ + 1) * NB]
                        T.matmul(bank(pb), lwh, rh, start=(ec == 0), stop=False)
                        last = T.matmul(bank(pb), lwl, rh, start=False,
                                        stop=(ec == ECH - 1))
                last.then_inc(s_mm.h, 1)
                s_mm.n += 1

                nc.vector.wait_ge(s_mm.h, s_mm.n)
                sqs = s_sq[pb % 2]
                if sqs.n > 0:
                    nc.vector.wait_ge(sqs.h, sqs.n)
                st_ = qst[:, 0:NB]
                hi = qhl[:, (pb % 2) * 2 * NB:(pb % 2) * 2 * NB + NB]
                lo = qhl[:, (pb % 2) * 2 * NB + NB:(pb % 2) * 2 * NB + 2 * NB]
                V.tensor_scalar(st_, bank(pb), c_(scl, SW[w]), None, OP.mult)
                V.tensor_copy(hi, st_)
                V.tensor_tensor(lo, st_, hi, OP.subtract)
                qkv_evac_marks.append(s_dv.n)

                nc.scalar.wait_ge(s_dv.h, s_dv.n)
                if w == "q":
                    C.dma_start(out=qh_d[tb, a], in_=hi).then_inc(sqs.h, 16)
                    C.dma_start(out=ql_d[tb, a], in_=lo).then_inc(sqs.h, 16)
                elif w == "k":
                    KPT = NB // 128
                    C.dma_start(out=kh_d[tb * KPT:(tb + 1) * KPT, a]
                                .transpose([1, 0, 2]),
                                in_=hi.rearrange("p (t r) -> p t r", r=128)
                                ).then_inc(sqs.h, 16)
                    C.dma_start(out=kl_d[tb * KPT:(tb + 1) * KPT, a]
                                .transpose([1, 0, 2]),
                                in_=lo.rearrange("p (t r) -> p t r", r=128)
                                ).then_inc(sqs.h, 16)
                else:
                    kt_ = tb * (NB // 128) + a
                    C.dma_start(out=vh_d[kt_, :, b_ * NB:(b_ + 1) * NB],
                                in_=hi).then_inc(sqs.h, 16)
                    C.dma_start(out=vl_d[kt_, :, b_ * NB:(b_ + 1) * NB],
                                in_=lo).then_inc(sqs.h, 16)
                sqs.n += 32
                qkv_spill_marks.append((pb % 2, sqs.n))
                if idx % 2 == 1:
                    take_prep(1)

        take_prep(len(prep_units))
        flush_prep()
        stQ.close()

        stT = contextlib.ExitStack()
        SS = sb("SS", [128, 2 * KT * NB], BF16, stT)
        QT = sb("QT", [128, 2 * ECH * NB], BF16, stT)
        KV = sb("KV", [128, 2 * 2 * ECH * 128], BF16, stT)
        VV = sb("VV", [128, 2 * 2 * E], BF16, stT)
        S32 = sb("S32", [128, 2 * NB], F32, stT)
        SUMR = sb("SUMR", [128, NB], F32, stT)
        X1P = sb("X1P", [128, ECH * NB], F32, stT)
        XB = sb("XB", [128, 2 * ECH * NB], BF16, stT)
        MEAN = sb("MEAN", [128, NB], F32, stT)
        VARB = sb("VARB", [128, NB], F32, stT)
        RSTD = sb("RSTD", [128, NB], F32, stT)
        X1O = sb("X1O", [128, 2 * NB], F32, stT)

        ah_s, al_s = xh_s, xl_s

        def sh_t(kt):
            return SS[:, kt * NB:(kt + 1) * NB]

        def sl_t(kt):
            return SS[:, KT * NB + kt * NB:KT * NB + (kt + 1) * NB]

        SY.wait_ge(s_sq[0].h, s_sq[0].n)
        SY.wait_ge(s_sq[1].h, s_sq[1].n)
        SY.wait_ge(s_mm.h, s_mm.n)
        SY.wait_ge(s_dv.h, s_dv.n)
        SY.wait_ge(s_sc.h, s_sc.n)
        SY.wait_ge(s_gp.h, s_gp.n)
        SY.wait_ge(prep_ps[0].h, prep_ps[0].n)
        SY.wait_ge(prep_ps[1].h, prep_ps[1].n)
        SY.wait_ge(prep_pw[0].h, prep_pw[0].n)
        SY.wait_ge(prep_pw[1].h, prep_pw[1].n)
        prev_evac = 0
        for qb in range(TB):
            SY.wait_ge(s_mm.h, s_mm.n)
            SY.dma_start(out=QT[:, :ECH * NB].rearrange("p (c n) -> p c n", n=NB),
                         in_=qh_d[qb].transpose([1, 0, 2])).then_inc(s_q.h, 16)
            SY.dma_start(out=QT[:, ECH * NB:].rearrange("p (c n) -> p c n", n=NB),
                         in_=ql_d[qb].transpose([1, 0, 2])).then_inc(s_q.h, 16)
            s_q.n += 32
            q_ready = s_q.n

            base_kmm = s_mm.n
            k_marks = []
            exp_marks = []
            sl_marks = []
            for kt in range(KT):
                dbk = kt % 2
                if kt >= 2:
                    SY.wait_ge(s_mm.h, base_kmm + kt - 1)
                skp = s_kp[dbk]
                SY.dma_start(
                    out=KV[:, dbk * 2 * ECH * 128: dbk * 2 * ECH * 128 + ECH * 128]
                    .rearrange("p (c n) -> p c n", n=128),
                    in_=kh_d[kt].transpose([1, 0, 2])).then_inc(skp.h, 16)
                SY.dma_start(
                    out=KV[:, dbk * 2 * ECH * 128 + ECH * 128:
                           (dbk + 1) * 2 * ECH * 128]
                    .rearrange("p (c n) -> p c n", n=128),
                    in_=kl_d[kt].transpose([1, 0, 2])).then_inc(skp.h, 16)
                skp.n += 32
                k_marks.append((dbk, skp.n))

            for kt in range(KT):
                dbk = kt % 2
                sbk = 6 + (kt % 2)
                nc.tensor.wait_ge(s_q.h, q_ready)
                nc.tensor.wait_ge(s_kp[k_marks[kt][0]].h, k_marks[kt][1])
                if kt < 2 and prev_evac > 0:
                    nc.tensor.wait_ge(s_dv.h, prev_evac)
                if len(exp_marks) >= 2:
                    nc.tensor.wait_ge(s_sc.h, exp_marks[-2])
                kh0 = KV[:, dbk * 2 * ECH * 128:]
                for ec in range(ECH):
                    lwh = kh0[:, ec * 128:(ec + 1) * 128]
                    lwl = kh0[:, ECH * 128 + ec * 128: ECH * 128 + (ec + 1) * 128]
                    T.matmul(bank(sbk), lwh, QT[:, ec * NB:(ec + 1) * NB],
                             start=(ec == 0), stop=False)
                    T.matmul(bank(sbk), lwh,
                             QT[:, ECH * NB + ec * NB: ECH * NB + (ec + 1) * NB],
                             start=False, stop=False)
                    last = T.matmul(bank(sbk), lwl, QT[:, ec * NB:(ec + 1) * NB],
                                    start=False, stop=(ec == ECH - 1))
                last.then_inc(s_mm.h, 1)
                s_mm.n += 1

                if kt >= 1:
                    nc.tensor.wait_ge(s_dv.h, s_dv.n)
                    T.matmul(bank(5), ones_b[:], sh_t(kt - 1),
                             start=(kt == 1), stop=False)
                    T.matmul(bank(5), ones_b[:], sl_t(kt - 1),
                             start=False, stop=False)

                nc.scalar.wait_ge(s_mm.h, s_mm.n)
                if len(sl_marks) >= 2:
                    nc.scalar.wait_ge(s_dv.h, sl_marks[-2])
                C.activation(S32[:, dbk * NB:(dbk + 1) * NB], bank(sbk), ACT.Exp,
                             bias=bexp_c[:], scale=0.125)
                C.activation(sh_t(kt), bank(sbk), ACT.Exp,
                             bias=bexp_c[:], scale=0.125)
                exp_marks.append(s_sc.n)

                nc.vector.wait_ge(s_sc.h, s_sc.n)
                V.tensor_tensor(sl_t(kt), S32[:, dbk * NB:(dbk + 1) * NB],
                                sh_t(kt), OP.subtract)
                sl_marks.append(s_dv.n)

            nc.tensor.wait_ge(s_dv.h, s_dv.n)
            T.matmul(bank(5), ones_b[:], sh_t(KT - 1),
                     start=(KT == 1), stop=False)
            T.matmul(bank(5), ones_b[:], sl_t(KT - 1),
                     start=False, stop=True).then_inc(s_mm.h, 1)
            s_mm.n += 1

            nc.vector.wait_ge(s_mm.h, s_mm.n)
            V.reciprocal(SUMR[:], bank(5))
            sumr_done = s_dv.n

            base_bmm = s_mm.n
            for kt in range(KT):
                dbk = kt % 2
                if kt >= 2:
                    SY.wait_ge(s_mm.h, base_bmm + kt - 1)
                svp = s_vp[dbk]
                SY.dma_start(out=VV[:, dbk * 2 * E: dbk * 2 * E + E],
                             in_=vh_d[kt]).then_inc(svp.h, 16)
                SY.dma_start(out=VV[:, dbk * 2 * E + E: (dbk + 1) * 2 * E],
                             in_=vl_d[kt]).then_inc(svp.h, 16)
                svp.n += 32

                nc.tensor.wait_ge(svp.h, svp.n)
                if kt == 0:
                    nc.tensor.wait_ge(s_dv.h, sumr_done)
                    nc.tensor.wait_ge(s_sc.h, s_sc.n)
                vh0 = VV[:, dbk * 2 * E:]
                for ec in range(ECH):
                    lwh = vh0[:, ec * 128:(ec + 1) * 128]
                    lwl = vh0[:, E + ec * 128: E + (ec + 1) * 128]
                    T.matmul(bank(ec), lwh, sh_t(kt), start=(kt == 0), stop=False)
                    T.matmul(bank(ec), lwh, sl_t(kt), start=False, stop=False)
                    last = T.matmul(bank(ec), lwl, sh_t(kt), start=False,
                                    stop=(kt == KT - 1))
                last.then_inc(s_mm.h, 1)
                s_mm.n += 1

            nc.vector.wait_ge(s_mm.h, s_mm.n)
            for ec in range(ECH):
                st_ = qst[:, 0:NB]
                V.tensor_tensor(st_, bank(ec), SUMR[:], OP.mult)
                V.tensor_copy(ah_s[:, ec * NB:(ec + 1) * NB], st_)
                V.tensor_tensor(al_s[:, ec * NB:(ec + 1) * NB], st_,
                                ah_s[:, ec * NB:(ec + 1) * NB], OP.subtract)
            evac_done = s_dv.n
            prev_evac = evac_done

            SY.wait_ge(s_dv.h, evac_done)
            SY.dma_start(out=xts[:].rearrange("p (c n) -> p c n", n=NB),
                         in_=xT_in[:, :, qb * NB:(qb + 1) * NB].transpose([1, 0, 2])
                         ).then_inc(s_x.h, 16)
            s_x.n += 16
            xts_ready = s_x.n

            proj_evac_marks = []
            for oc in range(ECH):
                pb = oc % 2
                nc.tensor.wait_ge(s_dv.h, evac_done)
                if len(proj_evac_marks) >= 2:
                    nc.tensor.wait_ge(s_dv.h, proj_evac_marks[-2])
                for ec in range(ECH):
                    lw = awq["p"][:, ec * E + oc * 128: ec * E + oc * 128 + 128]
                    T.matmul(bank(pb), lw, ah_s[:, ec * NB:(ec + 1) * NB],
                             start=(ec == 0), stop=False)
                    last = T.matmul(bank(pb), lw, al_s[:, ec * NB:(ec + 1) * NB],
                                    start=False, stop=(ec == ECH - 1))
                last.then_inc(s_mm.h, 1)
                s_mm.n += 1

                nc.vector.wait_ge(s_mm.h, s_mm.n)
                if oc == 0:
                    nc.vector.wait_ge(s_x.h, xts_ready)
                xp = X1P[:, oc * NB:(oc + 1) * NB]
                V.tensor_scalar(xp, bank(pb), c_(scl, SW["p"]),
                                c_(bias_sb, oc), OP.mult, OP.add)
                V.tensor_tensor(xp, xp, xts[:, oc * NB:(oc + 1) * NB], OP.add)
                V.tensor_copy(XB[:, oc * NB:(oc + 1) * NB], xp)
                proj_evac_marks.append(s_dv.n)

                nc.scalar.wait_ge(s_dv.h, s_dv.n)
                C.activation(XB[:, ECH * NB + oc * NB: ECH * NB + (oc + 1) * NB],
                             XB[:, oc * NB:(oc + 1) * NB], ACT.Square)

                nc.tensor.wait_ge(s_sc.h, s_sc.n)
                T.matmul(bank(2), ones_b[:], XB[:, oc * NB:(oc + 1) * NB],
                         start=(oc == 0), stop=(oc == ECH - 1))
                mm2 = T.matmul(bank(3), ones_b[:],
                               XB[:, ECH * NB + oc * NB: ECH * NB + (oc + 1) * NB],
                               start=(oc == 0), stop=(oc == ECH - 1))
                if oc == ECH - 1:
                    mm2.then_inc(s_mm.h, 1)
                    s_mm.n += 1

            nc.vector.wait_ge(s_mm.h, s_mm.n)
            V.tensor_scalar(MEAN[:], bank(2), 1.0 / E, None, OP.mult)
            V.tensor_scalar(VARB[:], bank(3), 1.0 / E, None, OP.mult)
            V.tensor_tensor(RSTD[:], MEAN[:], MEAN[:], OP.mult)
            V.tensor_tensor(VARB[:], VARB[:], RSTD[:], OP.subtract)
            nc.scalar.wait_ge(s_dv.h, s_dv.n)
            C.activation(RSTD[:], VARB[:], ACT.Sqrt, bias=eps_c[:])
            nc.vector.wait_ge(s_sc.h, s_sc.n)
            V.reciprocal(RSTD[:], RSTD[:])

            for oc in range(ECH):
                sxs = s_sx[oc % 2]
                if sxs.n > 0:
                    nc.vector.wait_ge(sxs.h, sxs.n)
                xo = X1O[:, (oc % 2) * NB:((oc % 2) + 1) * NB]
                xp = X1P[:, oc * NB:(oc + 1) * NB]
                V.tensor_tensor(xo, xp, MEAN[:], OP.subtract)
                V.tensor_tensor(xo, xo, RSTD[:], OP.mult)
                V.reduce_max(c_(colb, 6), xo, AX, apply_absolute_value=True)
                V.tensor_tensor(c_(colb, 0), c_(colb, 0), c_(colb, 6), OP.max)

                nc.scalar.wait_ge(s_dv.h, s_dv.n)
                C.dma_start(out=x1_d[oc, :, qb * NB:(qb + 1) * NB],
                            in_=xo).then_inc(sxs.h, 16)
                sxs.n += 16

        stT.close()
        stA.close()

        def emit_collective(idx, cols, finish_fn):
            nc.tensor.wait_ge(s_dv.h, s_dv.n)
            T.transpose(prow, colb[:, cols[0]:cols[0] + 1], ident[:]
                        ).then_inc(s_mm.h, 1)
            s_mm.n += 1
            if len(cols) > 1:
                T.transpose(prow2, colb[:, cols[1]:cols[1] + 1], ident[:]
                            ).then_inc(s_mm.h, 1)
                s_mm.n += 1
            nc.vector.wait_ge(s_mm.h, s_mm.n)
            V.reduce_max(ccs[:, 1:2], prow, AX)
            if len(cols) > 1:
                V.reduce_max(ccs[:, 2:3], prow2, AX)
            nc.gpsimd.wait_ge(s_dv.h, s_dv.n)
            G.dma_start(out=cc_in[idx], in_=ccs[:]).then_inc(s_gd.h, 16)
            s_gd.n += 16
            nc.gpsimd.wait_ge(s_gd.h, s_gd.n)
            G.collective_compute(
                "AllReduce", OP.max, replica_groups=[list(range(8))],
                ins=[cc_in[idx].opt()], outs=[cc_out[idx].opt()]
            ).then_inc(s_cc.h, 1)
            s_cc.n += 1
            nc.gpsimd.wait_ge(s_cc.h, s_cc.n)
            G.dma_start(out=ccs[:], in_=cc_out[idx]).then_inc(s_gd.h, 16)
            s_gd.n += 16
            nc.vector.wait_ge(s_gd.h, s_gd.n)
            finish_fn()
            nc.tensor.wait_ge(s_dv.h, s_dv.n)
            T.matmul(pbc8, ones_r[:], row[:1, 16:24], start=True, stop=True
                     ).then_inc(s_mm.h, 1)
            s_mm.n += 1
            nc.vector.wait_ge(s_mm.h, s_mm.n)
            nc.vector.wait_ge(s_sc.h, s_sc.n)
            V.tensor_copy(scl[:, 0:8], pbc8)
            return s_dv.n

        FSTI = sb("FSTI", [128, 2 * SL], F32, ctx)
        FOUT = sb("FOUT", [128, 2 * SL], F32, ctx)
        WCH = sb("WCH", [128, 2 * 2 * ECH * 128], BF16, ctx)
        stN = contextlib.ExitStack()
        NL = sb("NL", [128, 2 * ECH * S], BF16, stN)
        nh_s = NL[:, :ECH * S]
        nl_s = NL[:, ECH * S:]

        def fin1():
            V.tensor_scalar(c_(row, RSX), ccs[:, 1:2], c_(row, R32K), 1e-12,
                            OP.mult, OP.max)
            V.reciprocal(row[:1, 16:17], c_(row, RSX))
        emit_collective(0, (0,), fin1)

        SY.wait_ge(s_sx[0].h, s_sx[0].n)
        SY.wait_ge(s_sx[1].h, s_sx[1].n)
        p3_marks = []
        for tb in range(TB):
            dbk = tb % 2
            if len(p3_marks) >= 2:
                SY.wait_ge(s_dv.h, p3_marks[-2][0])
                SY.wait_ge(s_gp.h, p3_marks[-2][1])
            sfp = s_fp[dbk]
            SY.dma_start(out=FSTI[:, dbk * SL: dbk * SL + ECH * NB]
                         .rearrange("p (c n) -> p c n", n=NB),
                         in_=x1_d[:, :, tb * NB:(tb + 1) * NB].transpose([1, 0, 2])
                         ).then_inc(sfp.h, 16)
            sfp.n += 16

            nc.vector.wait_ge(sfp.h, sfp.n)
            src = FSTI[:, dbk * SL: dbk * SL + ECH * NB]
            st_ = FOUT[:, dbk * SL: dbk * SL + ECH * NB]
            st3 = st_.rearrange("p (c n) -> p c n", n=NB)
            nh_v = nh_s.rearrange("p (c s) -> p c s", s=S)[:, :, tb * NB:(tb + 1) * NB]
            nl_v = nl_s.rearrange("p (c s) -> p c s", s=S)[:, :, tb * NB:(tb + 1) * NB]
            V.tensor_scalar(st_, src, c_(scl, ISXc), MAGIC, OP.mult, OP.add)
            V.tensor_scalar(st_, st_, MAGIC, None, OP.subtract)
            V.tensor_copy(nh_v, st3)
            V.tensor_tensor(nl_v, st3, nh_v, OP.subtract)
            p3_marks.append((s_dv.n, s_gp.n))
        p3_done = s_dv.n

        SY.wait_ge(prep_ps[0].h, prep_ps[0].n)
        SY.wait_ge(prep_ps[1].h, prep_ps[1].n)
        wch_last = {0: 0, 1: 0}
        ev_marks = []

        def emit_w_chunk_load(which, ich):
            dbw = ich % 2
            if wch_last[dbw] > 0:
                SY.wait_ge(s_mm.h, wch_last[dbw])
            skp = s_kp[dbw]
            for half in range(2):
                src = whl_d[(which, half)][:, :, ich * 128:(ich + 1) * 128]
                SY.dma_start(
                    out=WCH[:, dbw * 2 * ECH * 128 + half * ECH * 128:
                           dbw * 2 * ECH * 128 + (half + 1) * ECH * 128]
                    .rearrange("p (c n) -> p c n", n=128),
                    in_=src.transpose([1, 0, 2])).then_inc(skp.h, 16)
            skp.n += 32
            return dbw, skp

        def emit_wmm_tiles(ich, dbw):
            for tb in range(TB):
                pb = (ich * TB + tb) % 4
                nc.tensor.wait_ge(s_dv.h, p3_marks[tb][0])
                if len(ev_marks) >= 4:
                    nc.tensor.wait_ge(s_sc.h, ev_marks[-4])
                w0 = WCH[:, dbw * 2 * ECH * 128:]
                for ec in range(ECH):
                    lwh = w0[:, ec * 128:(ec + 1) * 128]
                    lwl = w0[:, ECH * 128 + ec * 128: ECH * 128 + (ec + 1) * 128]
                    nh_v = nh_s[:, ec * S + tb * NB: ec * S + (tb + 1) * NB]
                    nl_v = nl_s[:, ec * S + tb * NB: ec * S + (tb + 1) * NB]
                    T.matmul(bank(pb), lwh, nh_v, start=(ec == 0), stop=False)
                    T.matmul(bank(pb), lwh, nl_v, start=False, stop=False)
                    last = T.matmul(bank(pb), lwl, nh_v, start=False,
                                    stop=(ec == ECH - 1))
                last.then_inc(s_mm.h, 1)
                s_mm.n += 1
                wch_last[dbw] = s_mm.n
                yield tb, pb

        wload_pend = {0: emit_w_chunk_load("g", 0)}
        for ich in range(ICH):
            dbw, skp = wload_pend.pop(ich)
            if ich + 1 < ICH:
                wload_pend[ich + 1] = emit_w_chunk_load("g", ich + 1)
            nc.tensor.wait_ge(skp.h, skp.n)
            sos = s_so[ich % 2]
            for tb, pb in emit_wmm_tiles(ich, dbw):
                nc.scalar.wait_ge(s_mm.h, s_mm.n)
                if tb == 0 and sos.n > 0:
                    nc.scalar.wait_ge(sos.h, sos.n)
                off = (ich % 2) * SL
                C.activation(FOUT[:, off + tb * NB: off + (tb + 1) * NB],
                             bank(pb), ACT.Copy)
                ev_marks.append(s_sc.n)
                nc.vector.wait_ge(s_sc.h, s_sc.n)
                V.reduce_max(c_(colb, 6),
                             FOUT[:, off + tb * NB: off + (tb + 1) * NB],
                             AX, apply_absolute_value=True)
                V.tensor_tensor(c_(colb, 2), c_(colb, 2), c_(colb, 6), OP.max)
            nc.scalar.wait_ge(s_dv.h, s_dv.n)
            C.dma_start(out=gtr_d[ich],
                        in_=FOUT[:, (ich % 2) * SL:(ich % 2) * SL + TB * NB]
                        ).then_inc(sos.h, 16)
            sos.n += 16

        def fin2g():
            V.tensor_tensor(c_(row, RT0), ccs[:, 1:2], c_(row, RSX), OP.mult)
            V.tensor_scalar(c_(row, RSG), c_(row, RT0), c_(row, R32K), 1e-12,
                            OP.mult, OP.max)
            V.reciprocal(c_(row, RT0), c_(row, RSG))
            V.tensor_tensor(row[:1, 17:18], c_(row, RSX), c_(row, RT0), OP.mult)
            V.tensor_copy(row[:1, 18:19], c_(row, RSG))
        emit_collective(1, (2,), fin2g)

        SY.wait_ge(s_so[0].h, s_so[0].n)
        SY.wait_ge(s_so[1].h, s_so[1].n)
        f2_spill_mark = [0]
        f4_state = {"vmark": 0, "spill": 0}

        def f4_units():
            for ich in range(ICH):
                src = FSTI[:, 0:S]
                gq = FSTI[:, SL:SL + S] if ich % 2 == 0 else FOUT[:, SL:SL + S]
                if f4_state["vmark"] > 0:
                    SY.wait_ge(s_dv.h, f4_state["vmark"])
                if f4_state["spill"] > 0:
                    SY.wait_ge(s_tr.h, f4_state["spill"])
                SY.dma_start(out=src, in_=gtr_d[ich]).then_inc(s_gn[0].h, 16)
                s_gn[0].n += 16

                nc.vector.wait_ge(s_gn[0].h, s_gn[0].n)
                V.tensor_scalar(gq, src, c_(scl, KGc), MAGIC, OP.mult, OP.add)
                V.tensor_scalar(gq, gq, MAGIC, None, OP.subtract)
                rmark = s_dv.n
                yield
                nc.scalar.wait_ge(s_dv.h, rmark)
                C.activation(src, gq, ACT.Sigmoid, scale=c_(scl, SGc))
                yield
                C.activation(src, src, ACT.Identity, bias=magic_c[:],
                             scale=65536.0)
                yield
                C.activation(src, src, ACT.Identity, bias=nmagic_c[:])
                smark = s_sc.n
                yield
                nc.vector.wait_ge(s_sc.h, smark)
                V.tensor_scalar(src, src, 65535.0, None, OP.min)
                V.tensor_tensor(src, src, gq, OP.mult)
                V.reduce_max(c_(colb, 6), src, AX, apply_absolute_value=True)
                V.tensor_tensor(c_(colb, 3), c_(colb, 3), c_(colb, 6), OP.max)
                f4_state["vmark"] = s_dv.n
                yield
                nc.gpsimd.wait_ge(s_dv.h, f4_state["vmark"])
                G.dma_start(out=tr_d[ich], in_=src).then_inc(s_tr.h, 16)
                s_tr.n += 16
                f4_state["spill"] = s_tr.n
                yield
        f4_iter = f4_units()

        wload_pend = {0: emit_w_chunk_load("u", 0)}
        for ich in range(ICH):
            dbw, skp = wload_pend.pop(ich)
            if ich + 1 < ICH:
                wload_pend[ich + 1] = emit_w_chunk_load("u", ich + 1)
            nc.tensor.wait_ge(skp.h, skp.n)
            for tb, pb in emit_wmm_tiles(ich, dbw):
                nc.scalar.wait_ge(s_mm.h, s_mm.n)
                if tb == 0 and f2_spill_mark[0] > 0:
                    nc.scalar.wait_ge(s_so[0].h, f2_spill_mark[0])
                C.activation(FOUT[:, tb * NB:(tb + 1) * NB], bank(pb), ACT.Copy)
                ev_marks.append(s_sc.n)
                nc.vector.wait_ge(s_sc.h, s_sc.n)
                V.reduce_max(c_(colb, 6), FOUT[:, tb * NB:(tb + 1) * NB],
                             AX, apply_absolute_value=True)
                V.tensor_tensor(c_(colb, 1), c_(colb, 1), c_(colb, 6), OP.max)
                if tb != 1 and tb != 2:
                    next(f4_iter, None)
            nc.scalar.wait_ge(s_dv.h, s_dv.n)
            C.dma_start(out=upr_d[ich], in_=FOUT[:, 0:TB * NB]
                        ).then_inc(s_so[0].h, 16)
            s_so[0].n += 16
            f2_spill_mark[0] = s_so[0].n
        for _ in f4_iter:
            pass

        def fin23():
            V.tensor_tensor(c_(row, RT0), ccs[:, 1:2], c_(row, RSX), OP.mult)
            V.tensor_scalar(c_(row, RSU), c_(row, RT0), c_(row, R32K), 1e-12,
                            OP.mult, OP.max)
            V.reciprocal(c_(row, RT0), c_(row, RSU))
            V.tensor_tensor(row[:1, 19:20], c_(row, RSX), c_(row, RT0), OP.mult)
            V.tensor_scalar(c_(row, RT0), ccs[:, 2:3], 1.0 / 65536.0, None,
                            OP.mult)
            V.tensor_tensor(c_(row, RT0), c_(row, RT0), c_(row, RSG), OP.mult)
            V.tensor_scalar(c_(row, RT0), c_(row, RT0), c_(row, R32K), None,
                            OP.mult)
            V.tensor_scalar(c_(row, RST), c_(row, RT0), 1e-12, None, OP.max)
            V.reciprocal(c_(row, RT0), c_(row, RST))
            V.tensor_scalar(c_(row, RT1), c_(row, RSG), 1.0 / 65536.0, None,
                            OP.mult)
            V.tensor_tensor(row[:1, 20:21], c_(row, RT1), c_(row, RT0), OP.mult)
        bc23 = emit_collective(2, (1, 3), fin23)

        stN.close()

        stF4 = contextlib.ExitStack()
        GN = sb("GN", [128, 2 * SL], F32, stF4)

        SY.wait_ge(s_so[0].h, s_so[0].n)
        SY.wait_ge(s_tr.h, s_tr.n)
        f5_dv = []
        f5_sc = []
        for ich in range(ICH):
            dbk = ich % 2
            if len(f5_dv) >= 2:
                SY.wait_ge(s_dv.h, f5_dv[-2])
            sfp = s_fp[dbk]
            sgn = s_gn[dbk]
            SY.dma_start(out=FSTI[:, dbk * SL: dbk * SL + S],
                         in_=upr_d[ich]).then_inc(sfp.h, 16)
            SY.dma_start(out=GN[:, dbk * SL: dbk * SL + S],
                         in_=tr_d[ich]).then_inc(sgn.h, 16)
            sfp.n += 16
            sgn.n += 16

            up = FSTI[:, dbk * SL: dbk * SL + S]
            tr = GN[:, dbk * SL: dbk * SL + S]
            ot = FOUT[:, dbk * SL: dbk * SL + S]
            nc.scalar.wait_ge(sfp.h, sfp.n)
            if ich == 0:
                nc.scalar.wait_ge(s_dv.h, bc23)
            C.activation(up, up, ACT.Identity, bias=magic_c[:], scale=c_(scl, KUc))
            C.activation(up, up, ACT.Identity, bias=nmagic_c[:])
            f5_sc.append(s_sc.n)

            nc.vector.wait_ge(sgn.h, sgn.n)
            sos = s_or[dbk]
            if ich == 0:
                nc.vector.wait_ge(s_so[0].h, s_so[0].n)
            if sos.n > 0:
                nc.vector.wait_ge(sos.h, sos.n)
            V.tensor_scalar(tr, tr, c_(scl, KTc), MAGIC, OP.mult, OP.add)
            V.tensor_scalar(tr, tr, MAGIC, None, OP.subtract)
            nc.vector.wait_ge(s_sc.h, f5_sc[-1])
            V.tensor_tensor(ot, tr, up, OP.mult)
            V.reduce_max(c_(colb, 6), ot, AX, apply_absolute_value=True)
            V.tensor_tensor(c_(colb, 4), c_(colb, 4), c_(colb, 6), OP.max)
            f5_dv.append(s_dv.n)

            nc.gpsimd.wait_ge(s_dv.h, s_dv.n)
            G.dma_start(out=or_d[ich], in_=FOUT[:, dbk * SL: dbk * SL + TB * NB]
                        ).then_inc(sos.h, 16)
            sos.n += 16

        stF4.close()


        def fin4():
            V.tensor_tensor(c_(row, RT0), c_(row, RST), c_(row, RSU), OP.mult)
            V.tensor_tensor(c_(row, RT1), ccs[:, 1:2], c_(row, RT0), OP.mult)
            V.tensor_scalar(c_(row, RSO), c_(row, RT1), c_(row, R32K), 1e-12,
                            OP.mult, OP.max)
            V.reciprocal(c_(row, RT1), c_(row, RSO))
            V.tensor_tensor(row[:1, 21:22], c_(row, RT0), c_(row, RT1), OP.mult)
            V.tensor_copy(row[:1, 22:23], c_(row, RSO))
        emit_collective(3, (4,), fin4)

        stF2 = contextlib.ExitStack()
        WD = sb("WD", [128, 2 * ICH * E], BF16, stF2)
        OTB = sb("OTB", [128, 2 * 2 * NB], BF16, stF2)
        XSB = sb("XSB", [128, 4 * NB], BF16, stF2)

        SY.wait_ge(s_or[0].h, s_or[0].n)
        SY.wait_ge(s_or[1].h, s_or[1].n)
        SY.dma_start(out=WD[:, :ICH * E].rearrange("p (c n) -> p c n", n=E),
                     in_=whl_d[("d", 0)].transpose([1, 0, 2])).then_inc(s_kp[0].h, 16)
        SY.dma_start(out=WD[:, ICH * E:].rearrange("p (c n) -> p c n", n=E),
                     in_=whl_d[("d", 1)].transpose([1, 0, 2])).then_inc(s_kp[0].h, 16)
        s_kp[0].n += 32
        nc.tensor.wait_ge(s_kp[0].h, s_kp[0].n)

        for tb in range(TB):
            SY.wait_ge(s_dv.h, s_dv.n)
            SY.wait_ge(s_out.h, s_out.n)
            SY.dma_start(out=FSTI[:, SL:SL + ECH * NB]
                         .rearrange("p (c n) -> p c n", n=NB),
                         in_=x1_d[:, :, tb * NB:(tb + 1) * NB].transpose([1, 0, 2])
                         ).then_inc(s_x1b.h, 16)
            s_x1b.n += 16

            ot_marks = []
            f6_dv = []
            for ich in range(ICH):
                dbk = ich % 2
                if len(f6_dv) >= 2:
                    SY.wait_ge(s_dv.h, f6_dv[-2])
                sfp = s_fp[dbk]
                SY.dma_start(out=FSTI[:, dbk * NB:(dbk + 1) * NB],
                             in_=or_d[ich, :, tb * NB:(tb + 1) * NB]
                             ).then_inc(sfp.h, 16)
                sfp.n += 16

                nc.vector.wait_ge(sfp.h, sfp.n)
                if len(ot_marks) >= 2:
                    nc.vector.wait_ge(s_mm.h, ot_marks[-2])
                src = FSTI[:, dbk * NB:(dbk + 1) * NB]
                oh = OTB[:, dbk * 2 * NB: dbk * 2 * NB + NB]
                ol = OTB[:, dbk * 2 * NB + NB: (dbk + 1) * 2 * NB]
                V.tensor_scalar(src, src, c_(scl, KOc), MAGIC, OP.mult, OP.add)
                V.tensor_scalar(src, src, MAGIC, None, OP.subtract)
                V.tensor_copy(oh, src)
                V.tensor_tensor(ol, src, oh, OP.subtract)
                f6_dv.append(s_dv.n)

                nc.tensor.wait_ge(s_dv.h, s_dv.n)
                for hc in range(ECH):
                    lwh = WD[:, ich * E + hc * 128: ich * E + hc * 128 + 128]
                    lwl = WD[:, ICH * E + ich * E + hc * 128:
                             ICH * E + ich * E + hc * 128 + 128]
                    T.matmul(bank(hc), lwh, oh, start=(ich == 0), stop=False)
                    T.matmul(bank(hc), lwh, ol, start=False, stop=False)
                    last = T.matmul(bank(hc), lwl, oh, start=False,
                                    stop=(ich == ICH - 1))
                last.then_inc(s_mm.h, 1)
                s_mm.n += 1
                ot_marks.append(s_mm.n)

            nc.vector.wait_ge(s_mm.h, s_mm.n)
            nc.vector.wait_ge(s_x1b.h, s_x1b.n)
            for hc in range(ECH):
                xp = FOUT[:, hc * NB:(hc + 1) * NB]
                V.tensor_scalar(xp, bank(hc), c_(scl, SOc), None, OP.mult)
                V.tensor_tensor(xp, xp, FSTI[:, SL + hc * NB: SL + (hc + 1) * NB],
                                OP.add)
            f6_evac_done = s_dv.n

            xstat_marks = []
            for hc in range(ECH):
                nc.scalar.wait_ge(s_dv.h, f6_evac_done)
                if len(xstat_marks) >= 2:
                    nc.scalar.wait_ge(s_mm.h, xstat_marks[-2])
                xp = FOUT[:, hc * NB:(hc + 1) * NB]
                xb = XSB[:, (hc % 2) * 2 * NB:(hc % 2) * 2 * NB + NB]
                xq = XSB[:, (hc % 2) * 2 * NB + NB:((hc % 2) + 1) * 2 * NB]
                C.activation(xb, xp, ACT.Copy)
                C.activation(xq, xp, ACT.Square)

                nc.tensor.wait_ge(s_sc.h, s_sc.n)
                nc.tensor.wait_ge(s_dv.h, f6_evac_done)
                T.matmul(bank(0), ones_b[:], xb,
                         start=(hc == 0), stop=(hc == ECH - 1))
                T.matmul(bank(1), ones_b[:], xq,
                         start=(hc == 0), stop=(hc == ECH - 1)
                         ).then_inc(s_mm.h, 1)
                s_mm.n += 1
                xstat_marks.append(s_mm.n)

            nc.vector.wait_ge(s_mm.h, s_mm.n)
            mn = FSTI[:, 2 * NB:3 * NB]
            vb = FSTI[:, 3 * NB:4 * NB]
            rs = FSTI[:, 4 * NB:5 * NB]
            V.tensor_scalar(mn, bank(0), 1.0 / E, None, OP.mult)
            V.tensor_scalar(vb, bank(1), 1.0 / E, None, OP.mult)
            V.tensor_tensor(rs, mn, mn, OP.mult)
            V.tensor_tensor(vb, vb, rs, OP.subtract)
            nc.scalar.wait_ge(s_dv.h, s_dv.n)
            C.activation(rs, vb, ACT.Sqrt, bias=eps_c[:])
            nc.vector.wait_ge(s_sc.h, s_sc.n)
            V.reciprocal(rs, rs)

            for hc in range(ECH):
                xp = FOUT[:, hc * NB:(hc + 1) * NB]
                V.tensor_tensor(xp, xp, mn, OP.subtract)
                V.tensor_tensor(xp, xp, rs, OP.mult)

                nc.scalar.wait_ge(s_dv.h, s_dv.n)
                C.dma_start(out=out_d[hc, :, tb * NB:(tb + 1) * NB],
                            in_=xp).then_inc(s_out.h, 16)
                s_out.n += 16

        nc.gpsimd.wait_ge(s_out.h, s_out.n)
        stF2.close()
    return nc


_CACHE = {}


def _get_program(S):
    if S not in _CACHE:
        _CACHE[S] = build_program(S)
    return _CACHE[S]


def stage_inputs(L0_attn_x, L0_attn_w_query, L0_attn_w_key, L0_attn_w_value,
                 L0_attn_w_proj, L0_attn_b_proj, L0_ffn_w_gate, L0_ffn_w_up,
                 L0_ffn_w_down):
    x = np.ascontiguousarray(np.asarray(L0_attn_x, dtype=np.float32))
    B, S, _E = x.shape

    def t8(w):
        return np.ascontiguousarray(np.asarray(w, np.float32).T.reshape(ECH, 128, E))

    shared = {
        "awT_q": t8(L0_attn_w_query), "awT_k": t8(L0_attn_w_key),
        "awT_v": t8(L0_attn_w_value), "awT_p": t8(L0_attn_w_proj),
        "bproj": np.ascontiguousarray(np.asarray(L0_attn_b_proj, np.float32)
                                      .reshape(ECH, 128).T),
        "wgT": np.ascontiguousarray(np.asarray(L0_ffn_w_gate, np.float32).T
                                    .reshape(ECH, 128, I3)),
        "wuT": np.ascontiguousarray(np.asarray(L0_ffn_w_up, np.float32).T
                                    .reshape(ECH, 128, I3)),
        "wdT": np.ascontiguousarray(np.asarray(L0_ffn_w_down, np.float32).T
                                    .reshape(ICH, 128, E)),
        "wg_nat": np.ascontiguousarray(np.asarray(L0_ffn_w_gate, np.float32)
                                       .reshape(ICH, 128, E)),
        "wu_nat": np.ascontiguousarray(np.asarray(L0_ffn_w_up, np.float32)
                                       .reshape(ICH, 128, E)),
        "wd_nat": np.ascontiguousarray(np.asarray(L0_ffn_w_down, np.float32)
                                       .reshape(ECH, 128, I3)),
        "ident32": np.eye(128, dtype=np.float32),
    }
    in_maps = []
    for c in range(B):
        m = dict(shared)
        m["xT"] = np.ascontiguousarray(x[c].T.reshape(ECH, 128, S))
        in_maps.append(m)
    return in_maps


def kernel(**inputs):
    in_maps = stage_inputs(**inputs)
    B = len(in_maps)
    S = in_maps[0]["xT"].shape[2]
    nc = _get_program(S)
    res = run_bass_kernel_spmd(nc, in_maps, core_ids=list(range(B)))
    outs = []
    for c in range(B):
        y = np.asarray(res.results[c]["out"]).reshape(E, S).T
        outs.append(y)
    return np.ascontiguousarray(np.stack(outs)).astype(np.float32)


# version 12
# speedup vs baseline: 1.0596x; 1.0092x over previous
import contextlib
import numpy as np

import concourse.bass as bass
import concourse.mybir as mybir
from concourse.bass_utils import run_bass_kernel_spmd

F32 = mybir.dt.float32
BF16 = mybir.dt.bfloat16
OP = mybir.AluOpType
ACT = mybir.ActivationFunctionType
AX = mybir.AxisListType.X

MAGIC = 12582912.0
EXP_SHIFT = -24.0
E = 1024
ECH = 8
I3 = 3072
ICH = 24
NB = 512


class Sem:
    def __init__(self, ctx, nc, name):
        self.h = ctx.enter_context(nc.semaphore(name))
        self.n = 0
        self.name = name


_CHAIN_OPS = {
    "tensor_copy", "tensor_tensor", "tensor_scalar", "tensor_single_scalar",
    "tensor_add", "tensor_sub", "tensor_mul", "tensor_max", "tensor_reduce",
    "reduce_max", "reduce_sum", "reciprocal", "memset", "activation", "iota",
    "scalar_tensor_tensor", "select", "tensor_scalar_add", "tensor_scalar_sub",
    "tensor_scalar_mul", "tensor_scalar_max", "tensor_scalar_min",
}


class Eng:

    def __init__(self, eng, sem):
        self._e = eng
        self._s = sem

    def __getattr__(self, name):
        fn = getattr(self._e, name)
        if name in _CHAIN_OPS:
            def wrap(*a, **k):
                if self._s.n > 0:
                    self._e.wait_ge(self._s.h, self._s.n)
                ins = fn(*a, **k)
                ins.then_inc(self._s.h, 1)
                self._s.n += 1
                return ins
            return wrap
        if name in ("dma_start", "collective_compute", "dma_start_transpose"):
            def wrapd(*a, **k):
                if self._s.n > 0:
                    self._e.wait_ge(self._s.h, self._s.n)
                return fn(*a, **k)
            return wrapd
        return fn


def build_program(S=4096):
    TB = S // NB
    KT = S // 128
    SL = max(ECH * NB, S)

    nc = bass.Bass("TRN2", target_bir_lowering=False)

    xT_in = nc.dram_tensor("xT", [ECH, 128, S], F32, kind="ExternalInput").ap()
    awT = {w: nc.dram_tensor(f"awT_{w}", [ECH, 128, E], F32, kind="ExternalInput").ap()
           for w in ("q", "k", "v", "p")}
    bias_in = nc.dram_tensor("bproj", [128, ECH], F32, kind="ExternalInput").ap()
    wT_in = {"g": nc.dram_tensor("wgT", [ECH, 128, I3], F32, kind="ExternalInput").ap(),
             "u": nc.dram_tensor("wuT", [ECH, 128, I3], F32, kind="ExternalInput").ap(),
             "d": nc.dram_tensor("wdT", [ICH, 128, E], F32, kind="ExternalInput").ap()}
    w_nat = {"g": nc.dram_tensor("wg_nat", [ICH, 128, E], F32, kind="ExternalInput").ap(),
             "u": nc.dram_tensor("wu_nat", [ICH, 128, E], F32, kind="ExternalInput").ap(),
             "d": nc.dram_tensor("wd_nat", [ECH, 128, I3], F32, kind="ExternalInput").ap()}
    ident_in = nc.dram_tensor("ident32", [128, 128], F32, kind="ExternalInput").ap()

    out_d = nc.dram_tensor("out", [ECH, 128, S], F32, kind="ExternalOutput").ap()

    qh_d = nc.dram_tensor("qh_d", [TB, ECH, 128, NB], BF16).ap()
    ql_d = nc.dram_tensor("ql_d", [TB, ECH, 128, NB], BF16).ap()
    kh_d = nc.dram_tensor("kh_d", [KT, ECH, 128, 128], BF16).ap()
    kl_d = nc.dram_tensor("kl_d", [KT, ECH, 128, 128], BF16).ap()
    vh_d = nc.dram_tensor("vh_d", [KT, 128, E], BF16).ap()
    vl_d = nc.dram_tensor("vl_d", [KT, 128, E], BF16).ap()
    x1_d = nc.dram_tensor("x1_d", [ECH, 128, S], F32).ap()
    whl_d = {("g", 0): nc.dram_tensor("wgh_d", [ECH, 128, I3], BF16).ap(),
             ("g", 1): nc.dram_tensor("wgl_d", [ECH, 128, I3], BF16).ap(),
             ("u", 0): nc.dram_tensor("wuh_d", [ECH, 128, I3], BF16).ap(),
             ("u", 1): nc.dram_tensor("wul_d", [ECH, 128, I3], BF16).ap(),
             ("d", 0): nc.dram_tensor("wdh_d", [ICH, 128, E], BF16).ap(),
             ("d", 1): nc.dram_tensor("wdl_d", [ICH, 128, E], BF16).ap()}
    sc_d = {"g": (nc.dram_tensor("sg_d", [32, I3], F32).ap(),
                  nc.dram_tensor("sgr_d", [32, I3], F32).ap()),
            "u": (nc.dram_tensor("su_d", [32, I3], F32).ap(),
                  nc.dram_tensor("sur_d", [32, I3], F32).ap()),
            "d": (nc.dram_tensor("sd_d", [96, E], F32).ap(),
                  nc.dram_tensor("sdr_d", [96, E], F32).ap())}
    upr_d = nc.dram_tensor("upr_d", [ICH, 128, S], F32).ap()
    gtr_d = nc.dram_tensor("gtr_d", [ICH, 128, S], F32).ap()
    tr_d = nc.dram_tensor("tr_d", [ICH, 128, S], F32).ap()
    or_d = nc.dram_tensor("or_d", [ICH, 128, S], F32).ap()
    cc_in = [nc.dram_tensor(f"cc_in{i}", [1, 16], F32).ap() for i in range(4)]
    cc_out = [nc.dram_tensor(f"cc_out{i}", [1, 16], F32).ap() for i in range(4)]

    ctx = contextlib.ExitStack()
    with ctx:
        def sb(name, shape, dtype, st=None):
            return (st or ctx).enter_context(nc.sbuf_tensor(name, shape, dtype))

        ones_b = sb("ones_b", [128, 128], BF16)
        ones_r = sb("ones_r", [1, 128], F32)
        ident = sb("ident", [128, 128], F32)
        scl = sb("scl", [128, 16], F32)
        row = sb("row", [1, 32], F32)
        colb = sb("colb", [128, 8], F32)
        eps_c = sb("eps_c", [128, 1], F32)
        bexp_c = sb("bexp_c", [128, 1], F32)
        magic_c = sb("magic_c", [128, 1], F32)
        nmagic_c = sb("nmagic_c", [128, 1], F32)
        bias_sb = sb("bias_sb", [128, ECH], F32)
        ccs = sb("ccs", [1, 16], F32)

        R127, RR127, R32K, RSX, RSU, RSG, RST, RSO, RT0, RT1 = range(10)
        ISXc, KGc, SGc, KUc, KTc, KOc, SOc = range(7)
        SW = {"q": 8, "k": 9, "v": 10, "p": 11}

        def c_(t, a):
            return t[:, a:a + 1]

        ps_all = ctx.enter_context(nc.psum_tensor("psall", [128, 8 * NB], F32))

        def bank(i):
            return ps_all[:, i * NB:(i + 1) * NB]

        prow = ps_all[:1, 7 * NB:7 * NB + 128]
        prow2 = ps_all[:1, 6 * NB:6 * NB + 128]
        pbc8 = ps_all[:, 7 * NB:7 * NB + 8]
        pbc2 = ps_all[:, 7 * NB:7 * NB + 2]

        s_in = Sem(ctx, nc, "s_in")
        s_x = Sem(ctx, nc, "s_x")
        s_pw = Sem(ctx, nc, "s_pw")
        s_kp = [Sem(ctx, nc, f"s_k{i}") for i in range(2)]
        s_vp = [Sem(ctx, nc, f"s_v{i}") for i in range(2)]
        s_fp = [Sem(ctx, nc, f"s_f{i}") for i in range(2)]
        s_gn = [Sem(ctx, nc, f"s_g{i}") for i in range(2)]
        s_x1b = Sem(ctx, nc, "s_x1b")
        s_q = Sem(ctx, nc, "s_q")
        s_ps = Sem(ctx, nc, "s_ps")
        s_sq = [Sem(ctx, nc, f"s_sq{i}") for i in range(2)]
        s_sx = [Sem(ctx, nc, f"s_sx{i}") for i in range(2)]
        s_so = [Sem(ctx, nc, f"s_so{i}") for i in range(2)]
        s_or = [Sem(ctx, nc, f"s_or{i}") for i in range(2)]
        s_tr = Sem(ctx, nc, "s_tr")
        s_gd = Sem(ctx, nc, "s_gd")
        s_mm = Sem(ctx, nc, "s_mm")
        s_dv = Sem(ctx, nc, "s_dv")
        s_sc = Sem(ctx, nc, "s_sc")
        s_cc = Sem(ctx, nc, "s_cc")
        s_gp = Sem(ctx, nc, "s_gp")
        s_out = Sem(ctx, nc, "s_out")

        block = ctx.enter_context(nc.Block())
        V = Eng(nc.vector, s_dv)
        C = Eng(nc.scalar, s_sc)
        G = Eng(nc.gpsimd, s_gp)
        T = nc.tensor
        SY = nc.sync

        stA = contextlib.ExitStack()
        xts = sb("xts", [128, ECH * NB], F32, stA)
        xspl = sb("xspl", [128, 2 * ECH * NB], BF16, stA)
        qst = sb("qst", [128, NB], F32, stA)
        qhl = sb("qhl", [128, 4 * NB], BF16, stA)
        awp = sb("awp", [128, ECH * E], BF16, stA)

        stQ = contextlib.ExitStack()
        aw3 = sb("aw3", [128, 3 * ECH * E], BF16, stQ)
        stW = contextlib.ExitStack()
        wstg = sb("wstg", [128, ECH * E], F32, stW)

        awq = {"q": aw3[:, 0:ECH * E], "k": aw3[:, ECH * E:2 * ECH * E],
               "v": aw3[:, 2 * ECH * E:], "p": awp[:]}

        xh_s, xl_s = xspl[:, :ECH * NB], xspl[:, ECH * NB:]

        SY.dma_start(out=ident[:], in_=ident_in).then_inc(s_in.h, 16)
        s_in.n += 16
        SY.dma_start(out=bias_sb[:], in_=bias_in).then_inc(s_in.h, 16)
        s_in.n += 16

        V.memset(ones_b[:], 1.0)
        V.memset(ones_r[:], 1.0)
        V.memset(eps_c[:], 1e-5)
        V.memset(bexp_c[:], EXP_SHIFT)
        V.memset(magic_c[:], MAGIC)
        V.memset(nmagic_c[:], -MAGIC)
        V.memset(colb[:], 0.0)
        V.memset(row[:], 0.0)
        V.memset(ccs[:], 0.0)
        V.memset(scl[:], 0.0)
        V.memset(c_(row, R127), 127.0)
        V.reciprocal(c_(row, RR127), c_(row, R127))
        V.memset(c_(row, R32K), 32767.0)
        V.reciprocal(c_(row, R32K), c_(row, R32K))
        V.memset(c_(colb, 5), 7.0)
        V.reciprocal(c_(colb, 5), c_(colb, 5))

        for j, w in enumerate(("q", "k", "v", "p")):
            SY.wait_ge(s_dv.h, s_dv.n)
            SY.dma_start(out=wstg[:].rearrange("p (c n) -> p c n", n=E),
                         in_=awT[w].transpose([1, 0, 2])).then_inc(s_in.h, 16)
            s_in.n += 16

            nc.vector.wait_ge(s_in.h, s_in.n)
            V.reduce_max(c_(colb, 6), wstg[:], AX, apply_absolute_value=True)
            nc.tensor.wait_ge(s_dv.h, s_dv.n)
            T.transpose(prow, colb[:, 6:7], ident[:]).then_inc(s_mm.h, 1)
            s_mm.n += 1

            nc.vector.wait_ge(s_mm.h, s_mm.n)
            V.reduce_max(c_(row, RT0), prow, AX)
            V.tensor_tensor(c_(row, RT0), c_(row, RT0), c_(row, RR127), OP.mult)
            V.reciprocal(c_(row, RT1), c_(row, RT0))
            nc.tensor.wait_ge(s_dv.h, s_dv.n)
            T.matmul(pbc2, ones_r[:], row[:1, RT0:RT0 + 2],
                     start=True, stop=True).then_inc(s_mm.h, 1)
            s_mm.n += 1

            nc.vector.wait_ge(s_mm.h, s_mm.n)
            V.tensor_copy(c_(scl, SW[w]), pbc2[:, 0:1])
            V.tensor_copy(c_(colb, 7), pbc2[:, 1:2])
            V.tensor_scalar(wstg[:], wstg[:], c_(colb, 7), MAGIC, OP.mult, OP.add)
            V.tensor_scalar(awq[w], wstg[:], MAGIC, None, OP.subtract)

        stW.close()
        SCLS = sb("SCLS", [128, 2304], F32, stQ)
        SCLR = sb("SCLR", [128, 2304], F32, stQ)
        prepw2 = sb("prepw2", [128, 2 * I3], F32, stQ)
        preps2 = sb("preps2", [128, 2 * I3], F32, stQ)
        prepr2 = sb("prepr2", [128, 2 * I3], F32, stQ)
        prepq2 = sb("prepq2", [128, I3], BF16, stQ)
        prepl2 = sb("prepl2", [128, I3], BF16, stQ)
        tbuf2 = sb("tbuf2", [96, 2 * 256], F32, stQ)

        prep_units = ([("scale", "g", c) for c in range(ICH)]
                      + [("scale", "u", c) for c in range(ICH)]
                      + [("scale", "d", c) for c in range(ECH)]
                      + [("tbatch", None, None)]
                      + [("deq", "g", c) for c in range(ECH)]
                      + [("deq", "u", c) for c in range(ECH)]
                      + [("deq", "d", c) for c in range(ICH)])
        scl_off = {"g": 0, "u": 768, "d": 1536}

        prep_slot = [0]
        prep_vmark = {0: 0, 1: 0}
        prep_pw = [Sem(ctx, nc, "s_pw0"), Sem(ctx, nc, "s_pw1")]
        prep_ps = [Sem(ctx, nc, "s_ps0"), Sem(ctx, nc, "s_ps1")]

        def emit_prep_load(kind, which, c):
            sl = prep_slot[0] & 1
            prep_slot[0] += 1
            prepw = prepw2[:, sl * I3:]
            preps = preps2[:, sl * I3:]
            prepr = prepr2[:, sl * I3:]
            spw = prep_pw[sl]
            sps = prep_ps[sl]
            if kind == "scale":
                n_in = E if which in ("g", "u") else I3
                if prep_vmark[sl] > 0:
                    SY.wait_ge(s_dv.h, prep_vmark[sl])
                SY.dma_start(out=prepw[:, :n_in], in_=w_nat[which][c]
                             ).then_inc(spw.h, 16)
                spw.n += 16
            else:
                n_in = I3 if which in ("g", "u") else E
                sd_, srd_ = sc_d[which]
                if prep_vmark[sl] > 0:
                    SY.wait_ge(s_dv.h, prep_vmark[sl])
                if spw.n > 0:
                    SY.wait_ge(spw.h, spw.n)
                SY.dma_start(out=prepw[:, :n_in], in_=wT_in[which][c]
                             ).then_inc(spw.h, 16)
                sin = sd_[c * 4:c * 4 + 4].unsqueeze(1).broadcast_to([4, 32, n_in])
                rin = srd_[c * 4:c * 4 + 4].unsqueeze(1).broadcast_to([4, 32, n_in])
                SY.dma_start(out=preps[:, :n_in], in_=sin).then_inc(spw.h, 16)
                SY.dma_start(out=prepr[:, :n_in], in_=rin).then_inc(spw.h, 16)
                spw.n += 48
            return sl

        def emit_prep_compute(kind, which, c, sl):
            prepw = prepw2[:, sl * I3:]
            preps = preps2[:, sl * I3:]
            prepr = prepr2[:, sl * I3:]
            prepq = prepq2[:, 0:]
            prepl = prepl2[:, 0:]
            spw = prep_pw[sl]
            sps = prep_ps[sl]
            if kind == "scale":
                n_in = E if which in ("g", "u") else I3
                nb_ = n_in // 32
                cols = scl_off[which] + c * nb_
                nc.vector.wait_ge(spw.h, spw.n)
                V.tensor_reduce(SCLS[:, cols:cols + nb_],
                                prepw[:, :n_in].rearrange("p (b c) -> p b c", c=32),
                                AX, OP.max, apply_absolute_value=True)
                V.tensor_scalar(SCLS[:, cols:cols + nb_], SCLS[:, cols:cols + nb_],
                                c_(colb, 5), 1e-12, OP.mult, OP.max)
                V.reciprocal(SCLR[:, cols:cols + nb_], SCLS[:, cols:cols + nb_])
                prep_vmark[sl] = s_dv.n
            else:
                n_in = I3 if which in ("g", "u") else E
                nc.vector.wait_ge(spw.h, spw.n)
                if sps.n > 0:
                    nc.vector.wait_ge(sps.h, sps.n)
                V.tensor_tensor(prepw[:, :n_in], prepw[:, :n_in],
                                prepr[:, :n_in], OP.mult)
                V.tensor_scalar(prepw[:, :n_in], prepw[:, :n_in], MAGIC, MAGIC,
                                OP.add, OP.subtract)
                V.tensor_tensor(prepw[:, :n_in], prepw[:, :n_in],
                                preps[:, :n_in], OP.mult)
                osps = prep_ps[1 - sl]
                if osps.n > 0:
                    nc.vector.wait_ge(osps.h, osps.n)
                V.tensor_copy(prepq[:, :n_in], prepw[:, :n_in])
                V.tensor_tensor(prepl[:, :n_in], prepw[:, :n_in],
                                prepq[:, :n_in], OP.subtract)
                prep_vmark[sl] = s_dv.n
                nc.scalar.wait_ge(s_dv.h, s_dv.n)
                for half in range(2):
                    src_t = (prepq if half == 0 else prepl)[:, :n_in]
                    C.dma_start(out=whl_d[(which, half)][c],
                                in_=src_t).then_inc(sps.h, 16)
                sps.n += 32

        def emit_tbatch():
                nc.tensor.wait_ge(s_dv.h, s_dv.n)
                jobs = []
                for wh in ("g", "u", "d"):
                    nchunks = ICH if wh in ("g", "u") else ECH
                    nb2 = 32 if wh in ("g", "u") else 96
                    for cc2 in range(nchunks):
                        for half2, (srcb, dstb) in enumerate(
                                ((SCLS, sc_d[wh][0]), (SCLR, sc_d[wh][1]))):
                            jobs.append((wh, cc2, nb2, srcb, dstb))
                tsems = [prep_ps[0], prep_ps[1], prep_pw[0], prep_pw[1]]
                vmarks = []
                smarks = []
                for ji, (wh, cc2, nb2, srcb, dstb) in enumerate(jobs):
                    cols = scl_off[wh] + cc2 * nb2
                    pslot = ji % 2
                    preg = ps_all[:nb2, (6 + pslot) * NB:(6 + pslot) * NB + 128]
                    if len(vmarks) >= 2:
                        nc.tensor.wait_ge(s_dv.h, vmarks[-2])
                    T.transpose(preg, srcb[:, cols:cols + nb2],
                                ident[:]).then_inc(s_mm.h, 1)
                    s_mm.n += 1
                    tslot = ji % 4
                    ts_ = tsems[tslot]
                    nc.vector.wait_ge(s_mm.h, s_mm.n)
                    if len(smarks) >= 4:
                        nc.vector.wait_ge(ts_.h, smarks[-4])
                    V.tensor_copy(tbuf2[:nb2, tslot * 128:(tslot + 1) * 128], preg)
                    vmarks.append(s_dv.n)
                    nc.scalar.wait_ge(s_dv.h, s_dv.n)
                    C.dma_start(out=dstb[:nb2, cc2 * 128:(cc2 + 1) * 128],
                                in_=tbuf2[:nb2, tslot * 128:(tslot + 1) * 128]
                                ).then_inc(ts_.h, 16)
                    ts_.n += 16
                    smarks.append(ts_.n)

        prep_iter = iter(prep_units)
        prep_pending = [None]

        def flush_prep():
            if prep_pending[0] is not None:
                emit_prep_compute(*prep_pending[0])
                prep_pending[0] = None

        def take_prep(kn):
            for _ in range(kn):
                u = next(prep_iter, None)
                if u is None:
                    continue
                if u[0] == "tbatch":
                    flush_prep()
                    emit_tbatch()
                else:
                    sl = emit_prep_load(*u)
                    flush_prep()
                    prep_pending[0] = (u[0], u[1], u[2], sl)

        qkv_evac_marks = []
        qkv_spill_marks = []

        for tb in range(TB):
            SY.wait_ge(s_dv.h, s_dv.n)
            SY.dma_start(out=xts[:].rearrange("p (c n) -> p c n", n=NB),
                         in_=xT_in[:, :, tb * NB:(tb + 1) * NB].transpose([1, 0, 2])
                         ).then_inc(s_x.h, 16)
            s_x.n += 16

            nc.vector.wait_ge(s_x.h, s_x.n)
            nc.vector.wait_ge(s_mm.h, s_mm.n)
            V.tensor_copy(xh_s, xts[:])
            V.tensor_tensor(xl_s, xts[:], xh_s, OP.subtract)
            sd_split = s_dv.n

            tiles = [("q", oc, None) for oc in range(ECH)] \
                + [("k", oc, None) for oc in range(ECH)] \
                + [("v", t_, h_) for t_ in range(NB // 128) for h_ in range(2)]
            for idx, (w, a, b_) in enumerate(tiles):
                pb = idx % 4
                nc.tensor.wait_ge(s_dv.h, sd_split)
                if len(qkv_evac_marks) >= 4:
                    nc.tensor.wait_ge(s_dv.h, qkv_evac_marks[-4])
                if w in ("q", "k"):
                    for ec in range(ECH):
                        lw = awq[w][:, ec * E + a * 128: ec * E + a * 128 + 128]
                        T.matmul(bank(pb), lw, xh_s[:, ec * NB:(ec + 1) * NB],
                                 start=(ec == 0), stop=False)
                        last = T.matmul(bank(pb), lw, xl_s[:, ec * NB:(ec + 1) * NB],
                                        start=False, stop=(ec == ECH - 1))
                else:
                    for ec in range(ECH):
                        lwh = xh_s[:, ec * NB + a * 128: ec * NB + a * 128 + 128]
                        lwl = xl_s[:, ec * NB + a * 128: ec * NB + a * 128 + 128]
                        rh = awq["v"][:, ec * E + b_ * NB: ec * E + (b_ + 1) * NB]
                        T.matmul(bank(pb), lwh, rh, start=(ec == 0), stop=False)
                        last = T.matmul(bank(pb), lwl, rh, start=False,
                                        stop=(ec == ECH - 1))
                last.then_inc(s_mm.h, 1)
                s_mm.n += 1

                nc.vector.wait_ge(s_mm.h, s_mm.n)
                sqs = s_sq[pb % 2]
                if sqs.n > 0:
                    nc.vector.wait_ge(sqs.h, sqs.n)
                st_ = qst[:, 0:NB]
                hi = qhl[:, (pb % 2) * 2 * NB:(pb % 2) * 2 * NB + NB]
                lo = qhl[:, (pb % 2) * 2 * NB + NB:(pb % 2) * 2 * NB + 2 * NB]
                V.tensor_scalar(st_, bank(pb), c_(scl, SW[w]), None, OP.mult)
                V.tensor_copy(hi, st_)
                V.tensor_tensor(lo, st_, hi, OP.subtract)
                qkv_evac_marks.append(s_dv.n)

                nc.scalar.wait_ge(s_dv.h, s_dv.n)
                if w == "q":
                    C.dma_start(out=qh_d[tb, a], in_=hi).then_inc(sqs.h, 16)
                    C.dma_start(out=ql_d[tb, a], in_=lo).then_inc(sqs.h, 16)
                elif w == "k":
                    KPT = NB // 128
                    C.dma_start(out=kh_d[tb * KPT:(tb + 1) * KPT, a]
                                .transpose([1, 0, 2]),
                                in_=hi.rearrange("p (t r) -> p t r", r=128)
                                ).then_inc(sqs.h, 16)
                    C.dma_start(out=kl_d[tb * KPT:(tb + 1) * KPT, a]
                                .transpose([1, 0, 2]),
                                in_=lo.rearrange("p (t r) -> p t r", r=128)
                                ).then_inc(sqs.h, 16)
                else:
                    kt_ = tb * (NB // 128) + a
                    C.dma_start(out=vh_d[kt_, :, b_ * NB:(b_ + 1) * NB],
                                in_=hi).then_inc(sqs.h, 16)
                    C.dma_start(out=vl_d[kt_, :, b_ * NB:(b_ + 1) * NB],
                                in_=lo).then_inc(sqs.h, 16)
                sqs.n += 32
                qkv_spill_marks.append((pb % 2, sqs.n))
                if idx % 2 == 1:
                    take_prep(1)

        take_prep(len(prep_units))
        flush_prep()
        stQ.close()

        stT = contextlib.ExitStack()
        SS = sb("SS", [128, 2 * KT * NB], BF16, stT)
        QT = sb("QT", [128, 2 * ECH * NB], BF16, stT)
        KV = sb("KV", [128, 2 * 2 * ECH * 128], BF16, stT)
        VV = sb("VV", [128, 2 * 2 * E], BF16, stT)
        S32 = sb("S32", [128, 2 * NB], F32, stT)
        SUMR = sb("SUMR", [128, NB], F32, stT)
        X1P = sb("X1P", [128, ECH * NB], F32, stT)
        XB = sb("XB", [128, 2 * ECH * NB], BF16, stT)
        MEAN = sb("MEAN", [128, NB], F32, stT)
        VARB = sb("VARB", [128, NB], F32, stT)
        RSTD = sb("RSTD", [128, NB], F32, stT)
        X1O = sb("X1O", [128, 2 * NB], F32, stT)

        ah_s, al_s = xh_s, xl_s

        def sh_t(kt):
            return SS[:, kt * NB:(kt + 1) * NB]

        def sl_t(kt):
            return SS[:, KT * NB + kt * NB:KT * NB + (kt + 1) * NB]

        SY.wait_ge(s_sq[0].h, s_sq[0].n)
        SY.wait_ge(s_sq[1].h, s_sq[1].n)
        SY.wait_ge(s_mm.h, s_mm.n)
        SY.wait_ge(s_dv.h, s_dv.n)
        SY.wait_ge(s_sc.h, s_sc.n)
        SY.wait_ge(s_gp.h, s_gp.n)
        SY.wait_ge(prep_ps[0].h, prep_ps[0].n)
        SY.wait_ge(prep_ps[1].h, prep_ps[1].n)
        SY.wait_ge(prep_pw[0].h, prep_pw[0].n)
        SY.wait_ge(prep_pw[1].h, prep_pw[1].n)
        prev_evac = 0
        for qb in range(TB):
            SY.wait_ge(s_mm.h, s_mm.n)
            SY.dma_start(out=QT[:, :ECH * NB].rearrange("p (c n) -> p c n", n=NB),
                         in_=qh_d[qb].transpose([1, 0, 2])).then_inc(s_q.h, 16)
            SY.dma_start(out=QT[:, ECH * NB:].rearrange("p (c n) -> p c n", n=NB),
                         in_=ql_d[qb].transpose([1, 0, 2])).then_inc(s_q.h, 16)
            s_q.n += 32
            q_ready = s_q.n

            base_kmm = s_mm.n
            k_marks = []
            exp_marks = []
            sl_marks = []
            for kt in range(KT):
                dbk = kt % 2
                if kt >= 2:
                    SY.wait_ge(s_mm.h, base_kmm + kt - 1)
                skp = s_kp[dbk]
                SY.dma_start(
                    out=KV[:, dbk * 2 * ECH * 128: dbk * 2 * ECH * 128 + ECH * 128]
                    .rearrange("p (c n) -> p c n", n=128),
                    in_=kh_d[kt].transpose([1, 0, 2])).then_inc(skp.h, 16)
                SY.dma_start(
                    out=KV[:, dbk * 2 * ECH * 128 + ECH * 128:
                           (dbk + 1) * 2 * ECH * 128]
                    .rearrange("p (c n) -> p c n", n=128),
                    in_=kl_d[kt].transpose([1, 0, 2])).then_inc(skp.h, 16)
                skp.n += 32
                k_marks.append((dbk, skp.n))

            for kt in range(KT):
                dbk = kt % 2
                sbk = 6 + (kt % 2)
                nc.tensor.wait_ge(s_q.h, q_ready)
                nc.tensor.wait_ge(s_kp[k_marks[kt][0]].h, k_marks[kt][1])
                if kt < 2 and prev_evac > 0:
                    nc.tensor.wait_ge(s_dv.h, prev_evac)
                if len(exp_marks) >= 2:
                    nc.tensor.wait_ge(s_sc.h, exp_marks[-2])
                kh0 = KV[:, dbk * 2 * ECH * 128:]
                for ec in range(ECH):
                    lwh = kh0[:, ec * 128:(ec + 1) * 128]
                    lwl = kh0[:, ECH * 128 + ec * 128: ECH * 128 + (ec + 1) * 128]
                    T.matmul(bank(sbk), lwh, QT[:, ec * NB:(ec + 1) * NB],
                             start=(ec == 0), stop=False)
                    T.matmul(bank(sbk), lwh,
                             QT[:, ECH * NB + ec * NB: ECH * NB + (ec + 1) * NB],
                             start=False, stop=False)
                    last = T.matmul(bank(sbk), lwl, QT[:, ec * NB:(ec + 1) * NB],
                                    start=False, stop=(ec == ECH - 1))
                last.then_inc(s_mm.h, 1)
                s_mm.n += 1

                if kt >= 1:
                    nc.tensor.wait_ge(s_dv.h, s_dv.n)
                    T.matmul(bank(5), ones_b[:], sh_t(kt - 1),
                             start=(kt == 1), stop=False)
                    T.matmul(bank(5), ones_b[:], sl_t(kt - 1),
                             start=False, stop=False)

                nc.scalar.wait_ge(s_mm.h, s_mm.n)
                if len(sl_marks) >= 2:
                    nc.scalar.wait_ge(s_dv.h, sl_marks[-2])
                C.activation(S32[:, dbk * NB:(dbk + 1) * NB], bank(sbk), ACT.Exp,
                             bias=bexp_c[:], scale=0.125)
                C.activation(sh_t(kt), bank(sbk), ACT.Exp,
                             bias=bexp_c[:], scale=0.125)
                exp_marks.append(s_sc.n)

                nc.vector.wait_ge(s_sc.h, s_sc.n)
                V.tensor_tensor(sl_t(kt), S32[:, dbk * NB:(dbk + 1) * NB],
                                sh_t(kt), OP.subtract)
                sl_marks.append(s_dv.n)

            nc.tensor.wait_ge(s_dv.h, s_dv.n)
            T.matmul(bank(5), ones_b[:], sh_t(KT - 1),
                     start=(KT == 1), stop=False)
            T.matmul(bank(5), ones_b[:], sl_t(KT - 1),
                     start=False, stop=True).then_inc(s_mm.h, 1)
            s_mm.n += 1

            nc.vector.wait_ge(s_mm.h, s_mm.n)
            V.reciprocal(SUMR[:], bank(5))
            sumr_done = s_dv.n

            base_bmm = s_mm.n
            for kt in range(KT):
                dbk = kt % 2
                if kt >= 2:
                    SY.wait_ge(s_mm.h, base_bmm + kt - 1)
                svp = s_vp[dbk]
                SY.dma_start(out=VV[:, dbk * 2 * E: dbk * 2 * E + E],
                             in_=vh_d[kt]).then_inc(svp.h, 16)
                SY.dma_start(out=VV[:, dbk * 2 * E + E: (dbk + 1) * 2 * E],
                             in_=vl_d[kt]).then_inc(svp.h, 16)
                svp.n += 32

                nc.tensor.wait_ge(svp.h, svp.n)
                if kt == 0:
                    nc.tensor.wait_ge(s_dv.h, sumr_done)
                    nc.tensor.wait_ge(s_sc.h, s_sc.n)
                vh0 = VV[:, dbk * 2 * E:]
                for ec in range(ECH):
                    lwh = vh0[:, ec * 128:(ec + 1) * 128]
                    lwl = vh0[:, E + ec * 128: E + (ec + 1) * 128]
                    T.matmul(bank(ec), lwh, sh_t(kt), start=(kt == 0), stop=False)
                    T.matmul(bank(ec), lwh, sl_t(kt), start=False, stop=False)
                    last = T.matmul(bank(ec), lwl, sh_t(kt), start=False,
                                    stop=(kt == KT - 1))
                last.then_inc(s_mm.h, 1)
                s_mm.n += 1

            nc.vector.wait_ge(s_mm.h, s_mm.n)
            for ec in range(ECH):
                st_ = qst[:, 0:NB]
                V.tensor_tensor(st_, bank(ec), SUMR[:], OP.mult)
                V.tensor_copy(ah_s[:, ec * NB:(ec + 1) * NB], st_)
                V.tensor_tensor(al_s[:, ec * NB:(ec + 1) * NB], st_,
                                ah_s[:, ec * NB:(ec + 1) * NB], OP.subtract)
            evac_done = s_dv.n
            prev_evac = evac_done

            SY.wait_ge(s_dv.h, evac_done)
            SY.dma_start(out=xts[:].rearrange("p (c n) -> p c n", n=NB),
                         in_=xT_in[:, :, qb * NB:(qb + 1) * NB].transpose([1, 0, 2])
                         ).then_inc(s_x.h, 16)
            s_x.n += 16
            xts_ready = s_x.n

            proj_evac_marks = []
            for oc in range(ECH):
                pb = oc % 2
                nc.tensor.wait_ge(s_dv.h, evac_done)
                if len(proj_evac_marks) >= 2:
                    nc.tensor.wait_ge(s_dv.h, proj_evac_marks[-2])
                for ec in range(ECH):
                    lw = awq["p"][:, ec * E + oc * 128: ec * E + oc * 128 + 128]
                    T.matmul(bank(pb), lw, ah_s[:, ec * NB:(ec + 1) * NB],
                             start=(ec == 0), stop=False)
                    last = T.matmul(bank(pb), lw, al_s[:, ec * NB:(ec + 1) * NB],
                                    start=False, stop=(ec == ECH - 1))
                last.then_inc(s_mm.h, 1)
                s_mm.n += 1

                nc.vector.wait_ge(s_mm.h, s_mm.n)
                if oc == 0:
                    nc.vector.wait_ge(s_x.h, xts_ready)
                xp = X1P[:, oc * NB:(oc + 1) * NB]
                V.tensor_scalar(xp, bank(pb), c_(scl, SW["p"]),
                                c_(bias_sb, oc), OP.mult, OP.add)
                V.tensor_tensor(xp, xp, xts[:, oc * NB:(oc + 1) * NB], OP.add)
                V.tensor_copy(XB[:, oc * NB:(oc + 1) * NB], xp)
                proj_evac_marks.append(s_dv.n)

                nc.scalar.wait_ge(s_dv.h, s_dv.n)
                C.activation(XB[:, ECH * NB + oc * NB: ECH * NB + (oc + 1) * NB],
                             XB[:, oc * NB:(oc + 1) * NB], ACT.Square)

                nc.tensor.wait_ge(s_sc.h, s_sc.n)
                T.matmul(bank(2), ones_b[:], XB[:, oc * NB:(oc + 1) * NB],
                         start=(oc == 0), stop=(oc == ECH - 1))
                mm2 = T.matmul(bank(3), ones_b[:],
                               XB[:, ECH * NB + oc * NB: ECH * NB + (oc + 1) * NB],
                               start=(oc == 0), stop=(oc == ECH - 1))
                if oc == ECH - 1:
                    mm2.then_inc(s_mm.h, 1)
                    s_mm.n += 1

            nc.vector.wait_ge(s_mm.h, s_mm.n)
            V.tensor_scalar(MEAN[:], bank(2), 1.0 / E, None, OP.mult)
            V.tensor_scalar(VARB[:], bank(3), 1.0 / E, None, OP.mult)
            V.tensor_tensor(RSTD[:], MEAN[:], MEAN[:], OP.mult)
            V.tensor_tensor(VARB[:], VARB[:], RSTD[:], OP.subtract)
            nc.scalar.wait_ge(s_dv.h, s_dv.n)
            C.activation(RSTD[:], VARB[:], ACT.Sqrt, bias=eps_c[:])
            nc.vector.wait_ge(s_sc.h, s_sc.n)
            V.reciprocal(RSTD[:], RSTD[:])

            for oc in range(ECH):
                sxs = s_sx[oc % 2]
                if sxs.n > 0:
                    nc.vector.wait_ge(sxs.h, sxs.n)
                xo = X1O[:, (oc % 2) * NB:((oc % 2) + 1) * NB]
                xp = X1P[:, oc * NB:(oc + 1) * NB]
                V.tensor_tensor(xo, xp, MEAN[:], OP.subtract)
                V.tensor_tensor(xo, xo, RSTD[:], OP.mult)
                V.reduce_max(c_(colb, 6), xo, AX, apply_absolute_value=True)
                V.tensor_tensor(c_(colb, 0), c_(colb, 0), c_(colb, 6), OP.max)

                nc.scalar.wait_ge(s_dv.h, s_dv.n)
                C.dma_start(out=x1_d[oc, :, qb * NB:(qb + 1) * NB],
                            in_=xo).then_inc(sxs.h, 16)
                sxs.n += 16

        stT.close()
        stA.close()

        def emit_collective(idx, cols, finish_fn):
            nc.tensor.wait_ge(s_dv.h, s_dv.n)
            T.transpose(prow, colb[:, cols[0]:cols[0] + 1], ident[:]
                        ).then_inc(s_mm.h, 1)
            s_mm.n += 1
            if len(cols) > 1:
                T.transpose(prow2, colb[:, cols[1]:cols[1] + 1], ident[:]
                            ).then_inc(s_mm.h, 1)
                s_mm.n += 1
            nc.vector.wait_ge(s_mm.h, s_mm.n)
            V.reduce_max(ccs[:, 1:2], prow, AX)
            if len(cols) > 1:
                V.reduce_max(ccs[:, 2:3], prow2, AX)
            nc.gpsimd.wait_ge(s_dv.h, s_dv.n)
            G.dma_start(out=cc_in[idx], in_=ccs[:]).then_inc(s_gd.h, 16)
            s_gd.n += 16
            nc.gpsimd.wait_ge(s_gd.h, s_gd.n)
            G.collective_compute(
                "AllReduce", OP.max, replica_groups=[list(range(8))],
                ins=[cc_in[idx].opt()], outs=[cc_out[idx].opt()]
            ).then_inc(s_cc.h, 1)
            s_cc.n += 1
            nc.gpsimd.wait_ge(s_cc.h, s_cc.n)
            G.dma_start(out=ccs[:], in_=cc_out[idx]).then_inc(s_gd.h, 16)
            s_gd.n += 16
            nc.vector.wait_ge(s_gd.h, s_gd.n)
            finish_fn()
            nc.tensor.wait_ge(s_dv.h, s_dv.n)
            T.matmul(pbc8, ones_r[:], row[:1, 16:24], start=True, stop=True
                     ).then_inc(s_mm.h, 1)
            s_mm.n += 1
            nc.vector.wait_ge(s_mm.h, s_mm.n)
            nc.vector.wait_ge(s_sc.h, s_sc.n)
            V.tensor_copy(scl[:, 0:8], pbc8)
            return s_dv.n

        FSTI = sb("FSTI", [128, 2 * SL], F32, ctx)
        FOUT = sb("FOUT", [128, 2 * SL], F32, ctx)
        WCH = sb("WCH", [128, 2 * 2 * ECH * 128], BF16, ctx)
        stN = contextlib.ExitStack()
        NL = sb("NL", [128, 2 * ECH * S], BF16, stN)
        nh_s = NL[:, :ECH * S]
        nl_s = NL[:, ECH * S:]

        def fin1():
            V.tensor_scalar(c_(row, RSX), ccs[:, 1:2], c_(row, R32K), 1e-12,
                            OP.mult, OP.max)
            V.reciprocal(row[:1, 16:17], c_(row, RSX))
        emit_collective(0, (0,), fin1)

        SY.wait_ge(s_sx[0].h, s_sx[0].n)
        SY.wait_ge(s_sx[1].h, s_sx[1].n)
        p3_marks = []
        for tb in range(TB):
            dbk = tb % 2
            if len(p3_marks) >= 2:
                SY.wait_ge(s_dv.h, p3_marks[-2][0])
                SY.wait_ge(s_gp.h, p3_marks[-2][1])
            sfp = s_fp[dbk]
            SY.dma_start(out=FSTI[:, dbk * SL: dbk * SL + ECH * NB]
                         .rearrange("p (c n) -> p c n", n=NB),
                         in_=x1_d[:, :, tb * NB:(tb + 1) * NB].transpose([1, 0, 2])
                         ).then_inc(sfp.h, 16)
            sfp.n += 16

            nc.vector.wait_ge(sfp.h, sfp.n)
            src = FSTI[:, dbk * SL: dbk * SL + ECH * NB]
            st_ = FOUT[:, dbk * SL: dbk * SL + ECH * NB]
            st3 = st_.rearrange("p (c n) -> p c n", n=NB)
            nh_v = nh_s.rearrange("p (c s) -> p c s", s=S)[:, :, tb * NB:(tb + 1) * NB]
            nl_v = nl_s.rearrange("p (c s) -> p c s", s=S)[:, :, tb * NB:(tb + 1) * NB]
            V.tensor_scalar(st_, src, c_(scl, ISXc), MAGIC, OP.mult, OP.add)
            V.tensor_scalar(st_, st_, MAGIC, None, OP.subtract)
            V.tensor_copy(nh_v, st3)
            V.tensor_tensor(nl_v, st3, nh_v, OP.subtract)
            p3_marks.append((s_dv.n, s_gp.n))
        p3_done = s_dv.n

        SY.wait_ge(prep_ps[0].h, prep_ps[0].n)
        SY.wait_ge(prep_ps[1].h, prep_ps[1].n)
        wch_last = {0: 0, 1: 0}
        ev_marks = []

        def emit_w_chunk_load(which, ich):
            dbw = ich % 2
            if wch_last[dbw] > 0:
                SY.wait_ge(s_mm.h, wch_last[dbw])
            skp = s_kp[dbw]
            for half in range(2):
                src = whl_d[(which, half)][:, :, ich * 128:(ich + 1) * 128]
                SY.dma_start(
                    out=WCH[:, dbw * 2 * ECH * 128 + half * ECH * 128:
                           dbw * 2 * ECH * 128 + (half + 1) * ECH * 128]
                    .rearrange("p (c n) -> p c n", n=128),
                    in_=src.transpose([1, 0, 2])).then_inc(skp.h, 16)
            skp.n += 32
            return dbw, skp

        def emit_wmm_tiles(ich, dbw):
            for tb in range(TB):
                pb = (ich * TB + tb) % 4
                nc.tensor.wait_ge(s_dv.h, p3_marks[tb][0])
                if len(ev_marks) >= 4:
                    nc.tensor.wait_ge(s_sc.h, ev_marks[-4])
                w0 = WCH[:, dbw * 2 * ECH * 128:]
                for ec in range(ECH):
                    lwh = w0[:, ec * 128:(ec + 1) * 128]
                    lwl = w0[:, ECH * 128 + ec * 128: ECH * 128 + (ec + 1) * 128]
                    nh_v = nh_s[:, ec * S + tb * NB: ec * S + (tb + 1) * NB]
                    nl_v = nl_s[:, ec * S + tb * NB: ec * S + (tb + 1) * NB]
                    T.matmul(bank(pb), lwh, nh_v, start=(ec == 0), stop=False)
                    T.matmul(bank(pb), lwh, nl_v, start=False, stop=False)
                    last = T.matmul(bank(pb), lwl, nh_v, start=False,
                                    stop=(ec == ECH - 1))
                last.then_inc(s_mm.h, 1)
                s_mm.n += 1
                wch_last[dbw] = s_mm.n
                yield tb, pb

        wload_pend = {0: emit_w_chunk_load("g", 0)}
        for ich in range(ICH):
            dbw, skp = wload_pend.pop(ich)
            if ich + 1 < ICH:
                wload_pend[ich + 1] = emit_w_chunk_load("g", ich + 1)
            nc.tensor.wait_ge(skp.h, skp.n)
            sos = s_so[ich % 2]
            for tb, pb in emit_wmm_tiles(ich, dbw):
                nc.scalar.wait_ge(s_mm.h, s_mm.n)
                if tb == 0 and sos.n > 0:
                    nc.scalar.wait_ge(sos.h, sos.n)
                off = (ich % 2) * SL
                C.activation(FOUT[:, off + tb * NB: off + (tb + 1) * NB],
                             bank(pb), ACT.Copy)
                ev_marks.append(s_sc.n)
                nc.vector.wait_ge(s_sc.h, s_sc.n)
                V.reduce_max(c_(colb, 6),
                             FOUT[:, off + tb * NB: off + (tb + 1) * NB],
                             AX, apply_absolute_value=True)
                V.tensor_tensor(c_(colb, 2), c_(colb, 2), c_(colb, 6), OP.max)
            nc.scalar.wait_ge(s_dv.h, s_dv.n)
            C.dma_start(out=gtr_d[ich],
                        in_=FOUT[:, (ich % 2) * SL:(ich % 2) * SL + TB * NB]
                        ).then_inc(sos.h, 16)
            sos.n += 16

        def fin2g():
            V.tensor_tensor(c_(row, RT0), ccs[:, 1:2], c_(row, RSX), OP.mult)
            V.tensor_scalar(c_(row, RSG), c_(row, RT0), c_(row, R32K), 1e-12,
                            OP.mult, OP.max)
            V.reciprocal(c_(row, RT0), c_(row, RSG))
            V.tensor_tensor(row[:1, 17:18], c_(row, RSX), c_(row, RT0), OP.mult)
            V.tensor_copy(row[:1, 18:19], c_(row, RSG))
        emit_collective(1, (2,), fin2g)

        SY.wait_ge(s_so[0].h, s_so[0].n)
        SY.wait_ge(s_so[1].h, s_so[1].n)
        f2_spill_mark = [0]
        f4_state = {"vmark": 0, "spill": 0}

        def f4_units():
            for ich in range(ICH):
                src = FSTI[:, 0:S]
                gq = FSTI[:, SL:SL + S] if ich % 2 == 0 else FOUT[:, SL:SL + S]
                if f4_state["vmark"] > 0:
                    SY.wait_ge(s_dv.h, f4_state["vmark"])
                if f4_state["spill"] > 0:
                    SY.wait_ge(s_tr.h, f4_state["spill"])
                SY.dma_start(out=src, in_=gtr_d[ich]).then_inc(s_gn[0].h, 16)
                s_gn[0].n += 16

                nc.vector.wait_ge(s_gn[0].h, s_gn[0].n)
                V.tensor_scalar(gq, src, c_(scl, KGc), MAGIC, OP.mult, OP.add)
                V.tensor_scalar(gq, gq, MAGIC, None, OP.subtract)
                rmark = s_dv.n
                yield
                nc.scalar.wait_ge(s_dv.h, rmark)
                C.activation(src, gq, ACT.Sigmoid, scale=c_(scl, SGc))
                yield
                C.activation(src, src, ACT.Identity, bias=magic_c[:],
                             scale=65536.0)
                yield
                C.activation(src, src, ACT.Identity, bias=nmagic_c[:])
                smark = s_sc.n
                yield
                nc.vector.wait_ge(s_sc.h, smark)
                V.tensor_scalar(src, src, 65535.0, None, OP.min)
                V.tensor_tensor(src, src, gq, OP.mult)
                V.reduce_max(c_(colb, 6), src, AX, apply_absolute_value=True)
                V.tensor_tensor(c_(colb, 3), c_(colb, 3), c_(colb, 6), OP.max)
                f4_state["vmark"] = s_dv.n
                yield
                nc.gpsimd.wait_ge(s_dv.h, f4_state["vmark"])
                G.dma_start(out=tr_d[ich], in_=src).then_inc(s_tr.h, 16)
                s_tr.n += 16
                f4_state["spill"] = s_tr.n
                yield
        f4_iter = f4_units()

        wload_pend = {0: emit_w_chunk_load("u", 0)}
        for ich in range(ICH):
            dbw, skp = wload_pend.pop(ich)
            if ich + 1 < ICH:
                wload_pend[ich + 1] = emit_w_chunk_load("u", ich + 1)
            nc.tensor.wait_ge(skp.h, skp.n)
            for tb, pb in emit_wmm_tiles(ich, dbw):
                nc.scalar.wait_ge(s_mm.h, s_mm.n)
                if tb == 0 and f2_spill_mark[0] > 0:
                    nc.scalar.wait_ge(s_so[0].h, f2_spill_mark[0])
                C.activation(FOUT[:, tb * NB:(tb + 1) * NB], bank(pb), ACT.Copy)
                ev_marks.append(s_sc.n)
                nc.vector.wait_ge(s_sc.h, s_sc.n)
                V.reduce_max(c_(colb, 6), FOUT[:, tb * NB:(tb + 1) * NB],
                             AX, apply_absolute_value=True)
                V.tensor_tensor(c_(colb, 1), c_(colb, 1), c_(colb, 6), OP.max)
                if tb != 1 and tb != 2:
                    next(f4_iter, None)
            nc.scalar.wait_ge(s_dv.h, s_dv.n)
            C.dma_start(out=upr_d[ich], in_=FOUT[:, 0:TB * NB]
                        ).then_inc(s_so[0].h, 16)
            s_so[0].n += 16
            f2_spill_mark[0] = s_so[0].n
        for _ in f4_iter:
            pass

        def fin23():
            V.tensor_tensor(c_(row, RT0), ccs[:, 1:2], c_(row, RSX), OP.mult)
            V.tensor_scalar(c_(row, RSU), c_(row, RT0), c_(row, R32K), 1e-12,
                            OP.mult, OP.max)
            V.reciprocal(c_(row, RT0), c_(row, RSU))
            V.tensor_tensor(row[:1, 19:20], c_(row, RSX), c_(row, RT0), OP.mult)
            V.tensor_scalar(c_(row, RT0), ccs[:, 2:3], 1.0 / 65536.0, None,
                            OP.mult)
            V.tensor_tensor(c_(row, RT0), c_(row, RT0), c_(row, RSG), OP.mult)
            V.tensor_scalar(c_(row, RT0), c_(row, RT0), c_(row, R32K), None,
                            OP.mult)
            V.tensor_scalar(c_(row, RST), c_(row, RT0), 1e-12, None, OP.max)
            V.reciprocal(c_(row, RT0), c_(row, RST))
            V.tensor_scalar(c_(row, RT1), c_(row, RSG), 1.0 / 65536.0, None,
                            OP.mult)
            V.tensor_tensor(row[:1, 20:21], c_(row, RT1), c_(row, RT0), OP.mult)
        bc23 = emit_collective(2, (1, 3), fin23)

        stN.close()

        stF4 = contextlib.ExitStack()
        GN = sb("GN", [128, 2 * SL], F32, stF4)

        SY.wait_ge(s_so[0].h, s_so[0].n)
        SY.wait_ge(s_tr.h, s_tr.n)
        f5_dv = []
        f5_sc = []
        for ich in range(ICH):
            dbk = ich % 2
            if len(f5_dv) >= 2:
                SY.wait_ge(s_dv.h, f5_dv[-2])
            sfp = s_fp[dbk]
            sgn = s_gn[dbk]
            SY.dma_start(out=FSTI[:, dbk * SL: dbk * SL + S],
                         in_=upr_d[ich]).then_inc(sfp.h, 16)
            SY.dma_start(out=GN[:, dbk * SL: dbk * SL + S],
                         in_=tr_d[ich]).then_inc(sgn.h, 16)
            sfp.n += 16
            sgn.n += 16

            up = FSTI[:, dbk * SL: dbk * SL + S]
            tr = GN[:, dbk * SL: dbk * SL + S]
            ot = FOUT[:, dbk * SL: dbk * SL + S]
            nc.scalar.wait_ge(sfp.h, sfp.n)
            if ich == 0:
                nc.scalar.wait_ge(s_dv.h, bc23)
            C.activation(up, up, ACT.Identity, bias=magic_c[:], scale=c_(scl, KUc))
            C.activation(up, up, ACT.Identity, bias=nmagic_c[:])
            f5_sc.append(s_sc.n)

            nc.vector.wait_ge(sgn.h, sgn.n)
            sos = s_or[dbk]
            if ich == 0:
                nc.vector.wait_ge(s_so[0].h, s_so[0].n)
            if sos.n > 0:
                nc.vector.wait_ge(sos.h, sos.n)
            V.tensor_scalar(tr, tr, c_(scl, KTc), MAGIC, OP.mult, OP.add)
            V.tensor_scalar(tr, tr, MAGIC, None, OP.subtract)
            nc.vector.wait_ge(s_sc.h, f5_sc[-1])
            V.tensor_tensor(ot, tr, up, OP.mult)
            V.reduce_max(c_(colb, 6), ot, AX, apply_absolute_value=True)
            V.tensor_tensor(c_(colb, 4), c_(colb, 4), c_(colb, 6), OP.max)
            f5_dv.append(s_dv.n)

            nc.gpsimd.wait_ge(s_dv.h, s_dv.n)
            G.dma_start(out=or_d[ich], in_=FOUT[:, dbk * SL: dbk * SL + TB * NB]
                        ).then_inc(sos.h, 16)
            sos.n += 16

        stF4.close()


        def fin4():
            V.tensor_tensor(c_(row, RT0), c_(row, RST), c_(row, RSU), OP.mult)
            V.tensor_tensor(c_(row, RT1), ccs[:, 1:2], c_(row, RT0), OP.mult)
            V.tensor_scalar(c_(row, RSO), c_(row, RT1), c_(row, R32K), 1e-12,
                            OP.mult, OP.max)
            V.reciprocal(c_(row, RT1), c_(row, RSO))
            V.tensor_tensor(row[:1, 21:22], c_(row, RT0), c_(row, RT1), OP.mult)
            V.tensor_copy(row[:1, 22:23], c_(row, RSO))
        stF2 = contextlib.ExitStack()
        WD = sb("WD", [128, 2 * ICH * E], BF16, stF2)
        OTB = sb("OTB", [128, 2 * 2 * NB], BF16, stF2)
        XSB = sb("XSB", [128, 4 * NB], BF16, stF2)

        SY.wait_ge(s_or[0].h, s_or[0].n)
        SY.wait_ge(s_or[1].h, s_or[1].n)
        SY.dma_start(out=WD[:, :ICH * E].rearrange("p (c n) -> p c n", n=E),
                     in_=whl_d[("d", 0)].transpose([1, 0, 2])).then_inc(s_kp[0].h, 16)
        SY.dma_start(out=WD[:, ICH * E:].rearrange("p (c n) -> p c n", n=E),
                     in_=whl_d[("d", 1)].transpose([1, 0, 2])).then_inc(s_kp[0].h, 16)
        s_kp[0].n += 32

        emit_collective(3, (4,), fin4)
        nc.tensor.wait_ge(s_kp[0].h, s_kp[0].n)

        ot_marks = []
        xstat_marks = []
        f6_mark = {}
        f6_dv_seq = []
        ln2_mark = [0]
        ev_done = [0]

        def f6_x1_load(tb):
            if ev_done[0] > 0:
                SY.wait_ge(s_dv.h, ev_done[0])
            SY.dma_start(out=FSTI[:, SL:SL + ECH * NB]
                         .rearrange("p (c n) -> p c n", n=NB),
                         in_=x1_d[:, :, tb * NB:(tb + 1) * NB].transpose([1, 0, 2])
                         ).then_inc(s_x1b.h, 16)
            s_x1b.n += 16

        def f6_or_unit(tb, ich):
            dbk = ich % 2
            if len(f6_dv_seq) >= 2:
                SY.wait_ge(s_dv.h, f6_dv_seq[-2])
            sfp = s_fp[dbk]
            SY.dma_start(out=FSTI[:, dbk * NB:(dbk + 1) * NB],
                         in_=or_d[ich, :, tb * NB:(tb + 1) * NB]
                         ).then_inc(sfp.h, 16)
            sfp.n += 16

            nc.vector.wait_ge(sfp.h, sfp.n)
            if len(ot_marks) >= 2:
                nc.vector.wait_ge(s_mm.h, ot_marks[-2])
            src = FSTI[:, dbk * NB:(dbk + 1) * NB]
            oh = OTB[:, dbk * 2 * NB: dbk * 2 * NB + NB]
            ol = OTB[:, dbk * 2 * NB + NB: (dbk + 1) * 2 * NB]
            V.tensor_scalar(src, src, c_(scl, KOc), MAGIC, OP.mult, OP.add)
            V.tensor_scalar(src, src, MAGIC, None, OP.subtract)
            V.tensor_copy(oh, src)
            V.tensor_tensor(ol, src, oh, OP.subtract)
            f6_mark[(tb, ich)] = s_dv.n
            f6_dv_seq.append(s_dv.n)

        f6_x1_load(0)
        f6_or_unit(0, 0)
        f6_or_unit(0, 1)

        for tb in range(TB):
            x1_ready = s_x1b.n
            for ich in range(ICH):
                dbk = ich % 2
                if ich >= 2:
                    f6_or_unit(tb, ich)
                oh = OTB[:, dbk * 2 * NB: dbk * 2 * NB + NB]
                ol = OTB[:, dbk * 2 * NB + NB: (dbk + 1) * 2 * NB]
                nc.tensor.wait_ge(s_dv.h, f6_mark[(tb, ich)])
                if ich == 0 and ln2_mark[0] > 0:
                    nc.tensor.wait_ge(s_dv.h, ln2_mark[0])
                for hc in range(ECH):
                    lwh = WD[:, ich * E + hc * 128: ich * E + hc * 128 + 128]
                    lwl = WD[:, ICH * E + ich * E + hc * 128:
                             ICH * E + ich * E + hc * 128 + 128]
                    T.matmul(bank(hc), lwh, oh, start=(ich == 0), stop=False)
                    T.matmul(bank(hc), lwh, ol, start=False, stop=False)
                    last = T.matmul(bank(hc), lwl, oh, start=False,
                                    stop=(ich == ICH - 1))
                last.then_inc(s_mm.h, 1)
                s_mm.n += 1
                ot_marks.append(s_mm.n)

            nc.vector.wait_ge(s_mm.h, ot_marks[-1])
            nc.vector.wait_ge(s_x1b.h, x1_ready)
            if s_out.n > 0:
                nc.vector.wait_ge(s_out.h, s_out.n)
            for hc in range(ECH):
                xp = FOUT[:, hc * NB:(hc + 1) * NB]
                V.tensor_scalar(xp, bank(hc), c_(scl, SOc), None, OP.mult)
                V.tensor_tensor(xp, xp, FSTI[:, SL + hc * NB: SL + (hc + 1) * NB],
                                OP.add)
            f6_evac_done = s_dv.n
            ev_done[0] = f6_evac_done

            if tb + 1 < TB:
                f6_x1_load(tb + 1)
                f6_or_unit(tb + 1, 0)
                f6_or_unit(tb + 1, 1)

            for hc in range(ECH):
                nc.scalar.wait_ge(s_dv.h, f6_evac_done)
                if len(xstat_marks) >= 2:
                    nc.scalar.wait_ge(s_mm.h, xstat_marks[-2])
                xp = FOUT[:, hc * NB:(hc + 1) * NB]
                xb = XSB[:, (hc % 2) * 2 * NB:(hc % 2) * 2 * NB + NB]
                xq = XSB[:, (hc % 2) * 2 * NB + NB:((hc % 2) + 1) * 2 * NB]
                C.activation(xb, xp, ACT.Copy)
                C.activation(xq, xp, ACT.Square)

                nc.tensor.wait_ge(s_sc.h, s_sc.n)
                nc.tensor.wait_ge(s_dv.h, f6_evac_done)
                T.matmul(bank(0), ones_b[:], xb,
                         start=(hc == 0), stop=(hc == ECH - 1))
                T.matmul(bank(1), ones_b[:], xq,
                         start=(hc == 0), stop=(hc == ECH - 1)
                         ).then_inc(s_mm.h, 1)
                s_mm.n += 1
                xstat_marks.append(s_mm.n)

            nc.vector.wait_ge(s_mm.h, xstat_marks[-1])
            mn = FSTI[:, 2 * NB:3 * NB]
            vb = FSTI[:, 3 * NB:4 * NB]
            rs = FSTI[:, 4 * NB:5 * NB]
            V.tensor_scalar(mn, bank(0), 1.0 / E, None, OP.mult)
            V.tensor_scalar(vb, bank(1), 1.0 / E, None, OP.mult)
            V.tensor_tensor(rs, mn, mn, OP.mult)
            V.tensor_tensor(vb, vb, rs, OP.subtract)
            ln2_mark[0] = s_dv.n
            nc.scalar.wait_ge(s_dv.h, ln2_mark[0])
            C.activation(rs, vb, ACT.Sqrt, bias=eps_c[:])
            nc.vector.wait_ge(s_sc.h, s_sc.n)
            V.reciprocal(rs, rs)

            for hc in range(ECH):
                xp = FOUT[:, hc * NB:(hc + 1) * NB]
                V.tensor_tensor(xp, xp, mn, OP.subtract)
                V.tensor_tensor(xp, xp, rs, OP.mult)

                nc.scalar.wait_ge(s_dv.h, s_dv.n)
                C.dma_start(out=out_d[hc, :, tb * NB:(tb + 1) * NB],
                            in_=xp).then_inc(s_out.h, 16)
                s_out.n += 16

        nc.gpsimd.wait_ge(s_out.h, s_out.n)
        stF2.close()
    return nc


_CACHE = {}


def _get_program(S):
    if S not in _CACHE:
        _CACHE[S] = build_program(S)
    return _CACHE[S]


def stage_inputs(L0_attn_x, L0_attn_w_query, L0_attn_w_key, L0_attn_w_value,
                 L0_attn_w_proj, L0_attn_b_proj, L0_ffn_w_gate, L0_ffn_w_up,
                 L0_ffn_w_down):
    x = np.ascontiguousarray(np.asarray(L0_attn_x, dtype=np.float32))
    B, S, _E = x.shape

    def t8(w):
        return np.ascontiguousarray(np.asarray(w, np.float32).T.reshape(ECH, 128, E))

    shared = {
        "awT_q": t8(L0_attn_w_query), "awT_k": t8(L0_attn_w_key),
        "awT_v": t8(L0_attn_w_value), "awT_p": t8(L0_attn_w_proj),
        "bproj": np.ascontiguousarray(np.asarray(L0_attn_b_proj, np.float32)
                                      .reshape(ECH, 128).T),
        "wgT": np.ascontiguousarray(np.asarray(L0_ffn_w_gate, np.float32).T
                                    .reshape(ECH, 128, I3)),
        "wuT": np.ascontiguousarray(np.asarray(L0_ffn_w_up, np.float32).T
                                    .reshape(ECH, 128, I3)),
        "wdT": np.ascontiguousarray(np.asarray(L0_ffn_w_down, np.float32).T
                                    .reshape(ICH, 128, E)),
        "wg_nat": np.ascontiguousarray(np.asarray(L0_ffn_w_gate, np.float32)
                                       .reshape(ICH, 128, E)),
        "wu_nat": np.ascontiguousarray(np.asarray(L0_ffn_w_up, np.float32)
                                       .reshape(ICH, 128, E)),
        "wd_nat": np.ascontiguousarray(np.asarray(L0_ffn_w_down, np.float32)
                                       .reshape(ECH, 128, I3)),
        "ident32": np.eye(128, dtype=np.float32),
    }
    in_maps = []
    for c in range(B):
        m = dict(shared)
        m["xT"] = np.ascontiguousarray(x[c].T.reshape(ECH, 128, S))
        in_maps.append(m)
    return in_maps


def kernel(**inputs):
    in_maps = stage_inputs(**inputs)
    B = len(in_maps)
    S = in_maps[0]["xT"].shape[2]
    nc = _get_program(S)
    res = run_bass_kernel_spmd(nc, in_maps, core_ids=list(range(B)))
    outs = []
    for c in range(B):
        y = np.asarray(res.results[c]["out"]).reshape(E, S).T
        outs.append(y)
    return np.ascontiguousarray(np.stack(outs)).astype(np.float32)


# version 13
# speedup vs baseline: 1.0617x; 1.0020x over previous
import contextlib
import numpy as np

import concourse.bass as bass
import concourse.mybir as mybir
from concourse.bass_utils import run_bass_kernel_spmd

F32 = mybir.dt.float32
BF16 = mybir.dt.bfloat16
OP = mybir.AluOpType
ACT = mybir.ActivationFunctionType
AX = mybir.AxisListType.X

MAGIC = 12582912.0
EXP_SHIFT = -24.0
E = 1024
ECH = 8
I3 = 3072
ICH = 24
NB = 512


class Sem:
    def __init__(self, ctx, nc, name):
        self.h = ctx.enter_context(nc.semaphore(name))
        self.n = 0
        self.name = name


_CHAIN_OPS = {
    "tensor_copy", "tensor_tensor", "tensor_scalar", "tensor_single_scalar",
    "tensor_add", "tensor_sub", "tensor_mul", "tensor_max", "tensor_reduce",
    "reduce_max", "reduce_sum", "reciprocal", "memset", "activation", "iota",
    "scalar_tensor_tensor", "select", "tensor_scalar_add", "tensor_scalar_sub",
    "tensor_scalar_mul", "tensor_scalar_max", "tensor_scalar_min",
}


class Eng:

    def __init__(self, eng, sem):
        self._e = eng
        self._s = sem

    def __getattr__(self, name):
        fn = getattr(self._e, name)
        if name in _CHAIN_OPS:
            def wrap(*a, **k):
                if self._s.n > 0:
                    self._e.wait_ge(self._s.h, self._s.n)
                ins = fn(*a, **k)
                ins.then_inc(self._s.h, 1)
                self._s.n += 1
                return ins
            return wrap
        if name in ("dma_start", "collective_compute", "dma_start_transpose"):
            def wrapd(*a, **k):
                if self._s.n > 0:
                    self._e.wait_ge(self._s.h, self._s.n)
                return fn(*a, **k)
            return wrapd
        return fn


def build_program(S=4096):
    TB = S // NB
    KT = S // 128
    SL = max(ECH * NB, S)

    nc = bass.Bass("TRN2", target_bir_lowering=False)

    xT_in = nc.dram_tensor("xT", [ECH, 128, S], F32, kind="ExternalInput").ap()
    awT = {w: nc.dram_tensor(f"awT_{w}", [ECH, 128, E], F32, kind="ExternalInput").ap()
           for w in ("q", "k", "v", "p")}
    bias_in = nc.dram_tensor("bproj", [128, ECH], F32, kind="ExternalInput").ap()
    wT_in = {"g": nc.dram_tensor("wgT", [ECH, 128, I3], F32, kind="ExternalInput").ap(),
             "u": nc.dram_tensor("wuT", [ECH, 128, I3], F32, kind="ExternalInput").ap(),
             "d": nc.dram_tensor("wdT", [ICH, 128, E], F32, kind="ExternalInput").ap()}
    w_nat = {"g": nc.dram_tensor("wg_nat", [ICH, 128, E], F32, kind="ExternalInput").ap(),
             "u": nc.dram_tensor("wu_nat", [ICH, 128, E], F32, kind="ExternalInput").ap(),
             "d": nc.dram_tensor("wd_nat", [ECH, 128, I3], F32, kind="ExternalInput").ap()}
    ident_in = nc.dram_tensor("ident32", [128, 128], F32, kind="ExternalInput").ap()

    out_d = nc.dram_tensor("out", [ECH, 128, S], F32, kind="ExternalOutput").ap()

    qh_d = nc.dram_tensor("qh_d", [TB, ECH, 128, NB], BF16).ap()
    ql_d = nc.dram_tensor("ql_d", [TB, ECH, 128, NB], BF16).ap()
    kh_d = nc.dram_tensor("kh_d", [KT, ECH, 128, 128], BF16).ap()
    kl_d = nc.dram_tensor("kl_d", [KT, ECH, 128, 128], BF16).ap()
    vh_d = nc.dram_tensor("vh_d", [KT, 128, E], BF16).ap()
    vl_d = nc.dram_tensor("vl_d", [KT, 128, E], BF16).ap()
    x1_d = nc.dram_tensor("x1_d", [ECH, 128, S], F32).ap()
    whl_d = {("g", 0): nc.dram_tensor("wgh_d", [ECH, 128, I3], BF16).ap(),
             ("g", 1): nc.dram_tensor("wgl_d", [ECH, 128, I3], BF16).ap(),
             ("u", 0): nc.dram_tensor("wuh_d", [ECH, 128, I3], BF16).ap(),
             ("u", 1): nc.dram_tensor("wul_d", [ECH, 128, I3], BF16).ap(),
             ("d", 0): nc.dram_tensor("wdh_d", [ICH, 128, E], BF16).ap(),
             ("d", 1): nc.dram_tensor("wdl_d", [ICH, 128, E], BF16).ap()}
    sc_d = {"g": (nc.dram_tensor("sg_d", [32, I3], F32).ap(),
                  nc.dram_tensor("sgr_d", [32, I3], F32).ap()),
            "u": (nc.dram_tensor("su_d", [32, I3], F32).ap(),
                  nc.dram_tensor("sur_d", [32, I3], F32).ap()),
            "d": (nc.dram_tensor("sd_d", [96, E], F32).ap(),
                  nc.dram_tensor("sdr_d", [96, E], F32).ap())}
    upr_d = nc.dram_tensor("upr_d", [ICH, 128, S], F32).ap()
    gtr_d = nc.dram_tensor("gtr_d", [ICH, 128, S], F32).ap()
    tr_d = nc.dram_tensor("tr_d", [ICH, 128, S], F32).ap()
    or_d = nc.dram_tensor("or_d", [ICH, 128, S], F32).ap()
    cc_in = [nc.dram_tensor(f"cc_in{i}", [1, 16], F32).ap() for i in range(4)]
    cc_out = [nc.dram_tensor(f"cc_out{i}", [1, 16], F32).ap() for i in range(4)]

    ctx = contextlib.ExitStack()
    with ctx:
        def sb(name, shape, dtype, st=None):
            return (st or ctx).enter_context(nc.sbuf_tensor(name, shape, dtype))

        ones_b = sb("ones_b", [128, 128], BF16)
        ones_r = sb("ones_r", [1, 128], F32)
        ident = sb("ident", [128, 128], F32)
        scl = sb("scl", [128, 16], F32)
        row = sb("row", [1, 32], F32)
        colb = sb("colb", [128, 8], F32)
        eps_c = sb("eps_c", [128, 1], F32)
        bexp_c = sb("bexp_c", [128, 1], F32)
        magic_c = sb("magic_c", [128, 1], F32)
        nmagic_c = sb("nmagic_c", [128, 1], F32)
        bias_sb = sb("bias_sb", [128, ECH], F32)
        ccs = sb("ccs", [1, 16], F32)

        R127, RR127, R32K, RSX, RSU, RSG, RST, RSO, RT0, RT1 = range(10)
        ISXc, KGc, SGc, KUc, KTc, KOc, SOc = range(7)
        SW = {"q": 8, "k": 9, "v": 10, "p": 11}

        def c_(t, a):
            return t[:, a:a + 1]

        ps_all = ctx.enter_context(nc.psum_tensor("psall", [128, 8 * NB], F32))

        def bank(i):
            return ps_all[:, i * NB:(i + 1) * NB]

        prow = ps_all[:1, 7 * NB:7 * NB + 128]
        prow2 = ps_all[:1, 6 * NB:6 * NB + 128]
        pbc8 = ps_all[:, 7 * NB:7 * NB + 8]
        pbc2 = ps_all[:, 7 * NB:7 * NB + 2]

        s_in = Sem(ctx, nc, "s_in")
        s_x = Sem(ctx, nc, "s_x")
        s_pw = Sem(ctx, nc, "s_pw")
        s_kp = [Sem(ctx, nc, f"s_k{i}") for i in range(2)]
        s_vp = [Sem(ctx, nc, f"s_v{i}") for i in range(2)]
        s_fp = [Sem(ctx, nc, f"s_f{i}") for i in range(2)]
        s_gn = [Sem(ctx, nc, f"s_g{i}") for i in range(2)]
        s_x1b = Sem(ctx, nc, "s_x1b")
        s_q = Sem(ctx, nc, "s_q")
        s_ps = Sem(ctx, nc, "s_ps")
        s_sq = [Sem(ctx, nc, f"s_sq{i}") for i in range(2)]
        s_sx = [Sem(ctx, nc, f"s_sx{i}") for i in range(2)]
        s_so = [Sem(ctx, nc, f"s_so{i}") for i in range(2)]
        s_or = [Sem(ctx, nc, f"s_or{i}") for i in range(2)]
        s_tr = Sem(ctx, nc, "s_tr")
        s_gd = Sem(ctx, nc, "s_gd")
        s_mm = Sem(ctx, nc, "s_mm")
        s_dv = Sem(ctx, nc, "s_dv")
        s_sc = Sem(ctx, nc, "s_sc")
        s_cc = Sem(ctx, nc, "s_cc")
        s_gp = Sem(ctx, nc, "s_gp")
        s_out = Sem(ctx, nc, "s_out")

        block = ctx.enter_context(nc.Block())
        V = Eng(nc.vector, s_dv)
        C = Eng(nc.scalar, s_sc)
        G = Eng(nc.gpsimd, s_gp)
        T = nc.tensor
        SY = nc.sync

        stA = contextlib.ExitStack()
        xts = sb("xts", [128, ECH * NB], F32, stA)
        xspl = sb("xspl", [128, 2 * ECH * NB], BF16, stA)
        qst = sb("qst", [128, NB], F32, stA)
        qhl = sb("qhl", [128, 4 * NB], BF16, stA)
        awp = sb("awp", [128, ECH * E], BF16, stA)

        stQ = contextlib.ExitStack()
        aw3 = sb("aw3", [128, 3 * ECH * E], BF16, stQ)
        stW = contextlib.ExitStack()
        wstg = sb("wstg", [128, ECH * E], F32, stW)

        awq = {"q": aw3[:, 0:ECH * E], "k": aw3[:, ECH * E:2 * ECH * E],
               "v": aw3[:, 2 * ECH * E:], "p": awp[:]}

        xh_s, xl_s = xspl[:, :ECH * NB], xspl[:, ECH * NB:]

        SY.dma_start(out=ident[:], in_=ident_in).then_inc(s_in.h, 16)
        s_in.n += 16
        SY.dma_start(out=bias_sb[:], in_=bias_in).then_inc(s_in.h, 16)
        s_in.n += 16

        V.memset(ones_b[:], 1.0)
        V.memset(ones_r[:], 1.0)
        V.memset(eps_c[:], 1e-5)
        V.memset(bexp_c[:], EXP_SHIFT)
        V.memset(magic_c[:], MAGIC)
        V.memset(nmagic_c[:], -MAGIC)
        V.memset(colb[:], 0.0)
        V.memset(row[:], 0.0)
        V.memset(ccs[:], 0.0)
        V.memset(scl[:], 0.0)
        V.memset(c_(row, R127), 127.0)
        V.reciprocal(c_(row, RR127), c_(row, R127))
        V.memset(c_(row, R32K), 32767.0)
        V.reciprocal(c_(row, R32K), c_(row, R32K))
        V.memset(c_(colb, 5), 7.0)
        V.reciprocal(c_(colb, 5), c_(colb, 5))

        for j, w in enumerate(("q", "k", "v", "p")):
            SY.wait_ge(s_dv.h, s_dv.n)
            SY.dma_start(out=wstg[:].rearrange("p (c n) -> p c n", n=E),
                         in_=awT[w].transpose([1, 0, 2])).then_inc(s_in.h, 16)
            s_in.n += 16

            nc.vector.wait_ge(s_in.h, s_in.n)
            V.reduce_max(c_(colb, 6), wstg[:], AX, apply_absolute_value=True)
            nc.tensor.wait_ge(s_dv.h, s_dv.n)
            T.transpose(prow, colb[:, 6:7], ident[:]).then_inc(s_mm.h, 1)
            s_mm.n += 1

            nc.vector.wait_ge(s_mm.h, s_mm.n)
            V.reduce_max(c_(row, RT0), prow, AX)
            V.tensor_tensor(c_(row, RT0), c_(row, RT0), c_(row, RR127), OP.mult)
            V.reciprocal(c_(row, RT1), c_(row, RT0))
            nc.tensor.wait_ge(s_dv.h, s_dv.n)
            T.matmul(pbc2, ones_r[:], row[:1, RT0:RT0 + 2],
                     start=True, stop=True).then_inc(s_mm.h, 1)
            s_mm.n += 1

            nc.vector.wait_ge(s_mm.h, s_mm.n)
            V.tensor_copy(c_(scl, SW[w]), pbc2[:, 0:1])
            V.tensor_copy(c_(colb, 7), pbc2[:, 1:2])
            V.tensor_scalar(wstg[:], wstg[:], c_(colb, 7), MAGIC, OP.mult, OP.add)
            V.tensor_scalar(awq[w], wstg[:], MAGIC, None, OP.subtract)

        stW.close()
        SCLS = sb("SCLS", [128, 2304], F32, stQ)
        SCLR = sb("SCLR", [128, 2304], F32, stQ)
        prepw2 = sb("prepw2", [128, 2 * I3], F32, stQ)
        preps2 = sb("preps2", [128, 2 * I3], F32, stQ)
        prepr2 = sb("prepr2", [128, 2 * I3], F32, stQ)
        prepq2 = sb("prepq2", [128, I3], BF16, stQ)
        prepl2 = sb("prepl2", [128, I3], BF16, stQ)
        tbuf2 = sb("tbuf2", [96, 2 * 256], F32, stQ)

        prep_units = ([("scale", "g", c) for c in range(ICH)]
                      + [("scale", "u", c) for c in range(ICH)]
                      + [("scale", "d", c) for c in range(ECH)]
                      + [("tbatch", None, None)]
                      + [("deq", "g", c) for c in range(ECH)]
                      + [("deq", "u", c) for c in range(ECH)]
                      + [("deq", "d", c) for c in range(ICH)])
        scl_off = {"g": 0, "u": 768, "d": 1536}

        prep_slot = [0]
        prep_vmark = {0: 0, 1: 0}
        prep_pw = [Sem(ctx, nc, "s_pw0"), Sem(ctx, nc, "s_pw1")]
        prep_ps = [Sem(ctx, nc, "s_ps0"), Sem(ctx, nc, "s_ps1")]

        def emit_prep_load(kind, which, c):
            sl = prep_slot[0] & 1
            prep_slot[0] += 1
            prepw = prepw2[:, sl * I3:]
            preps = preps2[:, sl * I3:]
            prepr = prepr2[:, sl * I3:]
            spw = prep_pw[sl]
            sps = prep_ps[sl]
            if kind == "scale":
                n_in = E if which in ("g", "u") else I3
                if prep_vmark[sl] > 0:
                    SY.wait_ge(s_dv.h, prep_vmark[sl])
                SY.dma_start(out=prepw[:, :n_in], in_=w_nat[which][c]
                             ).then_inc(spw.h, 16)
                spw.n += 16
            else:
                n_in = I3 if which in ("g", "u") else E
                sd_, srd_ = sc_d[which]
                if prep_vmark[sl] > 0:
                    SY.wait_ge(s_dv.h, prep_vmark[sl])
                if spw.n > 0:
                    SY.wait_ge(spw.h, spw.n)
                SY.dma_start(out=prepw[:, :n_in], in_=wT_in[which][c]
                             ).then_inc(spw.h, 16)
                sin = sd_[c * 4:c * 4 + 4].unsqueeze(1).broadcast_to([4, 32, n_in])
                rin = srd_[c * 4:c * 4 + 4].unsqueeze(1).broadcast_to([4, 32, n_in])
                SY.dma_start(out=preps[:, :n_in], in_=sin).then_inc(spw.h, 16)
                SY.dma_start(out=prepr[:, :n_in], in_=rin).then_inc(spw.h, 16)
                spw.n += 48
            return sl

        def emit_prep_compute(kind, which, c, sl):
            prepw = prepw2[:, sl * I3:]
            preps = preps2[:, sl * I3:]
            prepr = prepr2[:, sl * I3:]
            prepq = prepq2[:, 0:]
            prepl = prepl2[:, 0:]
            spw = prep_pw[sl]
            sps = prep_ps[sl]
            if kind == "scale":
                n_in = E if which in ("g", "u") else I3
                nb_ = n_in // 32
                cols = scl_off[which] + c * nb_
                nc.vector.wait_ge(spw.h, spw.n)
                V.tensor_reduce(SCLS[:, cols:cols + nb_],
                                prepw[:, :n_in].rearrange("p (b c) -> p b c", c=32),
                                AX, OP.max, apply_absolute_value=True)
                V.tensor_scalar(SCLS[:, cols:cols + nb_], SCLS[:, cols:cols + nb_],
                                c_(colb, 5), 1e-12, OP.mult, OP.max)
                V.reciprocal(SCLR[:, cols:cols + nb_], SCLS[:, cols:cols + nb_])
                prep_vmark[sl] = s_dv.n
            else:
                n_in = I3 if which in ("g", "u") else E
                nc.vector.wait_ge(spw.h, spw.n)
                if sps.n > 0:
                    nc.vector.wait_ge(sps.h, sps.n)
                V.tensor_tensor(prepw[:, :n_in], prepw[:, :n_in],
                                prepr[:, :n_in], OP.mult)
                V.tensor_scalar(prepw[:, :n_in], prepw[:, :n_in], MAGIC, MAGIC,
                                OP.add, OP.subtract)
                V.tensor_tensor(prepw[:, :n_in], prepw[:, :n_in],
                                preps[:, :n_in], OP.mult)
                osps = prep_ps[1 - sl]
                if osps.n > 0:
                    nc.vector.wait_ge(osps.h, osps.n)
                V.tensor_copy(prepq[:, :n_in], prepw[:, :n_in])
                V.tensor_tensor(prepl[:, :n_in], prepw[:, :n_in],
                                prepq[:, :n_in], OP.subtract)
                prep_vmark[sl] = s_dv.n
                nc.scalar.wait_ge(s_dv.h, s_dv.n)
                for half in range(2):
                    src_t = (prepq if half == 0 else prepl)[:, :n_in]
                    C.dma_start(out=whl_d[(which, half)][c],
                                in_=src_t).then_inc(sps.h, 16)
                sps.n += 32

        def emit_tbatch():
                nc.tensor.wait_ge(s_dv.h, s_dv.n)
                jobs = []
                for wh in ("g", "u", "d"):
                    nchunks = ICH if wh in ("g", "u") else ECH
                    nb2 = 32 if wh in ("g", "u") else 96
                    for cc2 in range(nchunks):
                        for half2, (srcb, dstb) in enumerate(
                                ((SCLS, sc_d[wh][0]), (SCLR, sc_d[wh][1]))):
                            jobs.append((wh, cc2, nb2, srcb, dstb))
                tsems = [prep_ps[0], prep_ps[1], prep_pw[0], prep_pw[1]]
                vmarks = []
                smarks = []
                for ji, (wh, cc2, nb2, srcb, dstb) in enumerate(jobs):
                    cols = scl_off[wh] + cc2 * nb2
                    pslot = ji % 2
                    preg = ps_all[:nb2, (6 + pslot) * NB:(6 + pslot) * NB + 128]
                    if len(vmarks) >= 2:
                        nc.tensor.wait_ge(s_dv.h, vmarks[-2])
                    T.transpose(preg, srcb[:, cols:cols + nb2],
                                ident[:]).then_inc(s_mm.h, 1)
                    s_mm.n += 1
                    tslot = ji % 4
                    ts_ = tsems[tslot]
                    nc.vector.wait_ge(s_mm.h, s_mm.n)
                    if len(smarks) >= 4:
                        nc.vector.wait_ge(ts_.h, smarks[-4])
                    V.tensor_copy(tbuf2[:nb2, tslot * 128:(tslot + 1) * 128], preg)
                    vmarks.append(s_dv.n)
                    nc.scalar.wait_ge(s_dv.h, s_dv.n)
                    C.dma_start(out=dstb[:nb2, cc2 * 128:(cc2 + 1) * 128],
                                in_=tbuf2[:nb2, tslot * 128:(tslot + 1) * 128]
                                ).then_inc(ts_.h, 16)
                    ts_.n += 16
                    smarks.append(ts_.n)

        prep_iter = iter(prep_units)
        prep_pending = [None]

        def flush_prep():
            if prep_pending[0] is not None:
                emit_prep_compute(*prep_pending[0])
                prep_pending[0] = None

        def take_prep(kn):
            for _ in range(kn):
                u = next(prep_iter, None)
                if u is None:
                    continue
                if u[0] == "tbatch":
                    flush_prep()
                    emit_tbatch()
                else:
                    sl = emit_prep_load(*u)
                    flush_prep()
                    prep_pending[0] = (u[0], u[1], u[2], sl)

        qkv_evac_marks = []
        qkv_spill_marks = []

        for tb in range(TB):
            SY.wait_ge(s_dv.h, s_dv.n)
            SY.dma_start(out=xts[:].rearrange("p (c n) -> p c n", n=NB),
                         in_=xT_in[:, :, tb * NB:(tb + 1) * NB].transpose([1, 0, 2])
                         ).then_inc(s_x.h, 16)
            s_x.n += 16

            nc.vector.wait_ge(s_x.h, s_x.n)
            nc.vector.wait_ge(s_mm.h, s_mm.n)
            V.tensor_copy(xh_s, xts[:])
            V.tensor_tensor(xl_s, xts[:], xh_s, OP.subtract)
            sd_split = s_dv.n

            tiles = [("q", oc, None) for oc in range(ECH)] \
                + [("k", oc, None) for oc in range(ECH)] \
                + [("v", t_, h_) for t_ in range(NB // 128) for h_ in range(2)]
            for idx, (w, a, b_) in enumerate(tiles):
                pb = idx % 4
                nc.tensor.wait_ge(s_dv.h, sd_split)
                if len(qkv_evac_marks) >= 4:
                    nc.tensor.wait_ge(s_dv.h, qkv_evac_marks[-4])
                if w in ("q", "k"):
                    for ec in range(ECH):
                        lw = awq[w][:, ec * E + a * 128: ec * E + a * 128 + 128]
                        T.matmul(bank(pb), lw, xh_s[:, ec * NB:(ec + 1) * NB],
                                 start=(ec == 0), stop=False)
                        last = T.matmul(bank(pb), lw, xl_s[:, ec * NB:(ec + 1) * NB],
                                        start=False, stop=(ec == ECH - 1))
                else:
                    for ec in range(ECH):
                        lwh = xh_s[:, ec * NB + a * 128: ec * NB + a * 128 + 128]
                        lwl = xl_s[:, ec * NB + a * 128: ec * NB + a * 128 + 128]
                        rh = awq["v"][:, ec * E + b_ * NB: ec * E + (b_ + 1) * NB]
                        T.matmul(bank(pb), lwh, rh, start=(ec == 0), stop=False)
                        last = T.matmul(bank(pb), lwl, rh, start=False,
                                        stop=(ec == ECH - 1))
                last.then_inc(s_mm.h, 1)
                s_mm.n += 1

                nc.vector.wait_ge(s_mm.h, s_mm.n)
                sqs = s_sq[pb % 2]
                if sqs.n > 0:
                    nc.vector.wait_ge(sqs.h, sqs.n)
                st_ = qst[:, 0:NB]
                hi = qhl[:, (pb % 2) * 2 * NB:(pb % 2) * 2 * NB + NB]
                lo = qhl[:, (pb % 2) * 2 * NB + NB:(pb % 2) * 2 * NB + 2 * NB]
                V.tensor_scalar(st_, bank(pb), c_(scl, SW[w]), None, OP.mult)
                V.tensor_copy(hi, st_)
                V.tensor_tensor(lo, st_, hi, OP.subtract)
                qkv_evac_marks.append(s_dv.n)

                nc.scalar.wait_ge(s_dv.h, s_dv.n)
                if w == "q":
                    C.dma_start(out=qh_d[tb, a], in_=hi).then_inc(sqs.h, 16)
                    C.dma_start(out=ql_d[tb, a], in_=lo).then_inc(sqs.h, 16)
                elif w == "k":
                    KPT = NB // 128
                    C.dma_start(out=kh_d[tb * KPT:(tb + 1) * KPT, a]
                                .transpose([1, 0, 2]),
                                in_=hi.rearrange("p (t r) -> p t r", r=128)
                                ).then_inc(sqs.h, 16)
                    C.dma_start(out=kl_d[tb * KPT:(tb + 1) * KPT, a]
                                .transpose([1, 0, 2]),
                                in_=lo.rearrange("p (t r) -> p t r", r=128)
                                ).then_inc(sqs.h, 16)
                else:
                    kt_ = tb * (NB // 128) + a
                    C.dma_start(out=vh_d[kt_, :, b_ * NB:(b_ + 1) * NB],
                                in_=hi).then_inc(sqs.h, 16)
                    C.dma_start(out=vl_d[kt_, :, b_ * NB:(b_ + 1) * NB],
                                in_=lo).then_inc(sqs.h, 16)
                sqs.n += 32
                qkv_spill_marks.append((pb % 2, sqs.n))
                if idx % 2 == 1:
                    take_prep(1)

        take_prep(len(prep_units))
        flush_prep()
        stQ.close()

        stT = contextlib.ExitStack()
        SS = sb("SS", [128, 2 * KT * NB], BF16, stT)
        QT = sb("QT", [128, 4 * ECH * NB], BF16, stT)
        KV = sb("KV", [128, 2 * 2 * ECH * 128], BF16, stT)
        VV = sb("VV", [128, 2 * 2 * E], BF16, stT)
        S32 = sb("S32", [128, 3 * NB], F32, stT)
        SUMR = sb("SUMR", [128, NB], F32, stT)
        X1P = sb("X1P", [128, ECH * NB], F32, stT)
        XB = sb("XB", [128, 2 * ECH * NB], BF16, stT)
        MEAN = S32[:, 0:NB]
        VARB = S32[:, NB:2 * NB]
        RSTD = S32[:, 2 * NB:3 * NB]

        ah_s, al_s = xh_s, xl_s

        def sh_t(kt):
            return SS[:, kt * NB:(kt + 1) * NB]

        def sl_t(kt):
            return SS[:, KT * NB + kt * NB:KT * NB + (kt + 1) * NB]

        SY.wait_ge(s_sq[0].h, s_sq[0].n)
        SY.wait_ge(s_sq[1].h, s_sq[1].n)
        SY.wait_ge(s_mm.h, s_mm.n)
        SY.wait_ge(s_dv.h, s_dv.n)
        SY.wait_ge(s_sc.h, s_sc.n)
        SY.wait_ge(s_gp.h, s_gp.n)
        SY.wait_ge(prep_ps[0].h, prep_ps[0].n)
        SY.wait_ge(prep_ps[1].h, prep_ps[1].n)
        SY.wait_ge(prep_pw[0].h, prep_pw[0].n)
        SY.wait_ge(prep_pw[1].h, prep_pw[1].n)
        q_marks = {}
        qscore_end = {}
        bmm_end = {}

        def load_q(qb_):
            qsl_ = qb_ & 1
            if qb_ >= 2:
                SY.wait_ge(s_mm.h, qscore_end[qb_ - 2])
            qbase = qsl_ * 2 * ECH * NB
            SY.dma_start(out=QT[:, qbase:qbase + ECH * NB]
                         .rearrange("p (c n) -> p c n", n=NB),
                         in_=qh_d[qb_].transpose([1, 0, 2])).then_inc(s_q.h, 16)
            SY.dma_start(out=QT[:, qbase + ECH * NB:qbase + 2 * ECH * NB]
                         .rearrange("p (c n) -> p c n", n=NB),
                         in_=ql_d[qb_].transpose([1, 0, 2])).then_inc(s_q.h, 16)
            s_q.n += 32
            q_marks[qb_] = s_q.n

        prev_evac = 0
        x1v_mark = [0]
        for qb in range(TB):
            load_q(qb)
            QTs = QT[:, (qb & 1) * 2 * ECH * NB:]
            q_ready = q_marks[qb]
            if qb >= 1:
                SY.wait_ge(s_mm.h, qscore_end[qb - 1])

            base_kmm = s_mm.n
            k_marks = []
            exp_marks = []
            sl_marks = []
            for kt in range(KT):
                dbk = kt % 2
                if kt >= 2:
                    SY.wait_ge(s_mm.h, base_kmm + kt - 1)
                skp = s_kp[dbk]
                SY.dma_start(
                    out=KV[:, dbk * 2 * ECH * 128: dbk * 2 * ECH * 128 + ECH * 128]
                    .rearrange("p (c n) -> p c n", n=128),
                    in_=kh_d[kt].transpose([1, 0, 2])).then_inc(skp.h, 16)
                SY.dma_start(
                    out=KV[:, dbk * 2 * ECH * 128 + ECH * 128:
                           (dbk + 1) * 2 * ECH * 128]
                    .rearrange("p (c n) -> p c n", n=128),
                    in_=kl_d[kt].transpose([1, 0, 2])).then_inc(skp.h, 16)
                skp.n += 32
                k_marks.append((dbk, skp.n))

            for kt in range(KT):
                dbk = kt % 2
                sbk = 6 + (kt % 2)
                nc.tensor.wait_ge(s_q.h, q_ready)
                if kt == 0 and x1v_mark[0] > 0:
                    nc.scalar.wait_ge(s_dv.h, x1v_mark[0])
                nc.tensor.wait_ge(s_kp[k_marks[kt][0]].h, k_marks[kt][1])
                if kt < 2 and prev_evac > 0:
                    nc.tensor.wait_ge(s_dv.h, prev_evac)
                if len(exp_marks) >= 2:
                    nc.tensor.wait_ge(s_sc.h, exp_marks[-2])
                kh0 = KV[:, dbk * 2 * ECH * 128:]
                for ec in range(ECH):
                    lwh = kh0[:, ec * 128:(ec + 1) * 128]
                    lwl = kh0[:, ECH * 128 + ec * 128: ECH * 128 + (ec + 1) * 128]
                    T.matmul(bank(sbk), lwh, QTs[:, ec * NB:(ec + 1) * NB],
                             start=(ec == 0), stop=False)
                    T.matmul(bank(sbk), lwh,
                             QTs[:, ECH * NB + ec * NB: ECH * NB + (ec + 1) * NB],
                             start=False, stop=False)
                    last = T.matmul(bank(sbk), lwl, QTs[:, ec * NB:(ec + 1) * NB],
                                    start=False, stop=(ec == ECH - 1))
                last.then_inc(s_mm.h, 1)
                s_mm.n += 1

                if kt >= 1:
                    nc.tensor.wait_ge(s_dv.h, s_dv.n)
                    T.matmul(bank(5), ones_b[:], sh_t(kt - 1),
                             start=(kt == 1), stop=False)
                    T.matmul(bank(5), ones_b[:], sl_t(kt - 1),
                             start=False, stop=False)

                nc.scalar.wait_ge(s_mm.h, s_mm.n)
                if len(sl_marks) >= 2:
                    nc.scalar.wait_ge(s_dv.h, sl_marks[-2])
                C.activation(S32[:, dbk * NB:(dbk + 1) * NB], bank(sbk), ACT.Exp,
                             bias=bexp_c[:], scale=0.125)
                C.activation(sh_t(kt), bank(sbk), ACT.Exp,
                             bias=bexp_c[:], scale=0.125)
                exp_marks.append(s_sc.n)

                nc.vector.wait_ge(s_sc.h, s_sc.n)
                V.tensor_tensor(sl_t(kt), S32[:, dbk * NB:(dbk + 1) * NB],
                                sh_t(kt), OP.subtract)
                sl_marks.append(s_dv.n)

            qscore_end[qb] = s_mm.n
            nc.tensor.wait_ge(s_dv.h, s_dv.n)
            T.matmul(bank(5), ones_b[:], sh_t(KT - 1),
                     start=(KT == 1), stop=False)
            T.matmul(bank(5), ones_b[:], sl_t(KT - 1),
                     start=False, stop=True).then_inc(s_mm.h, 1)
            s_mm.n += 1

            nc.vector.wait_ge(s_mm.h, s_mm.n)
            V.reciprocal(SUMR[:], bank(5))
            sumr_done = s_dv.n

            if qb >= 1:
                SY.wait_ge(s_mm.h, bmm_end[qb - 1])
            base_bmm = s_mm.n
            for kt in range(KT):
                dbk = kt % 2
                if kt >= 2:
                    SY.wait_ge(s_mm.h, base_bmm + kt - 1)
                svp = s_vp[dbk]
                SY.dma_start(out=VV[:, dbk * 2 * E: dbk * 2 * E + E],
                             in_=vh_d[kt]).then_inc(svp.h, 16)
                SY.dma_start(out=VV[:, dbk * 2 * E + E: (dbk + 1) * 2 * E],
                             in_=vl_d[kt]).then_inc(svp.h, 16)
                svp.n += 32

                nc.tensor.wait_ge(svp.h, svp.n)
                if kt == 0:
                    nc.tensor.wait_ge(s_dv.h, sumr_done)
                    nc.tensor.wait_ge(s_sc.h, s_sc.n)
                vh0 = VV[:, dbk * 2 * E:]
                for ec in range(ECH):
                    lwh = vh0[:, ec * 128:(ec + 1) * 128]
                    lwl = vh0[:, E + ec * 128: E + (ec + 1) * 128]
                    T.matmul(bank(ec), lwh, sh_t(kt), start=(kt == 0), stop=False)
                    T.matmul(bank(ec), lwh, sl_t(kt), start=False, stop=False)
                    last = T.matmul(bank(ec), lwl, sh_t(kt), start=False,
                                    stop=(kt == KT - 1))
                last.then_inc(s_mm.h, 1)
                s_mm.n += 1

            bmm_end[qb] = s_mm.n
            nc.vector.wait_ge(s_mm.h, s_mm.n)
            for ec in range(ECH):
                V.tensor_copy(ah_s[:, ec * NB:(ec + 1) * NB], bank(ec))
                V.tensor_tensor(al_s[:, ec * NB:(ec + 1) * NB], bank(ec),
                                ah_s[:, ec * NB:(ec + 1) * NB], OP.subtract)
            evac_done = s_dv.n
            prev_evac = evac_done

            SY.wait_ge(s_dv.h, evac_done)
            SY.dma_start(out=xts[:].rearrange("p (c n) -> p c n", n=NB),
                         in_=xT_in[:, :, qb * NB:(qb + 1) * NB].transpose([1, 0, 2])
                         ).then_inc(s_x.h, 16)
            s_x.n += 16
            xts_ready = s_x.n

            proj_evac_marks = []
            for oc in range(ECH):
                pb = oc % 2
                nc.tensor.wait_ge(s_dv.h, evac_done)
                if len(proj_evac_marks) >= 2:
                    nc.tensor.wait_ge(s_dv.h, proj_evac_marks[-2])
                for ec in range(ECH):
                    lw = awq["p"][:, ec * E + oc * 128: ec * E + oc * 128 + 128]
                    T.matmul(bank(pb), lw, ah_s[:, ec * NB:(ec + 1) * NB],
                             start=(ec == 0), stop=False)
                    last = T.matmul(bank(pb), lw, al_s[:, ec * NB:(ec + 1) * NB],
                                    start=False, stop=(ec == ECH - 1))
                last.then_inc(s_mm.h, 1)
                s_mm.n += 1

                nc.vector.wait_ge(s_mm.h, s_mm.n)
                if oc == 0:
                    nc.vector.wait_ge(s_x.h, xts_ready)
                xp = X1P[:, oc * NB:(oc + 1) * NB]
                sxs2 = s_sx[oc % 2]
                if sxs2.n > 0:
                    nc.vector.wait_ge(sxs2.h, sxs2.n)
                V.tensor_tensor(xp, bank(pb), SUMR[:], OP.mult)
                V.tensor_scalar(xp, xp, c_(scl, SW["p"]),
                                c_(bias_sb, oc), OP.mult, OP.add)
                V.tensor_tensor(xp, xp, xts[:, oc * NB:(oc + 1) * NB], OP.add)
                V.tensor_copy(XB[:, oc * NB:(oc + 1) * NB], xp)
                proj_evac_marks.append(s_dv.n)

                nc.scalar.wait_ge(s_dv.h, s_dv.n)
                C.activation(XB[:, ECH * NB + oc * NB: ECH * NB + (oc + 1) * NB],
                             XB[:, oc * NB:(oc + 1) * NB], ACT.Square)

                nc.tensor.wait_ge(s_sc.h, s_sc.n)
                T.matmul(bank(2), ones_b[:], XB[:, oc * NB:(oc + 1) * NB],
                         start=(oc == 0), stop=(oc == ECH - 1))
                mm2 = T.matmul(bank(3), ones_b[:],
                               XB[:, ECH * NB + oc * NB: ECH * NB + (oc + 1) * NB],
                               start=(oc == 0), stop=(oc == ECH - 1))
                if oc == ECH - 1:
                    mm2.then_inc(s_mm.h, 1)
                    s_mm.n += 1

            nc.vector.wait_ge(s_mm.h, s_mm.n)
            V.tensor_scalar(MEAN[:], bank(2), 1.0 / E, None, OP.mult)
            V.tensor_scalar(VARB[:], bank(3), 1.0 / E, None, OP.mult)
            V.tensor_tensor(RSTD[:], MEAN[:], MEAN[:], OP.mult)
            V.tensor_tensor(VARB[:], VARB[:], RSTD[:], OP.subtract)
            nc.scalar.wait_ge(s_dv.h, s_dv.n)
            C.activation(RSTD[:], VARB[:], ACT.Sqrt, bias=eps_c[:])
            nc.vector.wait_ge(s_sc.h, s_sc.n)
            V.reciprocal(RSTD[:], RSTD[:])

            for oc in range(ECH):
                sxs = s_sx[oc % 2]
                xp = X1P[:, oc * NB:(oc + 1) * NB]
                V.tensor_tensor(xp, xp, MEAN[:], OP.subtract)
                V.tensor_tensor(xp, xp, RSTD[:], OP.mult)
                V.reduce_max(c_(colb, 6), xp, AX, apply_absolute_value=True)
                V.tensor_tensor(c_(colb, 0), c_(colb, 0), c_(colb, 6), OP.max)

                nc.scalar.wait_ge(s_dv.h, s_dv.n)
                C.dma_start(out=x1_d[oc, :, qb * NB:(qb + 1) * NB],
                            in_=xp).then_inc(sxs.h, 16)
                sxs.n += 16
            x1v_mark[0] = s_dv.n

        stT.close()
        stA.close()

        def emit_collective(idx, cols, finish_fn):
            nc.tensor.wait_ge(s_dv.h, s_dv.n)
            T.transpose(prow, colb[:, cols[0]:cols[0] + 1], ident[:]
                        ).then_inc(s_mm.h, 1)
            s_mm.n += 1
            if len(cols) > 1:
                T.transpose(prow2, colb[:, cols[1]:cols[1] + 1], ident[:]
                            ).then_inc(s_mm.h, 1)
                s_mm.n += 1
            nc.vector.wait_ge(s_mm.h, s_mm.n)
            V.reduce_max(ccs[:, 1:2], prow, AX)
            if len(cols) > 1:
                V.reduce_max(ccs[:, 2:3], prow2, AX)
            nc.gpsimd.wait_ge(s_dv.h, s_dv.n)
            G.dma_start(out=cc_in[idx], in_=ccs[:]).then_inc(s_gd.h, 16)
            s_gd.n += 16
            nc.gpsimd.wait_ge(s_gd.h, s_gd.n)
            G.collective_compute(
                "AllReduce", OP.max, replica_groups=[list(range(8))],
                ins=[cc_in[idx].opt()], outs=[cc_out[idx].opt()]
            ).then_inc(s_cc.h, 1)
            s_cc.n += 1
            nc.gpsimd.wait_ge(s_cc.h, s_cc.n)
            G.dma_start(out=ccs[:], in_=cc_out[idx]).then_inc(s_gd.h, 16)
            s_gd.n += 16
            nc.vector.wait_ge(s_gd.h, s_gd.n)
            finish_fn()
            nc.tensor.wait_ge(s_dv.h, s_dv.n)
            T.matmul(pbc8, ones_r[:], row[:1, 16:24], start=True, stop=True
                     ).then_inc(s_mm.h, 1)
            s_mm.n += 1
            nc.vector.wait_ge(s_mm.h, s_mm.n)
            nc.vector.wait_ge(s_sc.h, s_sc.n)
            V.tensor_copy(scl[:, 0:8], pbc8)
            return s_dv.n

        FSTI = sb("FSTI", [128, 2 * SL], F32, ctx)
        FOUT = sb("FOUT", [128, 2 * SL], F32, ctx)
        WCH = sb("WCH", [128, 2 * 2 * ECH * 128], BF16, ctx)
        stN = contextlib.ExitStack()
        NL = sb("NL", [128, 2 * ECH * S], BF16, stN)
        nh_s = NL[:, :ECH * S]
        nl_s = NL[:, ECH * S:]

        def fin1():
            V.tensor_scalar(c_(row, RSX), ccs[:, 1:2], c_(row, R32K), 1e-12,
                            OP.mult, OP.max)
            V.reciprocal(row[:1, 16:17], c_(row, RSX))
        emit_collective(0, (0,), fin1)

        SY.wait_ge(s_sx[0].h, s_sx[0].n)
        SY.wait_ge(s_sx[1].h, s_sx[1].n)
        p3_marks = []
        for tb in range(TB):
            dbk = tb % 2
            if len(p3_marks) >= 2:
                SY.wait_ge(s_dv.h, p3_marks[-2][0])
                SY.wait_ge(s_gp.h, p3_marks[-2][1])
            sfp = s_fp[dbk]
            SY.dma_start(out=FSTI[:, dbk * SL: dbk * SL + ECH * NB]
                         .rearrange("p (c n) -> p c n", n=NB),
                         in_=x1_d[:, :, tb * NB:(tb + 1) * NB].transpose([1, 0, 2])
                         ).then_inc(sfp.h, 16)
            sfp.n += 16

            nc.vector.wait_ge(sfp.h, sfp.n)
            src = FSTI[:, dbk * SL: dbk * SL + ECH * NB]
            st_ = FOUT[:, dbk * SL: dbk * SL + ECH * NB]
            st3 = st_.rearrange("p (c n) -> p c n", n=NB)
            nh_v = nh_s.rearrange("p (c s) -> p c s", s=S)[:, :, tb * NB:(tb + 1) * NB]
            nl_v = nl_s.rearrange("p (c s) -> p c s", s=S)[:, :, tb * NB:(tb + 1) * NB]
            V.tensor_scalar(st_, src, c_(scl, ISXc), MAGIC, OP.mult, OP.add)
            V.tensor_scalar(st_, st_, MAGIC, None, OP.subtract)
            V.tensor_copy(nh_v, st3)
            V.tensor_tensor(nl_v, st3, nh_v, OP.subtract)
            p3_marks.append((s_dv.n, s_gp.n))
        p3_done = s_dv.n

        SY.wait_ge(prep_ps[0].h, prep_ps[0].n)
        SY.wait_ge(prep_ps[1].h, prep_ps[1].n)
        wch_last = {0: 0, 1: 0}
        ev_marks = []

        def emit_w_chunk_load(which, ich):
            dbw = ich % 2
            if wch_last[dbw] > 0:
                SY.wait_ge(s_mm.h, wch_last[dbw])
            skp = s_kp[dbw]
            for half in range(2):
                src = whl_d[(which, half)][:, :, ich * 128:(ich + 1) * 128]
                SY.dma_start(
                    out=WCH[:, dbw * 2 * ECH * 128 + half * ECH * 128:
                           dbw * 2 * ECH * 128 + (half + 1) * ECH * 128]
                    .rearrange("p (c n) -> p c n", n=128),
                    in_=src.transpose([1, 0, 2])).then_inc(skp.h, 16)
            skp.n += 32
            return dbw, skp

        def emit_wmm_tiles(ich, dbw):
            for tb in range(TB):
                pb = (ich * TB + tb) % 4
                nc.tensor.wait_ge(s_dv.h, p3_marks[tb][0])
                if len(ev_marks) >= 4:
                    nc.tensor.wait_ge(s_sc.h, ev_marks[-4])
                w0 = WCH[:, dbw * 2 * ECH * 128:]
                for ec in range(ECH):
                    lwh = w0[:, ec * 128:(ec + 1) * 128]
                    lwl = w0[:, ECH * 128 + ec * 128: ECH * 128 + (ec + 1) * 128]
                    nh_v = nh_s[:, ec * S + tb * NB: ec * S + (tb + 1) * NB]
                    nl_v = nl_s[:, ec * S + tb * NB: ec * S + (tb + 1) * NB]
                    T.matmul(bank(pb), lwh, nh_v, start=(ec == 0), stop=False)
                    T.matmul(bank(pb), lwh, nl_v, start=False, stop=False)
                    last = T.matmul(bank(pb), lwl, nh_v, start=False,
                                    stop=(ec == ECH - 1))
                last.then_inc(s_mm.h, 1)
                s_mm.n += 1
                wch_last[dbw] = s_mm.n
                yield tb, pb

        wload_pend = {0: emit_w_chunk_load("g", 0)}
        for ich in range(ICH):
            dbw, skp = wload_pend.pop(ich)
            if ich + 1 < ICH:
                wload_pend[ich + 1] = emit_w_chunk_load("g", ich + 1)
            nc.tensor.wait_ge(skp.h, skp.n)
            sos = s_so[ich % 2]
            for tb, pb in emit_wmm_tiles(ich, dbw):
                nc.scalar.wait_ge(s_mm.h, s_mm.n)
                if tb == 0 and sos.n > 0:
                    nc.scalar.wait_ge(sos.h, sos.n)
                off = (ich % 2) * SL
                C.activation(FOUT[:, off + tb * NB: off + (tb + 1) * NB],
                             bank(pb), ACT.Copy)
                ev_marks.append(s_sc.n)
                nc.vector.wait_ge(s_sc.h, s_sc.n)
                V.reduce_max(c_(colb, 6),
                             FOUT[:, off + tb * NB: off + (tb + 1) * NB],
                             AX, apply_absolute_value=True)
                V.tensor_tensor(c_(colb, 2), c_(colb, 2), c_(colb, 6), OP.max)
            nc.scalar.wait_ge(s_dv.h, s_dv.n)
            C.dma_start(out=gtr_d[ich],
                        in_=FOUT[:, (ich % 2) * SL:(ich % 2) * SL + TB * NB]
                        ).then_inc(sos.h, 16)
            sos.n += 16

        def fin2g():
            V.tensor_tensor(c_(row, RT0), ccs[:, 1:2], c_(row, RSX), OP.mult)
            V.tensor_scalar(c_(row, RSG), c_(row, RT0), c_(row, R32K), 1e-12,
                            OP.mult, OP.max)
            V.reciprocal(c_(row, RT0), c_(row, RSG))
            V.tensor_tensor(row[:1, 17:18], c_(row, RSX), c_(row, RT0), OP.mult)
            V.tensor_copy(row[:1, 18:19], c_(row, RSG))
        emit_collective(1, (2,), fin2g)

        SY.wait_ge(s_so[0].h, s_so[0].n)
        SY.wait_ge(s_so[1].h, s_so[1].n)
        f2_spill_mark = [0]
        f4_state = {"vmark": 0, "spill": 0}

        def f4_units():
            for ich in range(ICH):
                src = FSTI[:, 0:S]
                gq = FSTI[:, SL:SL + S] if ich % 2 == 0 else FOUT[:, SL:SL + S]
                if f4_state["vmark"] > 0:
                    SY.wait_ge(s_dv.h, f4_state["vmark"])
                if f4_state["spill"] > 0:
                    SY.wait_ge(s_tr.h, f4_state["spill"])
                SY.dma_start(out=src, in_=gtr_d[ich]).then_inc(s_gn[0].h, 16)
                s_gn[0].n += 16

                nc.vector.wait_ge(s_gn[0].h, s_gn[0].n)
                V.tensor_scalar(gq, src, c_(scl, KGc), MAGIC, OP.mult, OP.add)
                V.tensor_scalar(gq, gq, MAGIC, None, OP.subtract)
                rmark = s_dv.n
                yield
                nc.scalar.wait_ge(s_dv.h, rmark)
                C.activation(src, gq, ACT.Sigmoid, scale=c_(scl, SGc))
                yield
                C.activation(src, src, ACT.Identity, bias=magic_c[:],
                             scale=65536.0)
                yield
                C.activation(src, src, ACT.Identity, bias=nmagic_c[:])
                smark = s_sc.n
                yield
                nc.vector.wait_ge(s_sc.h, smark)
                V.tensor_scalar(src, src, 65535.0, None, OP.min)
                V.tensor_tensor(src, src, gq, OP.mult)
                V.reduce_max(c_(colb, 6), src, AX, apply_absolute_value=True)
                V.tensor_tensor(c_(colb, 3), c_(colb, 3), c_(colb, 6), OP.max)
                f4_state["vmark"] = s_dv.n
                yield
                nc.gpsimd.wait_ge(s_dv.h, f4_state["vmark"])
                G.dma_start(out=tr_d[ich], in_=src).then_inc(s_tr.h, 16)
                s_tr.n += 16
                f4_state["spill"] = s_tr.n
                yield
        f4_iter = f4_units()

        wload_pend = {0: emit_w_chunk_load("u", 0)}
        for ich in range(ICH):
            dbw, skp = wload_pend.pop(ich)
            if ich + 1 < ICH:
                wload_pend[ich + 1] = emit_w_chunk_load("u", ich + 1)
            nc.tensor.wait_ge(skp.h, skp.n)
            for tb, pb in emit_wmm_tiles(ich, dbw):
                nc.scalar.wait_ge(s_mm.h, s_mm.n)
                if tb == 0 and f2_spill_mark[0] > 0:
                    nc.scalar.wait_ge(s_so[0].h, f2_spill_mark[0])
                C.activation(FOUT[:, tb * NB:(tb + 1) * NB], bank(pb), ACT.Copy)
                ev_marks.append(s_sc.n)
                nc.vector.wait_ge(s_sc.h, s_sc.n)
                V.reduce_max(c_(colb, 6), FOUT[:, tb * NB:(tb + 1) * NB],
                             AX, apply_absolute_value=True)
                V.tensor_tensor(c_(colb, 1), c_(colb, 1), c_(colb, 6), OP.max)
                if tb != 1 and tb != 2:
                    next(f4_iter, None)
            nc.scalar.wait_ge(s_dv.h, s_dv.n)
            C.dma_start(out=upr_d[ich], in_=FOUT[:, 0:TB * NB]
                        ).then_inc(s_so[0].h, 16)
            s_so[0].n += 16
            f2_spill_mark[0] = s_so[0].n
        for _ in f4_iter:
            pass

        def fin23():
            V.tensor_tensor(c_(row, RT0), ccs[:, 1:2], c_(row, RSX), OP.mult)
            V.tensor_scalar(c_(row, RSU), c_(row, RT0), c_(row, R32K), 1e-12,
                            OP.mult, OP.max)
            V.reciprocal(c_(row, RT0), c_(row, RSU))
            V.tensor_tensor(row[:1, 19:20], c_(row, RSX), c_(row, RT0), OP.mult)
            V.tensor_scalar(c_(row, RT0), ccs[:, 2:3], 1.0 / 65536.0, None,
                            OP.mult)
            V.tensor_tensor(c_(row, RT0), c_(row, RT0), c_(row, RSG), OP.mult)
            V.tensor_scalar(c_(row, RT0), c_(row, RT0), c_(row, R32K), None,
                            OP.mult)
            V.tensor_scalar(c_(row, RST), c_(row, RT0), 1e-12, None, OP.max)
            V.reciprocal(c_(row, RT0), c_(row, RST))
            V.tensor_scalar(c_(row, RT1), c_(row, RSG), 1.0 / 65536.0, None,
                            OP.mult)
            V.tensor_tensor(row[:1, 20:21], c_(row, RT1), c_(row, RT0), OP.mult)
        bc23 = emit_collective(2, (1, 3), fin23)

        stN.close()

        stF4 = contextlib.ExitStack()
        GN = sb("GN", [128, 2 * SL], F32, stF4)

        SY.wait_ge(s_so[0].h, s_so[0].n)
        SY.wait_ge(s_tr.h, s_tr.n)
        f5_dv = []
        f5_sc = []
        for ich in range(ICH):
            dbk = ich % 2
            if len(f5_dv) >= 2:
                SY.wait_ge(s_dv.h, f5_dv[-2])
            sfp = s_fp[dbk]
            sgn = s_gn[dbk]
            SY.dma_start(out=FSTI[:, dbk * SL: dbk * SL + S],
                         in_=upr_d[ich]).then_inc(sfp.h, 16)
            SY.dma_start(out=GN[:, dbk * SL: dbk * SL + S],
                         in_=tr_d[ich]).then_inc(sgn.h, 16)
            sfp.n += 16
            sgn.n += 16

            up = FSTI[:, dbk * SL: dbk * SL + S]
            tr = GN[:, dbk * SL: dbk * SL + S]
            ot = FOUT[:, dbk * SL: dbk * SL + S]
            nc.scalar.wait_ge(sfp.h, sfp.n)
            if ich == 0:
                nc.scalar.wait_ge(s_dv.h, bc23)
            C.activation(up, up, ACT.Identity, bias=magic_c[:], scale=c_(scl, KUc))
            C.activation(up, up, ACT.Identity, bias=nmagic_c[:])
            f5_sc.append(s_sc.n)

            nc.vector.wait_ge(sgn.h, sgn.n)
            sos = s_or[dbk]
            if ich == 0:
                nc.vector.wait_ge(s_so[0].h, s_so[0].n)
            if sos.n > 0:
                nc.vector.wait_ge(sos.h, sos.n)
            V.tensor_scalar(tr, tr, c_(scl, KTc), MAGIC, OP.mult, OP.add)
            V.tensor_scalar(tr, tr, MAGIC, None, OP.subtract)
            nc.vector.wait_ge(s_sc.h, f5_sc[-1])
            V.tensor_tensor(ot, tr, up, OP.mult)
            V.reduce_max(c_(colb, 6), ot, AX, apply_absolute_value=True)
            V.tensor_tensor(c_(colb, 4), c_(colb, 4), c_(colb, 6), OP.max)
            f5_dv.append(s_dv.n)

            nc.gpsimd.wait_ge(s_dv.h, s_dv.n)
            G.dma_start(out=or_d[ich], in_=FOUT[:, dbk * SL: dbk * SL + TB * NB]
                        ).then_inc(sos.h, 16)
            sos.n += 16

        stF4.close()


        def fin4():
            V.tensor_tensor(c_(row, RT0), c_(row, RST), c_(row, RSU), OP.mult)
            V.tensor_tensor(c_(row, RT1), ccs[:, 1:2], c_(row, RT0), OP.mult)
            V.tensor_scalar(c_(row, RSO), c_(row, RT1), c_(row, R32K), 1e-12,
                            OP.mult, OP.max)
            V.reciprocal(c_(row, RT1), c_(row, RSO))
            V.tensor_tensor(row[:1, 21:22], c_(row, RT0), c_(row, RT1), OP.mult)
            V.tensor_copy(row[:1, 22:23], c_(row, RSO))
        stF2 = contextlib.ExitStack()
        WD = sb("WD", [128, 2 * ICH * E], BF16, stF2)
        OTB = sb("OTB", [128, 2 * 2 * NB], BF16, stF2)
        XSB = sb("XSB", [128, 4 * NB], BF16, stF2)

        SY.wait_ge(s_or[0].h, s_or[0].n)
        SY.wait_ge(s_or[1].h, s_or[1].n)
        SY.dma_start(out=WD[:, :ICH * E].rearrange("p (c n) -> p c n", n=E),
                     in_=whl_d[("d", 0)].transpose([1, 0, 2])).then_inc(s_kp[0].h, 16)
        SY.dma_start(out=WD[:, ICH * E:].rearrange("p (c n) -> p c n", n=E),
                     in_=whl_d[("d", 1)].transpose([1, 0, 2])).then_inc(s_kp[0].h, 16)
        s_kp[0].n += 32

        emit_collective(3, (4,), fin4)
        nc.tensor.wait_ge(s_kp[0].h, s_kp[0].n)

        ot_marks = []
        xstat_marks = []
        f6_mark = {}
        f6_dv_seq = []
        ln2_mark = [0]
        ev_done = [0]

        def f6_x1_load(tb):
            if ev_done[0] > 0:
                SY.wait_ge(s_dv.h, ev_done[0])
            SY.dma_start(out=FSTI[:, SL:SL + ECH * NB]
                         .rearrange("p (c n) -> p c n", n=NB),
                         in_=x1_d[:, :, tb * NB:(tb + 1) * NB].transpose([1, 0, 2])
                         ).then_inc(s_x1b.h, 16)
            s_x1b.n += 16

        def f6_or_unit(tb, ich):
            dbk = ich % 2
            if len(f6_dv_seq) >= 2:
                SY.wait_ge(s_dv.h, f6_dv_seq[-2])
            sfp = s_fp[dbk]
            SY.dma_start(out=FSTI[:, dbk * NB:(dbk + 1) * NB],
                         in_=or_d[ich, :, tb * NB:(tb + 1) * NB]
                         ).then_inc(sfp.h, 16)
            sfp.n += 16

            nc.vector.wait_ge(sfp.h, sfp.n)
            if len(ot_marks) >= 2:
                nc.vector.wait_ge(s_mm.h, ot_marks[-2])
            src = FSTI[:, dbk * NB:(dbk + 1) * NB]
            oh = OTB[:, dbk * 2 * NB: dbk * 2 * NB + NB]
            ol = OTB[:, dbk * 2 * NB + NB: (dbk + 1) * 2 * NB]
            V.tensor_scalar(src, src, c_(scl, KOc), MAGIC, OP.mult, OP.add)
            V.tensor_scalar(src, src, MAGIC, None, OP.subtract)
            V.tensor_copy(oh, src)
            V.tensor_tensor(ol, src, oh, OP.subtract)
            f6_mark[(tb, ich)] = s_dv.n
            f6_dv_seq.append(s_dv.n)

        f6_x1_load(0)
        f6_or_unit(0, 0)
        f6_or_unit(0, 1)

        for tb in range(TB):
            x1_ready = s_x1b.n
            for ich in range(ICH):
                dbk = ich % 2
                if ich >= 2:
                    f6_or_unit(tb, ich)
                oh = OTB[:, dbk * 2 * NB: dbk * 2 * NB + NB]
                ol = OTB[:, dbk * 2 * NB + NB: (dbk + 1) * 2 * NB]
                nc.tensor.wait_ge(s_dv.h, f6_mark[(tb, ich)])
                if ich == 0 and ln2_mark[0] > 0:
                    nc.tensor.wait_ge(s_dv.h, ln2_mark[0])
                for hc in range(ECH):
                    lwh = WD[:, ich * E + hc * 128: ich * E + hc * 128 + 128]
                    lwl = WD[:, ICH * E + ich * E + hc * 128:
                             ICH * E + ich * E + hc * 128 + 128]
                    T.matmul(bank(hc), lwh, oh, start=(ich == 0), stop=False)
                    T.matmul(bank(hc), lwh, ol, start=False, stop=False)
                    last = T.matmul(bank(hc), lwl, oh, start=False,
                                    stop=(ich == ICH - 1))
                last.then_inc(s_mm.h, 1)
                s_mm.n += 1
                ot_marks.append(s_mm.n)

            nc.vector.wait_ge(s_mm.h, ot_marks[-1])
            nc.vector.wait_ge(s_x1b.h, x1_ready)
            if s_out.n > 0:
                nc.vector.wait_ge(s_out.h, s_out.n)
            for hc in range(ECH):
                xp = FOUT[:, hc * NB:(hc + 1) * NB]
                V.tensor_scalar(xp, bank(hc), c_(scl, SOc), None, OP.mult)
                V.tensor_tensor(xp, xp, FSTI[:, SL + hc * NB: SL + (hc + 1) * NB],
                                OP.add)
            f6_evac_done = s_dv.n
            ev_done[0] = f6_evac_done

            if tb + 1 < TB:
                f6_x1_load(tb + 1)
                f6_or_unit(tb + 1, 0)
                f6_or_unit(tb + 1, 1)

            for hc in range(ECH):
                nc.scalar.wait_ge(s_dv.h, f6_evac_done)
                if len(xstat_marks) >= 2:
                    nc.scalar.wait_ge(s_mm.h, xstat_marks[-2])
                xp = FOUT[:, hc * NB:(hc + 1) * NB]
                xb = XSB[:, (hc % 2) * 2 * NB:(hc % 2) * 2 * NB + NB]
                xq = XSB[:, (hc % 2) * 2 * NB + NB:((hc % 2) + 1) * 2 * NB]
                C.activation(xb, xp, ACT.Copy)
                C.activation(xq, xp, ACT.Square)

                nc.tensor.wait_ge(s_sc.h, s_sc.n)
                nc.tensor.wait_ge(s_dv.h, f6_evac_done)
                T.matmul(bank(0), ones_b[:], xb,
                         start=(hc == 0), stop=(hc == ECH - 1))
                T.matmul(bank(1), ones_b[:], xq,
                         start=(hc == 0), stop=(hc == ECH - 1)
                         ).then_inc(s_mm.h, 1)
                s_mm.n += 1
                xstat_marks.append(s_mm.n)

            nc.vector.wait_ge(s_mm.h, xstat_marks[-1])
            mn = FSTI[:, 2 * NB:3 * NB]
            vb = FSTI[:, 3 * NB:4 * NB]
            rs = FSTI[:, 4 * NB:5 * NB]
            V.tensor_scalar(mn, bank(0), 1.0 / E, None, OP.mult)
            V.tensor_scalar(vb, bank(1), 1.0 / E, None, OP.mult)
            V.tensor_tensor(rs, mn, mn, OP.mult)
            V.tensor_tensor(vb, vb, rs, OP.subtract)
            ln2_mark[0] = s_dv.n
            nc.scalar.wait_ge(s_dv.h, ln2_mark[0])
            C.activation(rs, vb, ACT.Sqrt, bias=eps_c[:])
            nc.vector.wait_ge(s_sc.h, s_sc.n)
            V.reciprocal(rs, rs)

            for hc in range(ECH):
                xp = FOUT[:, hc * NB:(hc + 1) * NB]
                V.tensor_tensor(xp, xp, mn, OP.subtract)
                V.tensor_tensor(xp, xp, rs, OP.mult)

                nc.scalar.wait_ge(s_dv.h, s_dv.n)
                C.dma_start(out=out_d[hc, :, tb * NB:(tb + 1) * NB],
                            in_=xp).then_inc(s_out.h, 16)
                s_out.n += 16

        nc.gpsimd.wait_ge(s_out.h, s_out.n)
        stF2.close()
    return nc


_CACHE = {}


def _get_program(S):
    if S not in _CACHE:
        _CACHE[S] = build_program(S)
    return _CACHE[S]


def stage_inputs(L0_attn_x, L0_attn_w_query, L0_attn_w_key, L0_attn_w_value,
                 L0_attn_w_proj, L0_attn_b_proj, L0_ffn_w_gate, L0_ffn_w_up,
                 L0_ffn_w_down):
    x = np.ascontiguousarray(np.asarray(L0_attn_x, dtype=np.float32))
    B, S, _E = x.shape

    def t8(w):
        return np.ascontiguousarray(np.asarray(w, np.float32).T.reshape(ECH, 128, E))

    shared = {
        "awT_q": t8(L0_attn_w_query), "awT_k": t8(L0_attn_w_key),
        "awT_v": t8(L0_attn_w_value), "awT_p": t8(L0_attn_w_proj),
        "bproj": np.ascontiguousarray(np.asarray(L0_attn_b_proj, np.float32)
                                      .reshape(ECH, 128).T),
        "wgT": np.ascontiguousarray(np.asarray(L0_ffn_w_gate, np.float32).T
                                    .reshape(ECH, 128, I3)),
        "wuT": np.ascontiguousarray(np.asarray(L0_ffn_w_up, np.float32).T
                                    .reshape(ECH, 128, I3)),
        "wdT": np.ascontiguousarray(np.asarray(L0_ffn_w_down, np.float32).T
                                    .reshape(ICH, 128, E)),
        "wg_nat": np.ascontiguousarray(np.asarray(L0_ffn_w_gate, np.float32)
                                       .reshape(ICH, 128, E)),
        "wu_nat": np.ascontiguousarray(np.asarray(L0_ffn_w_up, np.float32)
                                       .reshape(ICH, 128, E)),
        "wd_nat": np.ascontiguousarray(np.asarray(L0_ffn_w_down, np.float32)
                                       .reshape(ECH, 128, I3)),
        "ident32": np.eye(128, dtype=np.float32),
    }
    in_maps = []
    for c in range(B):
        m = dict(shared)
        m["xT"] = np.ascontiguousarray(x[c].T.reshape(ECH, 128, S))
        in_maps.append(m)
    return in_maps


def kernel(**inputs):
    in_maps = stage_inputs(**inputs)
    B = len(in_maps)
    S = in_maps[0]["xT"].shape[2]
    nc = _get_program(S)
    res = run_bass_kernel_spmd(nc, in_maps, core_ids=list(range(B)))
    outs = []
    for c in range(B):
        y = np.asarray(res.results[c]["out"]).reshape(E, S).T
        outs.append(y)
    return np.ascontiguousarray(np.stack(outs)).astype(np.float32)


# version 14
# speedup vs baseline: 1.0730x; 1.0107x over previous
import contextlib
import numpy as np

import concourse.bass as bass
import concourse.mybir as mybir
from concourse.bass_utils import run_bass_kernel_spmd

F32 = mybir.dt.float32
BF16 = mybir.dt.bfloat16
OP = mybir.AluOpType
ACT = mybir.ActivationFunctionType
AX = mybir.AxisListType.X

MAGIC = 12582912.0
EXP_SHIFT = -24.0
E = 1024
ECH = 8
I3 = 3072
ICH = 24
NB = 512


class Sem:
    def __init__(self, ctx, nc, name):
        self.h = ctx.enter_context(nc.semaphore(name))
        self.n = 0
        self.name = name


_CHAIN_OPS = {
    "tensor_copy", "tensor_tensor", "tensor_scalar", "tensor_single_scalar",
    "tensor_add", "tensor_sub", "tensor_mul", "tensor_max", "tensor_reduce",
    "reduce_max", "reduce_sum", "reciprocal", "memset", "activation", "iota",
    "scalar_tensor_tensor", "select", "tensor_scalar_add", "tensor_scalar_sub",
    "tensor_scalar_mul", "tensor_scalar_max", "tensor_scalar_min",
}


class Eng:

    def __init__(self, eng, sem):
        self._e = eng
        self._s = sem

    def __getattr__(self, name):
        fn = getattr(self._e, name)
        if name in _CHAIN_OPS:
            def wrap(*a, **k):
                if self._s.n > 0:
                    self._e.wait_ge(self._s.h, self._s.n)
                ins = fn(*a, **k)
                ins.then_inc(self._s.h, 1)
                self._s.n += 1
                return ins
            return wrap
        if name in ("dma_start", "collective_compute", "dma_start_transpose"):
            def wrapd(*a, **k):
                if self._s.n > 0:
                    self._e.wait_ge(self._s.h, self._s.n)
                return fn(*a, **k)
            return wrapd
        return fn


def build_program(S=4096):
    TB = S // NB
    KT = S // 128
    SL = max(ECH * NB, S)

    nc = bass.Bass("TRN2", target_bir_lowering=False)

    xT_in = nc.dram_tensor("xT", [ECH, 128, S], F32, kind="ExternalInput").ap()
    awT = {w: nc.dram_tensor(f"awT_{w}", [ECH, 128, E], F32, kind="ExternalInput").ap()
           for w in ("q", "k", "v", "p")}
    bias_in = nc.dram_tensor("bproj", [128, ECH], F32, kind="ExternalInput").ap()
    wT_in = {"g": nc.dram_tensor("wgT", [ECH, 128, I3], F32, kind="ExternalInput").ap(),
             "u": nc.dram_tensor("wuT", [ECH, 128, I3], F32, kind="ExternalInput").ap(),
             "d": nc.dram_tensor("wdT", [ICH, 128, E], F32, kind="ExternalInput").ap()}
    w_nat = {"g": nc.dram_tensor("wg_nat", [ICH, 128, E], F32, kind="ExternalInput").ap(),
             "u": nc.dram_tensor("wu_nat", [ICH, 128, E], F32, kind="ExternalInput").ap(),
             "d": nc.dram_tensor("wd_nat", [ECH, 128, I3], F32, kind="ExternalInput").ap()}
    ident_in = nc.dram_tensor("ident32", [128, 128], F32, kind="ExternalInput").ap()

    out_d = nc.dram_tensor("out", [ECH, 128, S], F32, kind="ExternalOutput").ap()

    qh_d = nc.dram_tensor("qh_d", [TB, ECH, 128, NB], BF16).ap()
    ql_d = nc.dram_tensor("ql_d", [TB, ECH, 128, NB], BF16).ap()
    kh_d = nc.dram_tensor("kh_d", [KT, ECH, 128, 128], BF16).ap()
    kl_d = nc.dram_tensor("kl_d", [KT, ECH, 128, 128], BF16).ap()
    vh_d = nc.dram_tensor("vh_d", [KT, 128, E], BF16).ap()
    vl_d = nc.dram_tensor("vl_d", [KT, 128, E], BF16).ap()
    x1_d = nc.dram_tensor("x1_d", [ECH, 128, S], F32).ap()
    whl_d = {("g", 0): nc.dram_tensor("wgh_d", [ECH, 128, I3], BF16).ap(),
             ("g", 1): nc.dram_tensor("wgl_d", [ECH, 128, I3], BF16).ap(),
             ("u", 0): nc.dram_tensor("wuh_d", [ECH, 128, I3], BF16).ap(),
             ("u", 1): nc.dram_tensor("wul_d", [ECH, 128, I3], BF16).ap(),
             ("d", 0): nc.dram_tensor("wdh_d", [ICH, 128, E], BF16).ap(),
             ("d", 1): nc.dram_tensor("wdl_d", [ICH, 128, E], BF16).ap()}
    sc_d = {"g": (nc.dram_tensor("sg_d", [32, I3], F32).ap(),
                  nc.dram_tensor("sgr_d", [32, I3], F32).ap()),
            "u": (nc.dram_tensor("su_d", [32, I3], F32).ap(),
                  nc.dram_tensor("sur_d", [32, I3], F32).ap()),
            "d": (nc.dram_tensor("sd_d", [96, E], F32).ap(),
                  nc.dram_tensor("sdr_d", [96, E], F32).ap())}
    upr_d = nc.dram_tensor("upr_d", [ICH, 128, S], F32).ap()
    gtr_d = nc.dram_tensor("gtr_d", [ICH, 128, S], F32).ap()
    tr_d = nc.dram_tensor("tr_d", [ICH, 128, S], F32).ap()
    or_d = nc.dram_tensor("or_d", [ICH, 128, S], F32).ap()
    cc_in = [nc.dram_tensor(f"cc_in{i}", [1, 16], F32).ap() for i in range(4)]
    cc_out = [nc.dram_tensor(f"cc_out{i}", [1, 16], F32).ap() for i in range(4)]

    ctx = contextlib.ExitStack()
    with ctx:
        def sb(name, shape, dtype, st=None):
            return (st or ctx).enter_context(nc.sbuf_tensor(name, shape, dtype))

        ones_b = sb("ones_b", [128, 128], BF16)
        ones_r = sb("ones_r", [1, 128], F32)
        ident = sb("ident", [128, 128], F32)
        scl = sb("scl", [128, 16], F32)
        row = sb("row", [1, 32], F32)
        colb = sb("colb", [128, 8], F32)
        eps_c = sb("eps_c", [128, 1], F32)
        bexp_c = sb("bexp_c", [128, 1], F32)
        magic_c = sb("magic_c", [128, 1], F32)
        nmagic_c = sb("nmagic_c", [128, 1], F32)
        bias_sb = sb("bias_sb", [128, ECH], F32)
        ccs = sb("ccs", [1, 16], F32)

        R127, RR127, R32K, RSX, RSU, RSG, RST, RSO, RT0, RT1 = range(10)
        ISXc, KGc, SGc, KUc, KTc, KOc, SOc = range(7)
        SW = {"q": 8, "k": 9, "v": 10, "p": 11}

        def c_(t, a):
            return t[:, a:a + 1]

        ps_all = ctx.enter_context(nc.psum_tensor("psall", [128, 8 * NB], F32))

        def bank(i):
            return ps_all[:, i * NB:(i + 1) * NB]

        prow = ps_all[:1, 7 * NB:7 * NB + 128]
        prow2 = ps_all[:1, 6 * NB:6 * NB + 128]
        pbc8 = ps_all[:, 7 * NB:7 * NB + 8]
        pbc2 = ps_all[:, 7 * NB:7 * NB + 2]

        s_in = Sem(ctx, nc, "s_in")
        s_x = Sem(ctx, nc, "s_x")
        s_pw = Sem(ctx, nc, "s_pw")
        s_kp = [Sem(ctx, nc, f"s_k{i}") for i in range(2)]
        s_vp = [Sem(ctx, nc, f"s_v{i}") for i in range(2)]
        s_fp = [Sem(ctx, nc, f"s_f{i}") for i in range(2)]
        s_gn = [Sem(ctx, nc, f"s_g{i}") for i in range(2)]
        s_x1b = Sem(ctx, nc, "s_x1b")
        s_q = Sem(ctx, nc, "s_q")
        s_ps = Sem(ctx, nc, "s_ps")
        s_sq = [Sem(ctx, nc, f"s_sq{i}") for i in range(2)]
        s_sx = [Sem(ctx, nc, f"s_sx{i}") for i in range(2)]
        s_so = [Sem(ctx, nc, f"s_so{i}") for i in range(2)]
        s_or = [Sem(ctx, nc, f"s_or{i}") for i in range(2)]
        s_tr = Sem(ctx, nc, "s_tr")
        s_gd = Sem(ctx, nc, "s_gd")
        s_mm = Sem(ctx, nc, "s_mm")
        s_dv = Sem(ctx, nc, "s_dv")
        s_sc = Sem(ctx, nc, "s_sc")
        s_cc = Sem(ctx, nc, "s_cc")
        s_gp = Sem(ctx, nc, "s_gp")
        s_out = Sem(ctx, nc, "s_out")

        block = ctx.enter_context(nc.Block())
        V = Eng(nc.vector, s_dv)
        C = Eng(nc.scalar, s_sc)
        G = Eng(nc.gpsimd, s_gp)
        T = nc.tensor
        SY = nc.sync

        stA = contextlib.ExitStack()
        xts = sb("xts", [128, ECH * NB], F32, stA)
        xspl = sb("xspl", [128, 2 * ECH * NB], BF16, stA)
        qst = sb("qst", [128, NB], F32, stA)
        qhl = sb("qhl", [128, 4 * NB], BF16, stA)
        awp = sb("awp", [128, ECH * E], BF16, stA)

        stQ = contextlib.ExitStack()
        aw3 = sb("aw3", [128, 3 * ECH * E], BF16, stQ)
        stW = contextlib.ExitStack()
        wstg = sb("wstg", [128, ECH * E], F32, stW)

        awq = {"q": aw3[:, 0:ECH * E], "k": aw3[:, ECH * E:2 * ECH * E],
               "v": aw3[:, 2 * ECH * E:], "p": awp[:]}

        xh_s, xl_s = xspl[:, :ECH * NB], xspl[:, ECH * NB:]

        SY.dma_start(out=ident[:], in_=ident_in).then_inc(s_in.h, 16)
        s_in.n += 16
        SY.dma_start(out=bias_sb[:], in_=bias_in).then_inc(s_in.h, 16)
        s_in.n += 16

        V.memset(ones_b[:], 1.0)
        V.memset(ones_r[:], 1.0)
        V.memset(eps_c[:], 1e-5)
        V.memset(bexp_c[:], EXP_SHIFT)
        V.memset(magic_c[:], MAGIC)
        V.memset(nmagic_c[:], -MAGIC)
        V.memset(colb[:], 0.0)
        V.memset(row[:], 0.0)
        V.memset(ccs[:], 0.0)
        V.memset(scl[:], 0.0)
        V.memset(c_(row, R127), 127.0)
        V.reciprocal(c_(row, RR127), c_(row, R127))
        V.memset(c_(row, R32K), 32767.0)
        V.reciprocal(c_(row, R32K), c_(row, R32K))
        V.memset(c_(colb, 5), 7.0)
        V.reciprocal(c_(colb, 5), c_(colb, 5))

        for j, w in enumerate(("q", "k", "v", "p")):
            SY.wait_ge(s_dv.h, s_dv.n)
            SY.dma_start(out=wstg[:].rearrange("p (c n) -> p c n", n=E),
                         in_=awT[w].transpose([1, 0, 2])).then_inc(s_in.h, 16)
            s_in.n += 16

            nc.vector.wait_ge(s_in.h, s_in.n)
            V.reduce_max(c_(colb, 6), wstg[:], AX, apply_absolute_value=True)
            nc.tensor.wait_ge(s_dv.h, s_dv.n)
            T.transpose(prow, colb[:, 6:7], ident[:]).then_inc(s_mm.h, 1)
            s_mm.n += 1

            nc.vector.wait_ge(s_mm.h, s_mm.n)
            V.reduce_max(c_(row, RT0), prow, AX)
            V.tensor_tensor(c_(row, RT0), c_(row, RT0), c_(row, RR127), OP.mult)
            V.reciprocal(c_(row, RT1), c_(row, RT0))
            nc.tensor.wait_ge(s_dv.h, s_dv.n)
            T.matmul(pbc2, ones_r[:], row[:1, RT0:RT0 + 2],
                     start=True, stop=True).then_inc(s_mm.h, 1)
            s_mm.n += 1

            nc.vector.wait_ge(s_mm.h, s_mm.n)
            V.tensor_copy(c_(scl, SW[w]), pbc2[:, 0:1])
            V.tensor_copy(c_(colb, 7), pbc2[:, 1:2])
            V.tensor_scalar(wstg[:], wstg[:], c_(colb, 7), MAGIC, OP.mult, OP.add)
            V.tensor_scalar(awq[w], wstg[:], MAGIC, None, OP.subtract)

        stW.close()
        SCLS = sb("SCLS", [128, 2304], F32, stQ)
        SCLR = sb("SCLR", [128, 2304], F32, stQ)
        prepw2 = sb("prepw2", [128, 2 * I3], F32, stQ)
        preps2 = sb("preps2", [128, 2 * I3], F32, stQ)
        prepr2 = sb("prepr2", [128, 2 * I3], F32, stQ)
        prepq2 = sb("prepq2", [128, I3], BF16, stQ)
        prepl2 = sb("prepl2", [128, I3], BF16, stQ)
        tbuf2 = sb("tbuf2", [96, 2 * 256], F32, stQ)

        prep_units = ([("scale", "g", c, 0, 0) for c in range(ICH)]
                      + [("scale", "u", c, 0, 0) for c in range(ICH)]
                      + [("scale", "d", c, 0, 0) for c in range(ECH)]
                      + [("tbatch", None, None, 0, 0)]
                      + [("deq", "g", c, off, I3 // 2) for c in range(ECH)
                         for off in (0, I3 // 2)]
                      + [("deq", "u", c, off, I3 // 2) for c in range(ECH)
                         for off in (0, I3 // 2)]
                      + [("deq", "d", c, off, E // 2) for c in range(ICH)
                         for off in (0, E // 2)])
        scl_off = {"g": 0, "u": 768, "d": 1536}

        prep_slot = [0]
        prep_vmark = {0: 0, 1: 0}
        prep_pw = [Sem(ctx, nc, "s_pw0"), Sem(ctx, nc, "s_pw1")]
        prep_ps = [Sem(ctx, nc, "s_ps0"), Sem(ctx, nc, "s_ps1")]

        def emit_prep_load(kind, which, c, off, ln):
            sl = prep_slot[0] & 1
            prep_slot[0] += 1
            prepw = prepw2[:, sl * I3:]
            preps = preps2[:, sl * I3:]
            prepr = prepr2[:, sl * I3:]
            spw = prep_pw[sl]
            sps = prep_ps[sl]
            if kind == "scale":
                n_in = E if which in ("g", "u") else I3
                if prep_vmark[sl] > 0:
                    SY.wait_ge(s_dv.h, prep_vmark[sl])
                SY.dma_start(out=prepw[:, :n_in], in_=w_nat[which][c]
                             ).then_inc(spw.h, 16)
                spw.n += 16
            else:
                sd_, srd_ = sc_d[which]
                if prep_vmark[sl] > 0:
                    SY.wait_ge(s_dv.h, prep_vmark[sl])
                if spw.n > 0:
                    SY.wait_ge(spw.h, spw.n)
                SY.dma_start(out=prepw[:, :ln], in_=wT_in[which][c][:, off:off + ln]
                             ).then_inc(spw.h, 16)
                sin = sd_[c * 4:c * 4 + 4, off:off + ln] \
                    .unsqueeze(1).broadcast_to([4, 32, ln])
                rin = srd_[c * 4:c * 4 + 4, off:off + ln] \
                    .unsqueeze(1).broadcast_to([4, 32, ln])
                SY.dma_start(out=preps[:, :ln], in_=sin).then_inc(spw.h, 16)
                SY.dma_start(out=prepr[:, :ln], in_=rin).then_inc(spw.h, 16)
                spw.n += 48
            return sl

        def emit_prep_compute(kind, which, c, off, ln, sl):
            prepw = prepw2[:, sl * I3:]
            preps = preps2[:, sl * I3:]
            prepr = prepr2[:, sl * I3:]
            prepq = prepq2[:, 0:]
            prepl = prepl2[:, 0:]
            spw = prep_pw[sl]
            sps = prep_ps[sl]
            if kind == "scale":
                n_in = E if which in ("g", "u") else I3
                nb_ = n_in // 32
                cols = scl_off[which] + c * nb_
                nc.vector.wait_ge(spw.h, spw.n)
                V.tensor_reduce(SCLS[:, cols:cols + nb_],
                                prepw[:, :n_in].rearrange("p (b c) -> p b c", c=32),
                                AX, OP.max, apply_absolute_value=True)
                V.tensor_scalar(SCLS[:, cols:cols + nb_], SCLS[:, cols:cols + nb_],
                                c_(colb, 5), 1e-12, OP.mult, OP.max)
                V.reciprocal(SCLR[:, cols:cols + nb_], SCLS[:, cols:cols + nb_])
                prep_vmark[sl] = s_dv.n
            else:
                nc.vector.wait_ge(spw.h, spw.n)
                if sps.n > 0:
                    nc.vector.wait_ge(sps.h, sps.n)
                V.tensor_tensor(prepw[:, :ln], prepw[:, :ln],
                                prepr[:, :ln], OP.mult)
                V.tensor_scalar(prepw[:, :ln], prepw[:, :ln], MAGIC, MAGIC,
                                OP.add, OP.subtract)
                V.tensor_tensor(prepw[:, :ln], prepw[:, :ln],
                                preps[:, :ln], OP.mult)
                osps = prep_ps[1 - sl]
                if osps.n > 0:
                    nc.vector.wait_ge(osps.h, osps.n)
                V.tensor_copy(prepq[:, :ln], prepw[:, :ln])
                V.tensor_tensor(prepl[:, :ln], prepw[:, :ln],
                                prepq[:, :ln], OP.subtract)
                prep_vmark[sl] = s_dv.n
                nc.scalar.wait_ge(s_dv.h, s_dv.n)
                for half in range(2):
                    src_t = (prepq if half == 0 else prepl)[:, :ln]
                    C.dma_start(out=whl_d[(which, half)][c][:, off:off + ln],
                                in_=src_t).then_inc(sps.h, 16)
                sps.n += 32

        def emit_tbatch():
                nc.tensor.wait_ge(s_dv.h, s_dv.n)
                jobs = []
                for wh in ("g", "u", "d"):
                    nchunks = ICH if wh in ("g", "u") else ECH
                    nb2 = 32 if wh in ("g", "u") else 96
                    for cc2 in range(nchunks):
                        for half2, (srcb, dstb) in enumerate(
                                ((SCLS, sc_d[wh][0]), (SCLR, sc_d[wh][1]))):
                            jobs.append((wh, cc2, nb2, srcb, dstb))
                tsems = [prep_ps[0], prep_ps[1], prep_pw[0], prep_pw[1]]
                vmarks = []
                smarks = []
                for ji, (wh, cc2, nb2, srcb, dstb) in enumerate(jobs):
                    cols = scl_off[wh] + cc2 * nb2
                    pslot = ji % 2
                    preg = ps_all[:nb2, (6 + pslot) * NB:(6 + pslot) * NB + 128]
                    if len(vmarks) >= 2:
                        nc.tensor.wait_ge(s_dv.h, vmarks[-2])
                    T.transpose(preg, srcb[:, cols:cols + nb2],
                                ident[:]).then_inc(s_mm.h, 1)
                    s_mm.n += 1
                    tslot = ji % 4
                    ts_ = tsems[tslot]
                    nc.vector.wait_ge(s_mm.h, s_mm.n)
                    if len(smarks) >= 4:
                        nc.vector.wait_ge(ts_.h, smarks[-4])
                    V.tensor_copy(tbuf2[:nb2, tslot * 128:(tslot + 1) * 128], preg)
                    vmarks.append(s_dv.n)
                    nc.scalar.wait_ge(s_dv.h, s_dv.n)
                    C.dma_start(out=dstb[:nb2, cc2 * 128:(cc2 + 1) * 128],
                                in_=tbuf2[:nb2, tslot * 128:(tslot + 1) * 128]
                                ).then_inc(ts_.h, 16)
                    ts_.n += 16
                    smarks.append(ts_.n)

        prep_iter = iter(prep_units)
        prep_pending = [None]

        def flush_prep():
            if prep_pending[0] is not None:
                emit_prep_compute(*prep_pending[0])
                prep_pending[0] = None

        def take_prep(kn):
            for _ in range(kn):
                u = next(prep_iter, None)
                if u is None:
                    continue
                if u[0] == "tbatch":
                    flush_prep()
                    emit_tbatch()
                else:
                    sl = emit_prep_load(*u)
                    flush_prep()
                    prep_pending[0] = u + (sl,)

        qkv_evac_marks = []
        qkv_spill_marks = []

        for tb in range(TB):
            SY.wait_ge(s_dv.h, s_dv.n)
            SY.dma_start(out=xts[:].rearrange("p (c n) -> p c n", n=NB),
                         in_=xT_in[:, :, tb * NB:(tb + 1) * NB].transpose([1, 0, 2])
                         ).then_inc(s_x.h, 16)
            s_x.n += 16

            nc.vector.wait_ge(s_x.h, s_x.n)
            nc.vector.wait_ge(s_mm.h, s_mm.n)
            V.tensor_copy(xh_s, xts[:])
            V.tensor_tensor(xl_s, xts[:], xh_s, OP.subtract)
            sd_split = s_dv.n

            tiles = [("q", oc, None) for oc in range(ECH)] \
                + [("k", oc, None) for oc in range(ECH)] \
                + [("v", t_, h_) for t_ in range(NB // 128) for h_ in range(2)]
            for idx, (w, a, b_) in enumerate(tiles):
                pb = idx % 4
                nc.tensor.wait_ge(s_dv.h, sd_split)
                if len(qkv_evac_marks) >= 4:
                    nc.tensor.wait_ge(s_dv.h, qkv_evac_marks[-4])
                if w in ("q", "k"):
                    for ec in range(ECH):
                        lw = awq[w][:, ec * E + a * 128: ec * E + a * 128 + 128]
                        T.matmul(bank(pb), lw, xh_s[:, ec * NB:(ec + 1) * NB],
                                 start=(ec == 0), stop=False)
                        last = T.matmul(bank(pb), lw, xl_s[:, ec * NB:(ec + 1) * NB],
                                        start=False, stop=(ec == ECH - 1))
                else:
                    for ec in range(ECH):
                        lwh = xh_s[:, ec * NB + a * 128: ec * NB + a * 128 + 128]
                        lwl = xl_s[:, ec * NB + a * 128: ec * NB + a * 128 + 128]
                        rh = awq["v"][:, ec * E + b_ * NB: ec * E + (b_ + 1) * NB]
                        T.matmul(bank(pb), lwh, rh, start=(ec == 0), stop=False)
                        last = T.matmul(bank(pb), lwl, rh, start=False,
                                        stop=(ec == ECH - 1))
                last.then_inc(s_mm.h, 1)
                s_mm.n += 1

                nc.vector.wait_ge(s_mm.h, s_mm.n)
                sqs = s_sq[pb % 2]
                if sqs.n > 0:
                    nc.vector.wait_ge(sqs.h, sqs.n)
                st_ = qst[:, 0:NB]
                hi = qhl[:, (pb % 2) * 2 * NB:(pb % 2) * 2 * NB + NB]
                lo = qhl[:, (pb % 2) * 2 * NB + NB:(pb % 2) * 2 * NB + 2 * NB]
                V.tensor_scalar(st_, bank(pb), c_(scl, SW[w]), None, OP.mult)
                V.tensor_copy(hi, st_)
                V.tensor_tensor(lo, st_, hi, OP.subtract)
                qkv_evac_marks.append(s_dv.n)

                nc.scalar.wait_ge(s_dv.h, s_dv.n)
                if w == "q":
                    C.dma_start(out=qh_d[tb, a], in_=hi).then_inc(sqs.h, 16)
                    C.dma_start(out=ql_d[tb, a], in_=lo).then_inc(sqs.h, 16)
                elif w == "k":
                    KPT = NB // 128
                    C.dma_start(out=kh_d[tb * KPT:(tb + 1) * KPT, a]
                                .transpose([1, 0, 2]),
                                in_=hi.rearrange("p (t r) -> p t r", r=128)
                                ).then_inc(sqs.h, 16)
                    C.dma_start(out=kl_d[tb * KPT:(tb + 1) * KPT, a]
                                .transpose([1, 0, 2]),
                                in_=lo.rearrange("p (t r) -> p t r", r=128)
                                ).then_inc(sqs.h, 16)
                else:
                    kt_ = tb * (NB // 128) + a
                    C.dma_start(out=vh_d[kt_, :, b_ * NB:(b_ + 1) * NB],
                                in_=hi).then_inc(sqs.h, 16)
                    C.dma_start(out=vl_d[kt_, :, b_ * NB:(b_ + 1) * NB],
                                in_=lo).then_inc(sqs.h, 16)
                sqs.n += 32
                qkv_spill_marks.append((pb % 2, sqs.n))
                if idx % 2 == 1:
                    take_prep(1)
                elif idx % 4 == 2:
                    take_prep(1)

        take_prep(len(prep_units))
        flush_prep()
        stQ.close()

        stT = contextlib.ExitStack()
        SS = sb("SS", [128, 2 * KT * NB], BF16, stT)
        QT = sb("QT", [128, 4 * ECH * NB], BF16, stT)
        KV = sb("KV", [128, 2 * 2 * ECH * 128], BF16, stT)
        VV = sb("VV", [128, 2 * 2 * E], BF16, stT)
        S32 = sb("S32", [128, 3 * NB], F32, stT)
        SUMR = sb("SUMR", [128, NB], F32, stT)
        X1P = sb("X1P", [128, ECH * NB], F32, stT)
        XB = sb("XB", [128, 2 * ECH * NB], BF16, stT)
        MEAN = S32[:, 0:NB]
        VARB = S32[:, NB:2 * NB]
        RSTD = S32[:, 2 * NB:3 * NB]

        ah_s, al_s = xh_s, xl_s

        def sh_t(kt):
            return SS[:, kt * NB:(kt + 1) * NB]

        def sl_t(kt):
            return SS[:, KT * NB + kt * NB:KT * NB + (kt + 1) * NB]

        SY.wait_ge(s_sq[0].h, s_sq[0].n)
        SY.wait_ge(s_sq[1].h, s_sq[1].n)
        SY.wait_ge(s_mm.h, s_mm.n)
        SY.wait_ge(s_dv.h, s_dv.n)
        SY.wait_ge(s_sc.h, s_sc.n)
        SY.wait_ge(s_gp.h, s_gp.n)
        SY.wait_ge(prep_ps[0].h, prep_ps[0].n)
        SY.wait_ge(prep_ps[1].h, prep_ps[1].n)
        SY.wait_ge(prep_pw[0].h, prep_pw[0].n)
        SY.wait_ge(prep_pw[1].h, prep_pw[1].n)
        q_marks = {}
        qscore_end = {}
        bmm_end = {}

        def load_q(qb_):
            qsl_ = qb_ & 1
            if qb_ >= 2:
                SY.wait_ge(s_mm.h, qscore_end[qb_ - 2])
            qbase = qsl_ * 2 * ECH * NB
            SY.dma_start(out=QT[:, qbase:qbase + ECH * NB]
                         .rearrange("p (c n) -> p c n", n=NB),
                         in_=qh_d[qb_].transpose([1, 0, 2])).then_inc(s_q.h, 16)
            SY.dma_start(out=QT[:, qbase + ECH * NB:qbase + 2 * ECH * NB]
                         .rearrange("p (c n) -> p c n", n=NB),
                         in_=ql_d[qb_].transpose([1, 0, 2])).then_inc(s_q.h, 16)
            s_q.n += 32
            q_marks[qb_] = s_q.n

        prev_evac = 0
        x1v_mark = [0]
        for qb in range(TB):
            load_q(qb)
            QTs = QT[:, (qb & 1) * 2 * ECH * NB:]
            q_ready = q_marks[qb]
            if qb >= 1:
                SY.wait_ge(s_mm.h, qscore_end[qb - 1])

            base_kmm = s_mm.n
            k_marks = []
            exp_marks = []
            sl_marks = []
            for kt in range(KT):
                dbk = kt % 2
                if kt >= 2:
                    SY.wait_ge(s_mm.h, base_kmm + kt - 1)
                skp = s_kp[dbk]
                SY.dma_start(
                    out=KV[:, dbk * 2 * ECH * 128: dbk * 2 * ECH * 128 + ECH * 128]
                    .rearrange("p (c n) -> p c n", n=128),
                    in_=kh_d[kt].transpose([1, 0, 2])).then_inc(skp.h, 16)
                SY.dma_start(
                    out=KV[:, dbk * 2 * ECH * 128 + ECH * 128:
                           (dbk + 1) * 2 * ECH * 128]
                    .rearrange("p (c n) -> p c n", n=128),
                    in_=kl_d[kt].transpose([1, 0, 2])).then_inc(skp.h, 16)
                skp.n += 32
                k_marks.append((dbk, skp.n))

            for kt in range(KT):
                dbk = kt % 2
                sbk = 6 + (kt % 2)
                nc.tensor.wait_ge(s_q.h, q_ready)
                if kt == 0 and x1v_mark[0] > 0:
                    nc.scalar.wait_ge(s_dv.h, x1v_mark[0])
                nc.tensor.wait_ge(s_kp[k_marks[kt][0]].h, k_marks[kt][1])
                if kt < 2 and prev_evac > 0:
                    nc.tensor.wait_ge(s_dv.h, prev_evac)
                if len(exp_marks) >= 2:
                    nc.tensor.wait_ge(s_sc.h, exp_marks[-2])
                kh0 = KV[:, dbk * 2 * ECH * 128:]
                for ec in range(ECH):
                    lwh = kh0[:, ec * 128:(ec + 1) * 128]
                    lwl = kh0[:, ECH * 128 + ec * 128: ECH * 128 + (ec + 1) * 128]
                    T.matmul(bank(sbk), lwh, QTs[:, ec * NB:(ec + 1) * NB],
                             start=(ec == 0), stop=False)
                    T.matmul(bank(sbk), lwh,
                             QTs[:, ECH * NB + ec * NB: ECH * NB + (ec + 1) * NB],
                             start=False, stop=False)
                    last = T.matmul(bank(sbk), lwl, QTs[:, ec * NB:(ec + 1) * NB],
                                    start=False, stop=(ec == ECH - 1))
                last.then_inc(s_mm.h, 1)
                s_mm.n += 1

                if kt >= 1:
                    nc.tensor.wait_ge(s_dv.h, s_dv.n)
                    T.matmul(bank(5), ones_b[:], sh_t(kt - 1),
                             start=(kt == 1), stop=False)
                    T.matmul(bank(5), ones_b[:], sl_t(kt - 1),
                             start=False, stop=False)

                nc.scalar.wait_ge(s_mm.h, s_mm.n)
                if len(sl_marks) >= 2:
                    nc.scalar.wait_ge(s_dv.h, sl_marks[-2])
                C.activation(S32[:, dbk * NB:(dbk + 1) * NB], bank(sbk), ACT.Exp,
                             bias=bexp_c[:], scale=0.125)
                C.activation(sh_t(kt), bank(sbk), ACT.Exp,
                             bias=bexp_c[:], scale=0.125)
                exp_marks.append(s_sc.n)

                nc.vector.wait_ge(s_sc.h, s_sc.n)
                V.tensor_tensor(sl_t(kt), S32[:, dbk * NB:(dbk + 1) * NB],
                                sh_t(kt), OP.subtract)
                sl_marks.append(s_dv.n)

            qscore_end[qb] = s_mm.n
            nc.tensor.wait_ge(s_dv.h, s_dv.n)
            T.matmul(bank(5), ones_b[:], sh_t(KT - 1),
                     start=(KT == 1), stop=False)
            T.matmul(bank(5), ones_b[:], sl_t(KT - 1),
                     start=False, stop=True).then_inc(s_mm.h, 1)
            s_mm.n += 1

            nc.vector.wait_ge(s_mm.h, s_mm.n)
            V.reciprocal(SUMR[:], bank(5))
            sumr_done = s_dv.n

            if qb >= 1:
                SY.wait_ge(s_mm.h, bmm_end[qb - 1])
            base_bmm = s_mm.n
            for kt in range(KT):
                dbk = kt % 2
                if kt >= 2:
                    SY.wait_ge(s_mm.h, base_bmm + kt - 1)
                svp = s_vp[dbk]
                SY.dma_start(out=VV[:, dbk * 2 * E: dbk * 2 * E + E],
                             in_=vh_d[kt]).then_inc(svp.h, 16)
                SY.dma_start(out=VV[:, dbk * 2 * E + E: (dbk + 1) * 2 * E],
                             in_=vl_d[kt]).then_inc(svp.h, 16)
                svp.n += 32

                nc.tensor.wait_ge(svp.h, svp.n)
                if kt == 0:
                    nc.tensor.wait_ge(s_dv.h, sumr_done)
                    nc.tensor.wait_ge(s_sc.h, s_sc.n)
                vh0 = VV[:, dbk * 2 * E:]
                for ec in range(ECH):
                    lwh = vh0[:, ec * 128:(ec + 1) * 128]
                    lwl = vh0[:, E + ec * 128: E + (ec + 1) * 128]
                    T.matmul(bank(ec), lwh, sh_t(kt), start=(kt == 0), stop=False)
                    T.matmul(bank(ec), lwh, sl_t(kt), start=False, stop=False)
                    last = T.matmul(bank(ec), lwl, sh_t(kt), start=False,
                                    stop=(kt == KT - 1))
                last.then_inc(s_mm.h, 1)
                s_mm.n += 1

            bmm_end[qb] = s_mm.n
            nc.vector.wait_ge(s_mm.h, s_mm.n)
            for ec in range(ECH):
                V.tensor_copy(ah_s[:, ec * NB:(ec + 1) * NB], bank(ec))
                V.tensor_tensor(al_s[:, ec * NB:(ec + 1) * NB], bank(ec),
                                ah_s[:, ec * NB:(ec + 1) * NB], OP.subtract)
            evac_done = s_dv.n
            prev_evac = evac_done

            SY.wait_ge(s_dv.h, evac_done)
            SY.dma_start(out=xts[:].rearrange("p (c n) -> p c n", n=NB),
                         in_=xT_in[:, :, qb * NB:(qb + 1) * NB].transpose([1, 0, 2])
                         ).then_inc(s_x.h, 16)
            s_x.n += 16
            xts_ready = s_x.n

            proj_evac_marks = []
            for oc in range(ECH):
                pb = oc % 2
                nc.tensor.wait_ge(s_dv.h, evac_done)
                if len(proj_evac_marks) >= 2:
                    nc.tensor.wait_ge(s_dv.h, proj_evac_marks[-2])
                for ec in range(ECH):
                    lw = awq["p"][:, ec * E + oc * 128: ec * E + oc * 128 + 128]
                    T.matmul(bank(pb), lw, ah_s[:, ec * NB:(ec + 1) * NB],
                             start=(ec == 0), stop=False)
                    last = T.matmul(bank(pb), lw, al_s[:, ec * NB:(ec + 1) * NB],
                                    start=False, stop=(ec == ECH - 1))
                last.then_inc(s_mm.h, 1)
                s_mm.n += 1

                nc.vector.wait_ge(s_mm.h, s_mm.n)
                if oc == 0:
                    nc.vector.wait_ge(s_x.h, xts_ready)
                xp = X1P[:, oc * NB:(oc + 1) * NB]
                sxs2 = s_sx[oc % 2]
                if sxs2.n > 0:
                    nc.vector.wait_ge(sxs2.h, sxs2.n)
                V.tensor_tensor(xp, bank(pb), SUMR[:], OP.mult)
                V.tensor_scalar(xp, xp, c_(scl, SW["p"]),
                                c_(bias_sb, oc), OP.mult, OP.add)
                V.tensor_tensor(xp, xp, xts[:, oc * NB:(oc + 1) * NB], OP.add)
                V.tensor_copy(XB[:, oc * NB:(oc + 1) * NB], xp)
                proj_evac_marks.append(s_dv.n)

                nc.scalar.wait_ge(s_dv.h, s_dv.n)
                C.activation(XB[:, ECH * NB + oc * NB: ECH * NB + (oc + 1) * NB],
                             XB[:, oc * NB:(oc + 1) * NB], ACT.Square)

                nc.tensor.wait_ge(s_sc.h, s_sc.n)
                T.matmul(bank(2), ones_b[:], XB[:, oc * NB:(oc + 1) * NB],
                         start=(oc == 0), stop=(oc == ECH - 1))
                mm2 = T.matmul(bank(3), ones_b[:],
                               XB[:, ECH * NB + oc * NB: ECH * NB + (oc + 1) * NB],
                               start=(oc == 0), stop=(oc == ECH - 1))
                if oc == ECH - 1:
                    mm2.then_inc(s_mm.h, 1)
                    s_mm.n += 1

            nc.vector.wait_ge(s_mm.h, s_mm.n)
            V.tensor_scalar(MEAN[:], bank(2), 1.0 / E, None, OP.mult)
            V.tensor_scalar(VARB[:], bank(3), 1.0 / E, None, OP.mult)
            V.tensor_tensor(RSTD[:], MEAN[:], MEAN[:], OP.mult)
            V.tensor_tensor(VARB[:], VARB[:], RSTD[:], OP.subtract)
            nc.scalar.wait_ge(s_dv.h, s_dv.n)
            C.activation(RSTD[:], VARB[:], ACT.Sqrt, bias=eps_c[:])
            nc.vector.wait_ge(s_sc.h, s_sc.n)
            V.reciprocal(RSTD[:], RSTD[:])

            for oc in range(ECH):
                sxs = s_sx[oc % 2]
                xp = X1P[:, oc * NB:(oc + 1) * NB]
                V.tensor_tensor(xp, xp, MEAN[:], OP.subtract)
                V.tensor_tensor(xp, xp, RSTD[:], OP.mult)
                V.reduce_max(c_(colb, 6), xp, AX, apply_absolute_value=True)
                V.tensor_tensor(c_(colb, 0), c_(colb, 0), c_(colb, 6), OP.max)

                nc.scalar.wait_ge(s_dv.h, s_dv.n)
                C.dma_start(out=x1_d[oc, :, qb * NB:(qb + 1) * NB],
                            in_=xp).then_inc(sxs.h, 16)
                sxs.n += 16
            x1v_mark[0] = s_dv.n

        stT.close()
        stA.close()

        def emit_collective(idx, cols, finish_fn):
            nc.tensor.wait_ge(s_dv.h, s_dv.n)
            T.transpose(prow, colb[:, cols[0]:cols[0] + 1], ident[:]
                        ).then_inc(s_mm.h, 1)
            s_mm.n += 1
            if len(cols) > 1:
                T.transpose(prow2, colb[:, cols[1]:cols[1] + 1], ident[:]
                            ).then_inc(s_mm.h, 1)
                s_mm.n += 1
            nc.vector.wait_ge(s_mm.h, s_mm.n)
            V.reduce_max(ccs[:, 1:2], prow, AX)
            if len(cols) > 1:
                V.reduce_max(ccs[:, 2:3], prow2, AX)
            nc.gpsimd.wait_ge(s_dv.h, s_dv.n)
            G.dma_start(out=cc_in[idx], in_=ccs[:]).then_inc(s_gd.h, 16)
            s_gd.n += 16
            nc.gpsimd.wait_ge(s_gd.h, s_gd.n)
            G.collective_compute(
                "AllReduce", OP.max, replica_groups=[list(range(8))],
                ins=[cc_in[idx].opt()], outs=[cc_out[idx].opt()]
            ).then_inc(s_cc.h, 1)
            s_cc.n += 1
            nc.gpsimd.wait_ge(s_cc.h, s_cc.n)
            G.dma_start(out=ccs[:], in_=cc_out[idx]).then_inc(s_gd.h, 16)
            s_gd.n += 16
            nc.vector.wait_ge(s_gd.h, s_gd.n)
            finish_fn()
            nc.tensor.wait_ge(s_dv.h, s_dv.n)
            T.matmul(pbc8, ones_r[:], row[:1, 16:24], start=True, stop=True
                     ).then_inc(s_mm.h, 1)
            s_mm.n += 1
            nc.vector.wait_ge(s_mm.h, s_mm.n)
            nc.vector.wait_ge(s_sc.h, s_sc.n)
            V.tensor_copy(scl[:, 0:8], pbc8)
            return s_dv.n

        FSTI = sb("FSTI", [128, 2 * SL], F32, ctx)
        FOUT = sb("FOUT", [128, 2 * SL], F32, ctx)
        WCH = sb("WCH", [128, 2 * 2 * ECH * 128], BF16, ctx)
        stN = contextlib.ExitStack()
        NL = sb("NL", [128, 2 * ECH * S], BF16, stN)
        nh_s = NL[:, :ECH * S]
        nl_s = NL[:, ECH * S:]

        def fin1():
            V.tensor_scalar(c_(row, RSX), ccs[:, 1:2], c_(row, R32K), 1e-12,
                            OP.mult, OP.max)
            V.reciprocal(row[:1, 16:17], c_(row, RSX))
        emit_collective(0, (0,), fin1)

        SY.wait_ge(s_sx[0].h, s_sx[0].n)
        SY.wait_ge(s_sx[1].h, s_sx[1].n)
        p3_marks = []
        for tb in range(TB):
            dbk = tb % 2
            if len(p3_marks) >= 2:
                SY.wait_ge(s_dv.h, p3_marks[-2][0])
                SY.wait_ge(s_gp.h, p3_marks[-2][1])
            sfp = s_fp[dbk]
            SY.dma_start(out=FSTI[:, dbk * SL: dbk * SL + ECH * NB]
                         .rearrange("p (c n) -> p c n", n=NB),
                         in_=x1_d[:, :, tb * NB:(tb + 1) * NB].transpose([1, 0, 2])
                         ).then_inc(sfp.h, 16)
            sfp.n += 16

            nc.vector.wait_ge(sfp.h, sfp.n)
            src = FSTI[:, dbk * SL: dbk * SL + ECH * NB]
            st_ = FOUT[:, dbk * SL: dbk * SL + ECH * NB]
            st3 = st_.rearrange("p (c n) -> p c n", n=NB)
            nh_v = nh_s.rearrange("p (c s) -> p c s", s=S)[:, :, tb * NB:(tb + 1) * NB]
            nl_v = nl_s.rearrange("p (c s) -> p c s", s=S)[:, :, tb * NB:(tb + 1) * NB]
            V.tensor_scalar(st_, src, c_(scl, ISXc), MAGIC, OP.mult, OP.add)
            V.tensor_scalar(st_, st_, MAGIC, None, OP.subtract)
            V.tensor_copy(nh_v, st3)
            V.tensor_tensor(nl_v, st3, nh_v, OP.subtract)
            p3_marks.append((s_dv.n, s_gp.n))
        p3_done = s_dv.n

        SY.wait_ge(prep_ps[0].h, prep_ps[0].n)
        SY.wait_ge(prep_ps[1].h, prep_ps[1].n)
        wch_last = {0: 0, 1: 0}
        ev_marks = []

        def emit_w_chunk_load(which, ich):
            dbw = ich % 2
            if wch_last[dbw] > 0:
                SY.wait_ge(s_mm.h, wch_last[dbw])
            skp = s_kp[dbw]
            for half in range(2):
                src = whl_d[(which, half)][:, :, ich * 128:(ich + 1) * 128]
                SY.dma_start(
                    out=WCH[:, dbw * 2 * ECH * 128 + half * ECH * 128:
                           dbw * 2 * ECH * 128 + (half + 1) * ECH * 128]
                    .rearrange("p (c n) -> p c n", n=128),
                    in_=src.transpose([1, 0, 2])).then_inc(skp.h, 16)
            skp.n += 32
            return dbw, skp

        def emit_wmm_tiles(ich, dbw):
            for tb in range(TB):
                pb = (ich * TB + tb) % 4
                nc.tensor.wait_ge(s_dv.h, p3_marks[tb][0])
                if len(ev_marks) >= 4:
                    nc.tensor.wait_ge(s_sc.h, ev_marks[-4])
                w0 = WCH[:, dbw * 2 * ECH * 128:]
                for ec in range(ECH):
                    lwh = w0[:, ec * 128:(ec + 1) * 128]
                    lwl = w0[:, ECH * 128 + ec * 128: ECH * 128 + (ec + 1) * 128]
                    nh_v = nh_s[:, ec * S + tb * NB: ec * S + (tb + 1) * NB]
                    nl_v = nl_s[:, ec * S + tb * NB: ec * S + (tb + 1) * NB]
                    T.matmul(bank(pb), lwh, nh_v, start=(ec == 0), stop=False)
                    T.matmul(bank(pb), lwh, nl_v, start=False, stop=False)
                    last = T.matmul(bank(pb), lwl, nh_v, start=False,
                                    stop=(ec == ECH - 1))
                last.then_inc(s_mm.h, 1)
                s_mm.n += 1
                wch_last[dbw] = s_mm.n
                yield tb, pb

        wload_pend = {0: emit_w_chunk_load("g", 0)}
        for ich in range(ICH):
            dbw, skp = wload_pend.pop(ich)
            if ich + 1 < ICH:
                wload_pend[ich + 1] = emit_w_chunk_load("g", ich + 1)
            nc.tensor.wait_ge(skp.h, skp.n)
            sos = s_so[ich % 2]
            for tb, pb in emit_wmm_tiles(ich, dbw):
                nc.scalar.wait_ge(s_mm.h, s_mm.n)
                if tb == 0 and sos.n > 0:
                    nc.scalar.wait_ge(sos.h, sos.n)
                off = (ich % 2) * SL
                C.activation(FOUT[:, off + tb * NB: off + (tb + 1) * NB],
                             bank(pb), ACT.Copy)
                ev_marks.append(s_sc.n)
                nc.vector.wait_ge(s_sc.h, s_sc.n)
                V.reduce_max(c_(colb, 6),
                             FOUT[:, off + tb * NB: off + (tb + 1) * NB],
                             AX, apply_absolute_value=True)
                V.tensor_tensor(c_(colb, 2), c_(colb, 2), c_(colb, 6), OP.max)
            nc.scalar.wait_ge(s_dv.h, s_dv.n)
            C.dma_start(out=gtr_d[ich],
                        in_=FOUT[:, (ich % 2) * SL:(ich % 2) * SL + TB * NB]
                        ).then_inc(sos.h, 16)
            sos.n += 16

        def fin2g():
            V.tensor_tensor(c_(row, RT0), ccs[:, 1:2], c_(row, RSX), OP.mult)
            V.tensor_scalar(c_(row, RSG), c_(row, RT0), c_(row, R32K), 1e-12,
                            OP.mult, OP.max)
            V.reciprocal(c_(row, RT0), c_(row, RSG))
            V.tensor_tensor(row[:1, 17:18], c_(row, RSX), c_(row, RT0), OP.mult)
            V.tensor_copy(row[:1, 18:19], c_(row, RSG))
        emit_collective(1, (2,), fin2g)

        SY.wait_ge(s_so[0].h, s_so[0].n)
        SY.wait_ge(s_so[1].h, s_so[1].n)
        f2_spill_mark = [0]
        f4_state = {"vmark": 0, "spill": 0}

        def f4_units():
            for ich in range(ICH):
                src = FSTI[:, 0:S]
                gq = FSTI[:, SL:SL + S] if ich % 2 == 0 else FOUT[:, SL:SL + S]
                if f4_state["vmark"] > 0:
                    SY.wait_ge(s_dv.h, f4_state["vmark"])
                if f4_state["spill"] > 0:
                    SY.wait_ge(s_tr.h, f4_state["spill"])
                SY.dma_start(out=src, in_=gtr_d[ich]).then_inc(s_gn[0].h, 16)
                s_gn[0].n += 16

                nc.vector.wait_ge(s_gn[0].h, s_gn[0].n)
                V.tensor_scalar(gq, src, c_(scl, KGc), MAGIC, OP.mult, OP.add)
                V.tensor_scalar(gq, gq, MAGIC, None, OP.subtract)
                rmark = s_dv.n
                yield
                nc.scalar.wait_ge(s_dv.h, rmark)
                C.activation(src, gq, ACT.Sigmoid, scale=c_(scl, SGc))
                yield
                C.activation(src, src, ACT.Identity, bias=magic_c[:],
                             scale=65536.0)
                yield
                C.activation(src, src, ACT.Identity, bias=nmagic_c[:])
                smark = s_sc.n
                yield
                nc.vector.wait_ge(s_sc.h, smark)
                V.tensor_scalar(src, src, 65535.0, None, OP.min)
                V.tensor_tensor(src, src, gq, OP.mult)
                V.reduce_max(c_(colb, 6), src, AX, apply_absolute_value=True)
                V.tensor_tensor(c_(colb, 3), c_(colb, 3), c_(colb, 6), OP.max)
                f4_state["vmark"] = s_dv.n
                yield
                nc.gpsimd.wait_ge(s_dv.h, f4_state["vmark"])
                G.dma_start(out=tr_d[ich], in_=src).then_inc(s_tr.h, 16)
                s_tr.n += 16
                f4_state["spill"] = s_tr.n
                yield
        f4_iter = f4_units()

        wload_pend = {0: emit_w_chunk_load("u", 0)}
        for ich in range(ICH):
            dbw, skp = wload_pend.pop(ich)
            if ich + 1 < ICH:
                wload_pend[ich + 1] = emit_w_chunk_load("u", ich + 1)
            nc.tensor.wait_ge(skp.h, skp.n)
            for tb, pb in emit_wmm_tiles(ich, dbw):
                nc.scalar.wait_ge(s_mm.h, s_mm.n)
                if tb == 0 and f2_spill_mark[0] > 0:
                    nc.scalar.wait_ge(s_so[0].h, f2_spill_mark[0])
                C.activation(FOUT[:, tb * NB:(tb + 1) * NB], bank(pb), ACT.Copy)
                ev_marks.append(s_sc.n)
                nc.vector.wait_ge(s_sc.h, s_sc.n)
                V.reduce_max(c_(colb, 6), FOUT[:, tb * NB:(tb + 1) * NB],
                             AX, apply_absolute_value=True)
                V.tensor_tensor(c_(colb, 1), c_(colb, 1), c_(colb, 6), OP.max)
                if tb != 1 and tb != 2:
                    next(f4_iter, None)
            nc.scalar.wait_ge(s_dv.h, s_dv.n)
            C.dma_start(out=upr_d[ich], in_=FOUT[:, 0:TB * NB]
                        ).then_inc(s_so[0].h, 16)
            s_so[0].n += 16
            f2_spill_mark[0] = s_so[0].n
        for _ in f4_iter:
            pass

        def fin23():
            V.tensor_tensor(c_(row, RT0), ccs[:, 1:2], c_(row, RSX), OP.mult)
            V.tensor_scalar(c_(row, RSU), c_(row, RT0), c_(row, R32K), 1e-12,
                            OP.mult, OP.max)
            V.reciprocal(c_(row, RT0), c_(row, RSU))
            V.tensor_tensor(row[:1, 19:20], c_(row, RSX), c_(row, RT0), OP.mult)
            V.tensor_scalar(c_(row, RT0), ccs[:, 2:3], 1.0 / 65536.0, None,
                            OP.mult)
            V.tensor_tensor(c_(row, RT0), c_(row, RT0), c_(row, RSG), OP.mult)
            V.tensor_scalar(c_(row, RT0), c_(row, RT0), c_(row, R32K), None,
                            OP.mult)
            V.tensor_scalar(c_(row, RST), c_(row, RT0), 1e-12, None, OP.max)
            V.reciprocal(c_(row, RT0), c_(row, RST))
            V.tensor_scalar(c_(row, RT1), c_(row, RSG), 1.0 / 65536.0, None,
                            OP.mult)
            V.tensor_tensor(row[:1, 20:21], c_(row, RT1), c_(row, RT0), OP.mult)
        bc23 = emit_collective(2, (1, 3), fin23)

        stN.close()

        stF4 = contextlib.ExitStack()
        GN = sb("GN", [128, 2 * SL], F32, stF4)

        SY.wait_ge(s_so[0].h, s_so[0].n)
        SY.wait_ge(s_tr.h, s_tr.n)
        f5_dv = []
        f5_sc = []
        for ich in range(ICH):
            dbk = ich % 2
            if len(f5_dv) >= 2:
                SY.wait_ge(s_dv.h, f5_dv[-2])
            sfp = s_fp[dbk]
            sgn = s_gn[dbk]
            SY.dma_start(out=FSTI[:, dbk * SL: dbk * SL + S],
                         in_=upr_d[ich]).then_inc(sfp.h, 16)
            SY.dma_start(out=GN[:, dbk * SL: dbk * SL + S],
                         in_=tr_d[ich]).then_inc(sgn.h, 16)
            sfp.n += 16
            sgn.n += 16

            up = FSTI[:, dbk * SL: dbk * SL + S]
            tr = GN[:, dbk * SL: dbk * SL + S]
            ot = FOUT[:, dbk * SL: dbk * SL + S]
            nc.scalar.wait_ge(sfp.h, sfp.n)
            if ich == 0:
                nc.scalar.wait_ge(s_dv.h, bc23)
            C.activation(up, up, ACT.Identity, bias=magic_c[:], scale=c_(scl, KUc))
            C.activation(up, up, ACT.Identity, bias=nmagic_c[:])
            f5_sc.append(s_sc.n)

            nc.vector.wait_ge(sgn.h, sgn.n)
            sos = s_or[dbk]
            if ich == 0:
                nc.vector.wait_ge(s_so[0].h, s_so[0].n)
            if sos.n > 0:
                nc.vector.wait_ge(sos.h, sos.n)
            V.tensor_scalar(tr, tr, c_(scl, KTc), MAGIC, OP.mult, OP.add)
            V.tensor_scalar(tr, tr, MAGIC, None, OP.subtract)
            nc.vector.wait_ge(s_sc.h, f5_sc[-1])
            V.tensor_tensor(ot, tr, up, OP.mult)
            V.reduce_max(c_(colb, 6), ot, AX, apply_absolute_value=True)
            V.tensor_tensor(c_(colb, 4), c_(colb, 4), c_(colb, 6), OP.max)
            f5_dv.append(s_dv.n)

            nc.gpsimd.wait_ge(s_dv.h, s_dv.n)
            G.dma_start(out=or_d[ich], in_=FOUT[:, dbk * SL: dbk * SL + TB * NB]
                        ).then_inc(sos.h, 16)
            sos.n += 16

        stF4.close()


        def fin4():
            V.tensor_tensor(c_(row, RT0), c_(row, RST), c_(row, RSU), OP.mult)
            V.tensor_tensor(c_(row, RT1), ccs[:, 1:2], c_(row, RT0), OP.mult)
            V.tensor_scalar(c_(row, RSO), c_(row, RT1), c_(row, R32K), 1e-12,
                            OP.mult, OP.max)
            V.reciprocal(c_(row, RT1), c_(row, RSO))
            V.tensor_tensor(row[:1, 21:22], c_(row, RT0), c_(row, RT1), OP.mult)
            V.tensor_copy(row[:1, 22:23], c_(row, RSO))
        stF2 = contextlib.ExitStack()
        WD = sb("WD", [128, 2 * ICH * E], BF16, stF2)
        OTB = sb("OTB", [128, 2 * 2 * NB], BF16, stF2)
        XSB = sb("XSB", [128, 4 * NB], BF16, stF2)

        SY.wait_ge(s_or[0].h, s_or[0].n)
        SY.wait_ge(s_or[1].h, s_or[1].n)
        SY.dma_start(out=WD[:, :ICH * E].rearrange("p (c n) -> p c n", n=E),
                     in_=whl_d[("d", 0)].transpose([1, 0, 2])).then_inc(s_kp[0].h, 16)
        SY.dma_start(out=WD[:, ICH * E:].rearrange("p (c n) -> p c n", n=E),
                     in_=whl_d[("d", 1)].transpose([1, 0, 2])).then_inc(s_kp[0].h, 16)
        s_kp[0].n += 32

        emit_collective(3, (4,), fin4)
        nc.tensor.wait_ge(s_kp[0].h, s_kp[0].n)

        ot_marks = []
        xstat_marks = []
        f6_mark = {}
        f6_dv_seq = []
        ln2_mark = [0]
        ev_done = [0]

        def f6_x1_load(tb):
            if ev_done[0] > 0:
                SY.wait_ge(s_dv.h, ev_done[0])
            SY.dma_start(out=FSTI[:, SL:SL + ECH * NB]
                         .rearrange("p (c n) -> p c n", n=NB),
                         in_=x1_d[:, :, tb * NB:(tb + 1) * NB].transpose([1, 0, 2])
                         ).then_inc(s_x1b.h, 16)
            s_x1b.n += 16

        def f6_or_unit(tb, ich):
            dbk = ich % 2
            if len(f6_dv_seq) >= 2:
                SY.wait_ge(s_dv.h, f6_dv_seq[-2])
            sfp = s_fp[dbk]
            SY.dma_start(out=FSTI[:, dbk * NB:(dbk + 1) * NB],
                         in_=or_d[ich, :, tb * NB:(tb + 1) * NB]
                         ).then_inc(sfp.h, 16)
            sfp.n += 16

            nc.vector.wait_ge(sfp.h, sfp.n)
            if len(ot_marks) >= 2:
                nc.vector.wait_ge(s_mm.h, ot_marks[-2])
            src = FSTI[:, dbk * NB:(dbk + 1) * NB]
            oh = OTB[:, dbk * 2 * NB: dbk * 2 * NB + NB]
            ol = OTB[:, dbk * 2 * NB + NB: (dbk + 1) * 2 * NB]
            V.tensor_scalar(src, src, c_(scl, KOc), MAGIC, OP.mult, OP.add)
            V.tensor_scalar(src, src, MAGIC, None, OP.subtract)
            V.tensor_copy(oh, src)
            V.tensor_tensor(ol, src, oh, OP.subtract)
            f6_mark[(tb, ich)] = s_dv.n
            f6_dv_seq.append(s_dv.n)

        f6_x1_load(0)
        f6_or_unit(0, 0)
        f6_or_unit(0, 1)

        for tb in range(TB):
            x1_ready = s_x1b.n
            for ich in range(ICH):
                dbk = ich % 2
                if ich >= 2:
                    f6_or_unit(tb, ich)
                oh = OTB[:, dbk * 2 * NB: dbk * 2 * NB + NB]
                ol = OTB[:, dbk * 2 * NB + NB: (dbk + 1) * 2 * NB]
                nc.tensor.wait_ge(s_dv.h, f6_mark[(tb, ich)])
                if ich == 0 and ln2_mark[0] > 0:
                    nc.tensor.wait_ge(s_dv.h, ln2_mark[0])
                for hc in range(ECH):
                    lwh = WD[:, ich * E + hc * 128: ich * E + hc * 128 + 128]
                    lwl = WD[:, ICH * E + ich * E + hc * 128:
                             ICH * E + ich * E + hc * 128 + 128]
                    T.matmul(bank(hc), lwh, oh, start=(ich == 0), stop=False)
                    T.matmul(bank(hc), lwh, ol, start=False, stop=False)
                    last = T.matmul(bank(hc), lwl, oh, start=False,
                                    stop=(ich == ICH - 1))
                last.then_inc(s_mm.h, 1)
                s_mm.n += 1
                ot_marks.append(s_mm.n)

            nc.vector.wait_ge(s_mm.h, ot_marks[-1])
            nc.vector.wait_ge(s_x1b.h, x1_ready)
            if s_out.n > 0:
                nc.vector.wait_ge(s_out.h, s_out.n)
            for hc in range(ECH):
                xp = FOUT[:, hc * NB:(hc + 1) * NB]
                V.tensor_scalar(xp, bank(hc), c_(scl, SOc), None, OP.mult)
                V.tensor_tensor(xp, xp, FSTI[:, SL + hc * NB: SL + (hc + 1) * NB],
                                OP.add)
            f6_evac_done = s_dv.n
            ev_done[0] = f6_evac_done

            if tb + 1 < TB:
                f6_x1_load(tb + 1)
                f6_or_unit(tb + 1, 0)
                f6_or_unit(tb + 1, 1)

            for hc in range(ECH):
                nc.scalar.wait_ge(s_dv.h, f6_evac_done)
                if len(xstat_marks) >= 2:
                    nc.scalar.wait_ge(s_mm.h, xstat_marks[-2])
                xp = FOUT[:, hc * NB:(hc + 1) * NB]
                xb = XSB[:, (hc % 2) * 2 * NB:(hc % 2) * 2 * NB + NB]
                xq = XSB[:, (hc % 2) * 2 * NB + NB:((hc % 2) + 1) * 2 * NB]
                C.activation(xb, xp, ACT.Copy)
                C.activation(xq, xp, ACT.Square)

                nc.tensor.wait_ge(s_sc.h, s_sc.n)
                nc.tensor.wait_ge(s_dv.h, f6_evac_done)
                T.matmul(bank(0), ones_b[:], xb,
                         start=(hc == 0), stop=(hc == ECH - 1))
                T.matmul(bank(1), ones_b[:], xq,
                         start=(hc == 0), stop=(hc == ECH - 1)
                         ).then_inc(s_mm.h, 1)
                s_mm.n += 1
                xstat_marks.append(s_mm.n)

            nc.vector.wait_ge(s_mm.h, xstat_marks[-1])
            mn = FSTI[:, 2 * NB:3 * NB]
            vb = FSTI[:, 3 * NB:4 * NB]
            rs = FSTI[:, 4 * NB:5 * NB]
            V.tensor_scalar(mn, bank(0), 1.0 / E, None, OP.mult)
            V.tensor_scalar(vb, bank(1), 1.0 / E, None, OP.mult)
            V.tensor_tensor(rs, mn, mn, OP.mult)
            V.tensor_tensor(vb, vb, rs, OP.subtract)
            ln2_mark[0] = s_dv.n
            nc.scalar.wait_ge(s_dv.h, ln2_mark[0])
            C.activation(rs, vb, ACT.Sqrt, bias=eps_c[:])
            nc.vector.wait_ge(s_sc.h, s_sc.n)
            V.reciprocal(rs, rs)

            for hc in range(ECH):
                xp = FOUT[:, hc * NB:(hc + 1) * NB]
                V.tensor_tensor(xp, xp, mn, OP.subtract)
                V.tensor_tensor(xp, xp, rs, OP.mult)

                nc.scalar.wait_ge(s_dv.h, s_dv.n)
                C.dma_start(out=out_d[hc, :, tb * NB:(tb + 1) * NB],
                            in_=xp).then_inc(s_out.h, 16)
                s_out.n += 16

        nc.gpsimd.wait_ge(s_out.h, s_out.n)
        stF2.close()
    return nc


_CACHE = {}


def _get_program(S):
    if S not in _CACHE:
        _CACHE[S] = build_program(S)
    return _CACHE[S]


def stage_inputs(L0_attn_x, L0_attn_w_query, L0_attn_w_key, L0_attn_w_value,
                 L0_attn_w_proj, L0_attn_b_proj, L0_ffn_w_gate, L0_ffn_w_up,
                 L0_ffn_w_down):
    x = np.ascontiguousarray(np.asarray(L0_attn_x, dtype=np.float32))
    B, S, _E = x.shape

    def t8(w):
        return np.ascontiguousarray(np.asarray(w, np.float32).T.reshape(ECH, 128, E))

    shared = {
        "awT_q": t8(L0_attn_w_query), "awT_k": t8(L0_attn_w_key),
        "awT_v": t8(L0_attn_w_value), "awT_p": t8(L0_attn_w_proj),
        "bproj": np.ascontiguousarray(np.asarray(L0_attn_b_proj, np.float32)
                                      .reshape(ECH, 128).T),
        "wgT": np.ascontiguousarray(np.asarray(L0_ffn_w_gate, np.float32).T
                                    .reshape(ECH, 128, I3)),
        "wuT": np.ascontiguousarray(np.asarray(L0_ffn_w_up, np.float32).T
                                    .reshape(ECH, 128, I3)),
        "wdT": np.ascontiguousarray(np.asarray(L0_ffn_w_down, np.float32).T
                                    .reshape(ICH, 128, E)),
        "wg_nat": np.ascontiguousarray(np.asarray(L0_ffn_w_gate, np.float32)
                                       .reshape(ICH, 128, E)),
        "wu_nat": np.ascontiguousarray(np.asarray(L0_ffn_w_up, np.float32)
                                       .reshape(ICH, 128, E)),
        "wd_nat": np.ascontiguousarray(np.asarray(L0_ffn_w_down, np.float32)
                                       .reshape(ECH, 128, I3)),
        "ident32": np.eye(128, dtype=np.float32),
    }
    in_maps = []
    for c in range(B):
        m = dict(shared)
        m["xT"] = np.ascontiguousarray(x[c].T.reshape(ECH, 128, S))
        in_maps.append(m)
    return in_maps


def kernel(**inputs):
    in_maps = stage_inputs(**inputs)
    B = len(in_maps)
    S = in_maps[0]["xT"].shape[2]
    nc = _get_program(S)
    res = run_bass_kernel_spmd(nc, in_maps, core_ids=list(range(B)))
    outs = []
    for c in range(B):
        y = np.asarray(res.results[c]["out"]).reshape(E, S).T
        outs.append(y)
    return np.ascontiguousarray(np.stack(outs)).astype(np.float32)
